# version 12
# baseline (speedup 1.0000x reference)
import contextlib
import numpy as np
import ml_dtypes
import concourse.bass as bass
import concourse.mybir as mybir
from concourse.bass_utils import run_bass_kernel_spmd

F32 = mybir.dt.float32
BF16 = mybir.dt.bfloat16
I32 = mybir.dt.int32
AF = mybir.ActivationFunctionType
ALU = mybir.AluOpType
AX = mybir.AxisListType
BF = ml_dtypes.bfloat16

NB = 2
S = 2048
T = NB * S
D = 1024
NCH = 8
HID = 4096
HCH = 32
NH = 8
TT = 512
CS = 256
AMODE = "full"
LN127 = float(np.log(np.float64(127.0)))
PI2 = float(2.0 * np.pi)
PIH = float(np.pi / 2.0)
MAGIC = float(3 << 22)
ENGS = ("sy", "te", "ve", "sc", "gp")
KB_ = 1024


class K:

    SELF = {"te": "te", "ve": "ve", "sc": "sc", "gp": "gp"}

    DMASEM = {"sy": "ld", "gp": "gd", "sc": "sd", "ve": "vd"}

    def __init__(self, nc):
        self.nc = nc
        self.recs = {e: [] for e in ENGS}
        self.cnt = {k: 0 for k in ("ld", "gd", "sd", "vd", "te", "ve",
                                   "sc", "gp")}
        self.pend = {e: [] for e in ENGS}

    def rec(self, eng, fn, wait=None, inc=None, drain=False):
        if fn is None:
            return self.bar(eng, wait=wait, inc=inc)
        sem = self.SELF.get(eng)
        w = self.pend[eng] + list(wait or ())
        self.pend[eng] = []
        if sem is None:
            self.recs[eng].append((fn, tuple(w), None))
            return self.cnt.get(eng)
        if drain and self.cnt[sem] > 0:
            w.append((sem, self.cnt[sem]))
        self.cnt[sem] += 1
        out = self.cnt[sem]
        self.recs[eng].append((fn, tuple(w), sem))
        return out

    def bar(self, eng, wait=None, inc=None):
        if wait:
            self.pend[eng].extend(wait)
        return self.cnt.get(self.SELF.get(eng, eng), 0)

    def dma(self, eng, dst, src, wait=None, slow=False):
        sem = self.DMASEM[eng]
        prev = self.cnt[sem]
        self.cnt[sem] += 16
        out = self.cnt[sem]
        w = self.pend[eng] + list(wait or ())
        self.pend[eng] = []
        if prev > 0:
            w.append((sem, prev))
        self.recs[eng].append(
            (lambda e, d=dst, s=src, sl=slow: e.dma_start(
                d, s, allow_slow_non_contiguous=sl),
             tuple(w), sem))
        return out

    def v(self, sem):
        return self.cnt[sem]

    def emit(self, sems, eng, e):
        for fn, waits, inc in self.recs[eng]:
            dedup = {}
            for s, val in waits:
                if val > dedup.get(s, -1):
                    dedup[s] = val
            for s, val in dedup.items():
                if val > 0:
                    e.wait_ge(sems[s], val)
            inst = fn(e)
            if inc is not None:
                inst.then_inc(sems[inc], 16 if inc in ("ld", "gd", "sd", "vd") else 1)


def _drv(ap):
    return ap.rearrange("(c p) t -> p c t", p=128)


def build(phases=("prep", "ffna", "attn", "attno", "ffnc"), debug_out=None,
          taps=False):
    nc = bass.Bass("TRN2", target_bir_lowering=False)
    k = K(nc)
    k.taps = taps
    k.tapped = {}

    xT = nc.dram_tensor("xT", [D, T], F32, kind="ExternalInput").ap()
    w1T = nc.dram_tensor("w1T", [D, HID], F32, kind="ExternalInput").ap()
    w2T = nc.dram_tensor("w2T", [HID, D], F32, kind="ExternalInput").ap()
    qwT = nc.dram_tensor("qwT", [D, 4 * D], F32, kind="ExternalInput").ap()
    owT = nc.dram_tensor("owT", [D, D], F32, kind="ExternalInput").ap()
    qbT = nc.dram_tensor("qbT", [128, 32], F32, kind="ExternalInput").ap()
    obT = nc.dram_tensor("obT", [128, 8], F32, kind="ExternalInput").ap()
    cbT = nc.dram_tensor("cbT", [128, 32], F32, kind="ExternalInput").ap()
    cwT = nc.dram_tensor("cwT", [128, 96], F32, kind="ExternalInput").ap()
    thT = nc.dram_tensor("thT", [128, 8], F32, kind="ExternalInput").ap()
    slp = nc.dram_tensor("slp", [1, 8], F32, kind="ExternalInput").ap()
    idn = nc.dram_tensor("idn", [128, 128], F32, kind="ExternalInput").ap()
    tri = nc.dram_tensor("tri", [128, 128], F32, kind="ExternalInput").ap()
    iot = nc.dram_tensor("iot", [128, S], F32, kind="ExternalInput").ap()

    y = nc.dram_tensor("y", [D, T], F32, kind="ExternalOutput").ap()
    if taps:
        d_rp = nc.dram_tensor("d_rp", [128, 4096], BF16,
                              kind="ExternalOutput").ap()
        d_g = nc.dram_tensor("d_g", [128, HCH * 512], F32,
                             kind="ExternalOutput").ap()
        d_rh = nc.dram_tensor("d_rh", [128, HCH * 512], BF16,
                              kind="ExternalOutput").ap()
        d_ar2 = nc.dram_tensor("d_ar2", [128, 4096], F32,
                               kind="ExternalOutput").ap()
        d_rows = nc.dram_tensor("d_rows", [1, 4096], F32,
                                kind="ExternalOutput").ap()
    xs1 = nc.dram_tensor("xs1", [D, T], F32, kind="Internal").ap()
    xs2 = nc.dram_tensor("xs2", [D, T], F32, kind="Internal").ap()
    y2s = nc.dram_tensor("y2s", [D, T], BF16, kind="Internal").ap()
    rq3 = nc.dram_tensor("rq3", [D, T], BF16, kind="Internal").ap()
    wq1s = nc.dram_tensor("wq1s", [D, HID], BF16, kind="Internal").ap()
    wq2s = nc.dram_tensor("wq2s", [HID, D], BF16, kind="Internal").ap()

    SBASE = 16384
    A = lambda n, sh, dt, off: nc.alloc_sbuf_tensor_at(
        n, sh, dt, offset=off + SBASE)
    identf = A("identf", [128, 128], F32, 0)
    identb = A("identb", [128, 128], BF16, 512)
    trib = A("trib", [128, 128], F32, 768)
    tribh = A("tribh", [128, 128], F32, 1280)
    qb = A("qb", [128, 32], F32, 1792)
    ob = A("ob", [128, 8], F32, 1920)
    cb = A("cb", [128, 32], F32, 1952)
    cw = A("cw", [128, 96], F32, 2080)
    thetaT = A("thetaT", [128, 8], F32, 2464)
    th2pi = A("th2pi", [128, 8], F32, 2496)
    esm1c = A("esm1c", [128, 8], F32, 2528)
    onesc = A("onesc", [128, 1], F32, 2560)
    onescb = A("onescb", [128, 1], BF16, 2592)
    onesr = A("onesr", [1, 128], F32, 2624)
    slpr = A("slpr", [1, 8], F32, 4192)
    esm1r = A("esm1r", [1, 8], F32, 4224)
    wsum = A("wsum", [128, 1], F32, 4256)
    scol = A("scol", [128, 1], F32, 4288)
    mcol = A("mcol", [128, 1], F32, 4320)
    srow = A("srow", [1, 8], F32, 4352)
    hedge = A("hedge", [128, 64], F32, 4384)
    rsqtm = A("rsqtm", [128, 32], F32, 4640)
    ceps5 = A("ceps5", [128, 1], F32, 4768)
    ceps6 = A("ceps6", [128, 1], F32, 4800)
    cl127 = A("cl127", [128, 1], F32, 4832)
    zcol = A("zcol", [128, 1], F32, 4864)
    zcolb = A("zcolb", [128, 1], BF16, 4896)
    cpih = A("cpih", [128, 1], F32, 4928)
    xedge = A("xedge", [128, 32], F32, 4960)
    nc.const_aps.register(F32, PIH, cpih[:, 0:1])
    nc.const_aps.register(F32, 1e-5, ceps5[:, 0:1])
    nc.const_aps.register(F32, 1e-6, ceps6[:, 0:1])
    nc.const_aps.register(F32, LN127, cl127[:, 0:1])
    nc.const_aps.register(F32, 0.0, zcol[:, 0:1])
    nc.const_aps.register(F32, 1.0, onesc[:, 0:1])
    nc.const_aps.register(BF16, 0.0, zcolb[:, 0:1])
    nc.const_aps.register(BF16, 1.0, onescb[:, 0:1])
    rows = A("rows", [1, 3072], F32, 8 * KB_)
    hbc = A("hbc", [128, 1024], F32, 20 * KB_)
    bca = A("bca", [128, 1536], F32, 24 * KB_)
    xa = A("xa", [128, 4096], F32, 30 * KB_)
    xb = A("xb", [128, 4096], F32, 46 * KB_)
    wba = A("wba", [128, 4096], BF16, 62 * KB_)
    wbb = A("wbb", [128, 4096], BF16, 70 * KB_)
    rp = A("rp", [128, 4096], BF16, 78 * KB_)
    Rh = A("Rh", [128, HCH * 512], BF16, 86 * KB_)
    y2st = A("y2st", [128, 4096], BF16, 86 * KB_)
    y2sq = A("y2sq", [128, 4096], BF16, 94 * KB_)
    sq = A("sq", [128, 4096], BF16, 118 * KB_)
    g = A("g", [128, HCH * 512], F32, 126 * KB_)
    ar2 = A("ar2", [128, 4096], F32, 190 * KB_)
    wfull = A("wfull", [128, 32768], F32, 30 * KB_)
    p1xs = A("p1xs", [128, 4096], F32, 158 * KB_)
    p1wq = A("p1wq", [128, 4096], BF16, 8 * KB_)

    tgr = A("tgr", [128, S], F32, 8 * KB_)
    sinf = A("sinf", [128, S], BF16, 16 * KB_)
    cosf = A("cosf", [128, S], BF16, 20 * KB_)
    tgf = A("tgf", [128, S], F32, 158 * KB_)
    tgf2 = A("tgf2", [128, S], F32, 166 * KB_)
    asb = A("asb", [128, 256], BF16, 26752)
    bsbB = A("bsbB", [128, 256], BF16, 27264)
    bsbf = A("bsbf", [128, 256], F32, 27776)
    fsb = A("fsb", [128, 768], BF16, 28800)
    bsbA = A("bsbA", [128, 1536], BF16, 5088)
    sqat = A("sqat", [128, 256], BF16, 211968)
    y2 = A("y2", [128, NCH * T], BF16, 30 * KB_)
    pat = A("pat", [128, NH * T], BF16, 94 * KB_)
    silq = A("silq", [128, S], BF16, 158 * KB_)
    silk = A("silk", [128, S], BF16, 162 * KB_)
    qcb = A("qcb", [128, S], BF16, 166 * KB_)
    qsb = A("qsb", [128, S], BF16, 170 * KB_)
    kcb = A("kcb", [128, S], BF16, 174 * KB_)
    ksb = A("ksb", [128, S], BF16, 178 * KB_)
    kcT = A("kcT", [128, S], BF16, 182 * KB_)
    ksT = A("ksT", [128, S], BF16, 186 * KB_)
    vtm = A("vtm", [128, S], BF16, 190 * KB_)
    uh = A("uh", [128, S], BF16, 194 * KB_)
    wqh = A("wqh", [128, 4096], BF16, 198 * KB_)
    vfm = A("vfm", [128, 512], BF16, 206 * KB_)

    zsb = A("zsb", [128, 4096], BF16, 30 * KB_)
    owb = A("owb", [128, 8192], BF16, 38 * KB_)
    x4a = A("x4a", [128, 4096], F32, 54 * KB_)
    sq4 = A("sq4", [128, 4096], BF16, 70 * KB_)

    psA = nc.alloc_psum_tensor("psA", [128, 512], F32)
    psB = nc.alloc_psum_tensor("psB", [128, 512], F32)
    psC = nc.alloc_psum_tensor("psC", [128, 512], F32)
    psD = nc.alloc_psum_tensor("psD", [128, 512], F32)
    psE = nc.alloc_psum_tensor("psE", [128, 512], F32)
    psF = nc.alloc_psum_tensor("psF", [128, 512], F32)
    psG = nc.alloc_psum_tensor("psG", [128, 512], F32)
    psT = nc.alloc_psum_tensor("psT", [128, 1024], BF16)

    for dst, src in [(identf, idn), (trib, tri), (qb, qbT), (ob, obT),
                     (cb, cbT), (cw, cwT), (thetaT, thT),
                     (slpr, slp)]:
        k.dma("sy", dst[:], src)
    vld0 = k.v("ld")
    k.rec("ve", lambda e: e.memset(onesc[:], 1.0))
    k.rec("ve", lambda e: e.memset(ceps5[:], 1e-5))
    k.rec("ve", lambda e: e.memset(zcol[:], 0.0))
    k.rec("ve", lambda e: e.memset(zcolb[:], 0.0))
    k.rec("ve", lambda e: e.memset(ceps6[:], 1e-6))
    k.rec("ve", lambda e: e.memset(cl127[:], LN127))
    k.rec("ve", lambda e: e.memset(onescb[:], 1.0))
    k.rec("ve", lambda e: e.memset(onesr[:], 1.0))
    k.rec("ve", lambda e: e.memset(cpih[:], PIH))
    k.rec("ve", lambda e: e.memset(hedge[:], 0.0))
    k.rec("ve", lambda e: e.tensor_scalar_mul(th2pi[:], thetaT[:], 1.0 / PI2),
          wait=[("ld", vld0)], drain=True)
    k.rec("ve", lambda e: e.tensor_copy(identb[:], identf[:]), drain=True)
    v_ms = k.bar("ve", inc="ve")
    k.rec("sc", lambda e: e.activation(esm1r[:], slpr[:], AF.Exp),
          wait=[("ld", vld0), ("ve", v_ms)])
    vs = k.bar("sc", inc="sc")
    vv = k.rec("ve", lambda e: e.tensor_scalar_add(esm1r[:], esm1r[:], -1.0),
               wait=[("sc", vs)], inc="ve", drain=True)
    vt = k.rec("te", lambda e: e.matmul(psD[0:128, 0:8], onesr[0:1, :],
                                        esm1r[0:1, :], start=True, stop=True),
               wait=[("ve", vv)], inc="te")
    v_const = k.rec("ve", lambda e: e.tensor_copy(esm1c[:], psD[:, 0:8]),
                    wait=[("te", vt)], inc="ve", drain=True)

    def wprep(wsrc, wdst, nelem):
        nsl = nelem // (128 * 4096)
        wfv = wfull[:, 0:nsl * 4096]
        vld = k.dma("sy",
                    wfv.rearrange("p (c t) -> p c t",
                                  c=_drv(wsrc).shape[1]),
                    _drv(wsrc),
                    wait=[("ve", k.v("ve")), ("gd", k.v("gd"))])
        vv = k.rec("ve", lambda e: e.reduce_sum(
            wsum[:], wfv, AX.X, apply_absolute_value=True),
            wait=[("ld", vld)], inc="ve", drain=True)
        vt = k.rec("te", lambda e: e.matmul(
            psD[0:1, 0:1], wsum[:, 0:1], onesc[:, 0:1], start=True,
            stop=True), wait=[("ve", vv)], inc="te")
        k.rec("ve", lambda e, n=nelem: e.tensor_scalar(
            srow[0:1, 0:1], psD[0:1, 0:1], 1.0 / n, 1e-5, ALU.mult,
            ALU.max), wait=[("te", vt)], drain=True)
        vv = k.rec("ve", lambda e: e.reciprocal(srow[0:1, 1:2],
                                                srow[0:1, 0:1]),
                   inc="ve", drain=True)
        vt = k.rec("te", lambda e: e.matmul(
            psD[0:128, 4:6], onesr[0:1, :], srow[0:1, 0:2], start=True,
            stop=True), wait=[("ve", vv)], inc="te")
        k.rec("ve", lambda e: e.tensor_copy(mcol[:, 0:1], psD[:, 4:5]),
              wait=[("te", vt)], drain=True)
        k.rec("ve", lambda e: e.tensor_copy(scol[:, 0:1], psD[:, 5:6]),
              drain=True)
        vst = 0
        for i in range(nsl):
            sl = wfull[:, i * 4096:(i + 1) * 4096]
            k.rec("ve", lambda e, sl=sl: e.tensor_scalar(
                p1xs[:], sl, scol[:, 0:1], MAGIC, ALU.mult, ALU.add),
                wait=[("gd", vst)] if vst else None, drain=True)
            k.rec("ve", lambda e: e.tensor_scalar(
                p1xs[:], p1xs[:], -MAGIC, 1.0, ALU.add, ALU.min), drain=True)
            vv = k.rec("ve", lambda e: e.tensor_scalar(
                p1wq[:], p1xs[:], -1.0, mcol[:, 0:1], ALU.max, ALU.mult),
                inc="ve", drain=True)
            ncs = _drv(wdst).shape[1] * 4096 * 128 // nelem
            vst = k.dma("gp",
                        _drv(wdst)[:, i * ncs:(i + 1) * ncs, :],
                        p1wq[:].rearrange("p (c t) -> p c t", c=ncs),
                        wait=[("ve", vv)])
        return vst

    if "prep" in phases:
        wprep(w1T, wq1s, D * HID)
        v_prep = wprep(w2T, wq2s, HID * D)
    else:
        v_prep = 0

    def bcast(row_ap, w, wait):
        vt = k.rec("te", lambda e: e.matmul(
            psC[0:128, 0:w], onesr[0:1, :], row_ap, start=True, stop=True),
            wait=wait, inc="te")
        return k.rec("ve", lambda e: e.tensor_copy(bca[:, 0:w], psC[:, 0:w]),
                     wait=[("te", vt)], inc="ve", drain=True)

    def bcast2(row_ap, w, wait):
        vt = k.rec("te", lambda e: e.matmul(
            psC[0:128, 0:w], onesr[0:1, :], row_ap, start=True, stop=True),
            wait=wait, inc="te")
        return k.rec("ve", lambda e: e.tensor_copy(
            bca[:, 512:512 + w], psC[:, 0:w]), wait=[("te", vt)], inc="ve",
            drain=True)

    def bcast_to(dst, row_ap, w, wait):
        vt = k.rec("te", lambda e: e.matmul(
            psC[0:128, 0:w], onesr[0:1, :], row_ap, start=True, stop=True),
            wait=wait, inc="te")
        return k.rec("ve", lambda e, dst=dst: e.tensor_copy(
            dst[:, 0:w], psC[:, 0:w]), wait=[("te", vt)], inc="ve",
            drain=True)

    def cross_max(src01, w, row_dst, wait):
        nblk = (w + 127) // 128
        vts = []
        for j in range(nblk):
            wj = min(128, w - j * 128)
            vt = k.rec("te", lambda e, j=j, wj=wj: e.transpose(
                psF[0:wj, j * 128:j * 128 + 128],
                src01[:, j * 128:j * 128 + wj], identf[:]),
                wait=wait if j == 0 else None, inc="te")
            vts.append((j, wj, vt))
        v_all_tr = vts[-1][2]
        if w == 1:
            return k.rec("ve", lambda e: e.reduce_max(
                row_dst, psF[0:1, 0:128], AX.X, apply_absolute_value=True),
                wait=[("te", v_all_tr)], inc="ve", drain=True)
        vvs = []
        for j, wj, vt in vts:
            vv = k.rec("ve", lambda e, j=j, wj=wj: e.reduce_max(
                bca[0:wj, 1024 + j:1025 + j],
                psF[0:wj, j * 128:j * 128 + 128], AX.X,
                apply_absolute_value=True),
                wait=[("te", v_all_tr)] if j == 0 else None,
                inc="ve", drain=True)
            vvs.append((j, wj, vv))
        vt2 = None
        for j, wj, vv in vvs:
            vt2 = k.rec("te", lambda e, j=j, wj=wj: e.transpose(
                psG[0:1, j * 128:j * 128 + wj],
                bca[0:wj, 1024 + j:1025 + j], identf[0:wj, 0:wj]),
                wait=[("ve", vv)], inc="te")
        return k.rec("ve", lambda e, w=w: e.tensor_copy(
            row_dst, psG[0:1, 0:w]), wait=[("te", vt2)], inc="ve",
            drain=True)

    def rstd_rows(sum_ps_w, w, dst_off, wait=None):
        vv = k.rec("ve", lambda e: e.tensor_scalar_mul(
            rows[0:1, dst_off:dst_off + w], sum_ps_w, 1.0 / D),
            wait=wait, inc="ve", drain=True)
        k.rec("sc", lambda e: e.activation(
            rows[0:1, dst_off:dst_off + w], rows[0:1, dst_off:dst_off + w],
            AF.Ln, bias=1e-5), wait=[("ve", vv)], drain=True)
        return k.rec("sc", lambda e: e.activation(
            rows[0:1, dst_off:dst_off + w], rows[0:1, dst_off:dst_off + w],
            AF.Exp, scale=-0.5), inc="sc", drain=True)

    def stat_sums(src, w, ones, wait, stop_bank=None):
        vt = None
        for ch in range(NCH):
            vt = k.rec("te", lambda e, ch=ch: e.matmul(
                psD[0:1, 0:w], ones[:, 0:1],
                src[:, ch * 512:ch * 512 + w],
                start=(ch == 0), stop=(ch == NCH - 1)),
                wait=wait if ch == 0 else None, inc="te")
        return vt

    def round_chunk(dst, src, w, bc_rs_off, bc_a_off, wait, scale_first,
                    alpha_mult):
        if scale_first:
            k.rec("ve", lambda e: e.tensor_tensor(
                src, src, bca[:, bc_rs_off:bc_rs_off + w], ALU.mult),
                wait=wait, drain=True)
            wait = None
        if alpha_mult:
            k.rec("ve", lambda e: e.tensor_scalar(
                src, src, MAGIC, -MAGIC, ALU.add, ALU.add), wait=wait,
                drain=True)
            return k.rec("ve", lambda e: e.tensor_tensor(
                dst, src, bca[:, bc_a_off:bc_a_off + w], ALU.mult),
                inc="ve", drain=True)
        return k.rec("ve", lambda e: e.tensor_scalar(
            dst, src, MAGIC, -MAGIC, ALU.add, ALU.add), wait=wait,
            inc="ve", drain=True)

    def ln_quant(src, w, rdst, wait, xc_wait=None, sqb=None, xcb=None):
        xcb = ar2 if xcb is None else xcb
        vt = stat_sums(src, w, onesc,
                       list(wait) + [("ve", k.v("ve")), ("sc", k.v("sc"))])
        vv = k.rec("ve", lambda e: e.tensor_scalar_mul(
            rows[0:1, 0:w], psD[0:1, 0:w], 1.0 / D), wait=[("te", vt)],
            inc="ve", drain=True)
        vv = bcast(rows[0:1, 0:w], w, [("ve", vv)])
        for ch in range(NCH):
            wtx = ([("ve", vv)] + list(xc_wait or ())) if ch == 0 else None
            k.rec("ve", lambda e, ch=ch: e.tensor_tensor(
                xcb[:, ch * 512:ch * 512 + w],
                src[:, ch * 512:ch * 512 + w], bca[:, 0:w], ALU.subtract),
                wait=wtx, drain=True)
        vv = k.bar("ve", inc="ve")
        sqb = sq if sqb is None else sqb
        for ch in range(NCH):
            k.rec("sc", lambda e, ch=ch, sqb=sqb: e.activation(
                sqb[:, ch * 512:ch * 512 + w],
                xcb[:, ch * 512:ch * 512 + w], AF.Square),
                wait=[("ve", vv), ("te", k.v("te"))] if ch == 0 else None,
                drain=(ch == 0))
        vs = k.bar("sc", inc="sc")
        vt = stat_sums(sqb, w, onescb, [("sc", vs)])
        vs = rstd_rows(psD[0:1, 0:w], w, 512, wait=[("te", vt)])
        xcv = xcb[:, 0:NCH * 512].rearrange(
            "p (c t) -> p t c", c=NCH)[:, 0:w, :]
        vv = k.rec("ve", lambda e: e.reduce_max(
            bca[:, 512:512 + w], xcv, AX.X, apply_absolute_value=True),
            inc="ve", drain=True)
        vv = cross_max(bca[:, 512:], w, rows[0:1, 1024:1024 + w],
                       [("ve", vv)])
        k.rec("ve", lambda e: e.tensor_tensor(
            rows[0:1, 1024:1024 + w], rows[0:1, 1024:1024 + w],
            rows[0:1, 512:512 + w], ALU.mult), wait=[("sc", vs)],
            drain=True)
        k.rec("ve", lambda e: e.tensor_single_scalar(
            rows[0:1, 1024:1024 + w], rows[0:1, 1024:1024 + w], 1e-5,
            ALU.max), drain=True)
        vv = k.rec("ve", lambda e: e.tensor_scalar_mul(
            rows[0:1, 2048:2048 + w], rows[0:1, 1024:1024 + w],
            1.0 / 127.0), inc="ve", drain=True)
        k.rec("sc", lambda e: e.activation(
            rows[0:1, 1024:1024 + w], rows[0:1, 1024:1024 + w], AF.Ln),
            wait=[("ve", vv)], drain=True)
        vs = k.rec("sc", lambda e: e.activation(
            rows[0:1, 1024:1024 + w], rows[0:1, 1024:1024 + w], AF.Exp,
            scale=-1.0, bias=LN127), inc="sc", drain=True)
        vv = k.rec("ve", lambda e: e.tensor_tensor(
            rows[0:1, 1536:1536 + w], rows[0:1, 512:512 + w],
            rows[0:1, 1024:1024 + w], ALU.mult), wait=[("sc", vs)],
            inc="ve", drain=True)
        vv = bcast(rows[0:1, 1536:1536 + w], w, [("ve", vv)])
        vv = bcast2(rows[0:1, 2048:2048 + w], w, [("ve", vv)])
        vlast = None
        for ch in range(NCH):
            vlast = round_chunk(
                rdst(ch), xcb[:, ch * 512:ch * 512 + w], w, 0, 512,
                [("ve", vv)] if ch == 0 else None, True, True)
        return vlast

    def conv_group(cg, banks, vts, isfirst, w, extra_wait):
        scr = lambda cl, a, b: ar2[:, cl * 512 + a: cl * 512 + b]
        vs1 = None
        for cl in range(4):
            c = cg * 4 + cl
            hb = banks[cl]
            cw2 = cw[:, c * 3 + 2:c * 3 + 3]
            cbc = cb[:, c:c + 1]
            wt = [("te", vts[cl])]
            if cl == 0:
                wt += extra_wait
            if isfirst:
                vs1 = k.rec("sc", lambda e, cl=cl, hb=hb, cw2=cw2, cbc=cbc:
                            e.activation(scr(cl, 0, 511), hb[:, 1:512],
                                         AF.Identity, bias=cbc, scale=cw2),
                            wait=wt, inc="sc", drain=(cl == 0))
            else:
                vs1 = k.rec("sc", lambda e, cl=cl, hb=hb, cw2=cw2, cbc=cbc:
                            e.activation(scr(cl, 0, 512), hb[:, 0:512],
                                         AF.Identity, bias=cbc, scale=cw2),
                            wait=wt, inc="sc", drain=(cl == 0))
        for cl in range(4):
            c = cg * 4 + cl
            hb = banks[cl]
            cw1 = cw[:, c * 3 + 1:c * 3 + 2]
            he1 = hedge[:, c * 2 + 1:c * 2 + 2]
            wt = [("sc", vs1)] if cl == 0 else None
            if isfirst:
                k.rec("ve", lambda e, cl=cl, hb=hb, cw1=cw1:
                      e.scalar_tensor_tensor(
                    scr(cl, 0, 511), hb[:, 0:511], cw1, scr(cl, 0, 511),
                    ALU.mult, ALU.add), wait=wt, drain=(cl == 0))
            else:
                k.rec("ve", lambda e, cl=cl, hb=hb, cw1=cw1:
                      e.scalar_tensor_tensor(
                    scr(cl, 1, 512), hb[:, 0:511], cw1, scr(cl, 1, 512),
                    ALU.mult, ALU.add), wait=wt, drain=(cl == 0))
                k.rec("ve", lambda e, cl=cl, cw1=cw1, he1=he1:
                      e.scalar_tensor_tensor(
                    scr(cl, 0, 1), he1, cw1, scr(cl, 0, 1),
                    ALU.mult, ALU.add))
        for cl in range(4):
            c = cg * 4 + cl
            hb = banks[cl]
            cw0 = cw[:, c * 3 + 0:c * 3 + 1]
            he0 = hedge[:, c * 2:c * 2 + 1]
            he1 = hedge[:, c * 2 + 1:c * 2 + 2]
            gv = lambda a, b, c=c: g[:, c * 512 + a:c * 512 + b]
            if isfirst:
                k.rec("ve", lambda e, cl=cl, hb=hb, cw0=cw0, gv=gv:
                      e.scalar_tensor_tensor(
                    gv(1, 511), hb[:, 0:510], cw0, scr(cl, 1, 511),
                    ALU.mult, ALU.add), drain=(cl == 0))
                k.rec("ve", lambda e, cl=cl, gv=gv: e.tensor_copy(
                    gv(0, 1), scr(cl, 0, 1)))
            else:
                k.rec("ve", lambda e, cl=cl, hb=hb, cw0=cw0, gv=gv:
                      e.scalar_tensor_tensor(
                    gv(2, 512), hb[:, 0:510], cw0, scr(cl, 2, 512),
                    ALU.mult, ALU.add), drain=(cl == 0))
                k.rec("ve", lambda e, cl=cl, cw0=cw0, he0=he0, gv=gv:
                      e.scalar_tensor_tensor(
                    gv(0, 1), he0, cw0, scr(cl, 0, 1), ALU.mult, ALU.add))
                k.rec("ve", lambda e, cl=cl, cw0=cw0, he1=he1, gv=gv:
                      e.scalar_tensor_tensor(
                    gv(1, 2), he1, cw0, scr(cl, 1, 2), ALU.mult, ALU.add))
        vcar = None
        for cl in range(4):
            c = cg * 4 + cl
            hb = banks[cl]
            vcar = k.rec("ve", lambda e, c=c, hb=hb: e.tensor_copy(
                hedge[:, c * 2:c * 2 + 2], hb[:, 510:512]), inc="ve",
                drain=(cl == 0))
        vsil = None
        ncol = 511 if isfirst else 512
        for cl in range(4):
            c = cg * 4 + cl
            vsil = k.rec("sc", lambda e, c=c, ncol=ncol: e.activation(
                g[:, c * 512:c * 512 + ncol], g[:, c * 512:c * 512 + ncol],
                AF.Silu), wait=[("ve", vcar)] if cl == 0 else None,
                inc="sc", drain=(cl == 0))
        return vcar, vsil

    def round_hidden_grouped(w, v_rows, rh_gd=0):
        vR = None
        vpool = None
        for gi in range(8):
            h0 = gi * 4
            for cl in range(4):
                hc = h0 + cl
                vpool = k.rec("gp", lambda e, hc=hc, w=w: e.tensor_tensor(
                    g[:, hc * 512:hc * 512 + w],
                    g[:, hc * 512:hc * 512 + w], hbc[:, 0:w], ALU.mult),
                    wait=([("ve", v_rows), ("sc", k.v("sc"))]
                          if (gi == 0 and cl == 0) else None),
                    inc="gp", drain=(cl == 0))
            for cl in range(4):
                hc = h0 + cl
                wtr = None
                if cl == 0:
                    wtr = [("gp", vpool)]
                    if gi == 0 and rh_gd:
                        wtr.append(("gd", rh_gd))
                vR = k.rec("ve", lambda e, hc=hc, w=w: e.tensor_scalar(
                    Rh[:, hc * 512:hc * 512 + w],
                    g[:, hc * 512:hc * 512 + w], MAGIC, -MAGIC, ALU.add,
                    ALU.add), wait=wtr, inc="ve", drain=(cl == 0))
        return vR, vpool

    def conv_chunk(c, hb, v_te, isfirst, w):
        gv = lambda a, b: g[:, c * 512 + a:c * 512 + b]
        cw2 = cw[:, c * 3 + 2:c * 3 + 3]
        cw1 = cw[:, c * 3 + 1:c * 3 + 2]
        cw0 = cw[:, c * 3 + 0:c * 3 + 1]
        cbc = cb[:, c:c + 1]
        he0 = hedge[:, c * 2:c * 2 + 1]
        he1 = hedge[:, c * 2 + 1:c * 2 + 2]
        scr = bca[:, 1024:1536]
        wt = [("te", v_te)]
        if isfirst:
            k.rec("ve", lambda e: e.tensor_scalar(
                scr[:, 0:511], hb[:, 1:512], cw2, cbc, ALU.mult, ALU.add),
                wait=wt, drain=True)
            k.rec("ve", lambda e: e.scalar_tensor_tensor(
                scr[:, 0:511], hb[:, 0:511], cw1, scr[:, 0:511], ALU.mult,
                ALU.add), drain=True)
            k.rec("ve", lambda e: e.scalar_tensor_tensor(
                gv(1, 511), hb[:, 0:510], cw0, scr[:, 1:511], ALU.mult,
                ALU.add), drain=True)
            k.rec("ve", lambda e: e.tensor_copy(gv(0, 1), scr[:, 0:1]),
                  drain=True)
        else:
            k.rec("ve", lambda e: e.tensor_scalar(
                scr[:, 0:512], hb[:, 0:512], cw2, cbc, ALU.mult, ALU.add),
                wait=wt, drain=True)
            k.rec("ve", lambda e: e.scalar_tensor_tensor(
                scr[:, 1:512], hb[:, 0:511], cw1, scr[:, 1:512], ALU.mult,
                ALU.add), drain=True)
            k.rec("ve", lambda e: e.scalar_tensor_tensor(
                scr[:, 0:1], he1, cw1, scr[:, 0:1], ALU.mult, ALU.add),
                drain=True)
            k.rec("ve", lambda e: e.scalar_tensor_tensor(
                gv(2, 512), hb[:, 0:510], cw0, scr[:, 2:512], ALU.mult,
                ALU.add), drain=True)
            k.rec("ve", lambda e: e.scalar_tensor_tensor(
                gv(0, 1), he0, cw0, scr[:, 0:1], ALU.mult, ALU.add),
                drain=True)
            k.rec("ve", lambda e: e.scalar_tensor_tensor(
                gv(1, 2), he1, cw0, scr[:, 1:2], ALU.mult, ALU.add),
                drain=True)
        k.rec("ve", lambda e: e.tensor_copy(he0, hb[:, 510:511]), drain=True)
        vv = k.rec("ve", lambda e: e.tensor_copy(he1, hb[:, 511:512]),
                   inc="ve", drain=True)
        k.rec("sc", lambda e, w=w: e.activation(gv(0, w), gv(0, w),
                                                AF.Silu),
              wait=[("ve", vv)], drain=True)

    def tail_col():
        he0v = hedge[:].rearrange("p (c e) -> p c e", e=2)[:, :, 0:1]
        he1v = hedge[:].rearrange("p (c e) -> p c e", e=2)[:, :, 1:2]
        cw0v = cw[:].rearrange("p (c e) -> p c e", e=3)[:, :, 0:1]
        cw1v = cw[:].rearrange("p (c e) -> p c e", e=3)[:, :, 1:2]
        cbv = cb[:].rearrange("p (c e) -> p c e", e=1)
        gtv = g[:].rearrange("p (c t) -> p c t", t=512)[:, :, 0:1]
        scr = bca[:, 1504:1536].rearrange("p (c e) -> p c e", e=1)
        k.rec("ve", lambda e: e.tensor_tensor(scr, he0v, cw0v, ALU.mult),
              drain=True)
        k.rec("ve", lambda e: e.tensor_tensor(gtv, he1v, cw1v, ALU.mult),
              drain=True)
        k.rec("ve", lambda e: e.tensor_tensor(gtv, gtv, scr, ALU.add),
              drain=True)
        vv = k.rec("ve", lambda e: e.tensor_tensor(gtv, gtv, cbv, ALU.add),
                   inc="ve", drain=True)
        k.rec("sc", lambda e: e.activation(
            g[:].rearrange("p (c t) -> p c t", t=512)[:, :, 0:1],
            g[:].rearrange("p (c t) -> p c t", t=512)[:, :, 0:1], AF.Silu),
            wait=[("ve", vv)], drain=True)

    def hidden_rows(w):
        vs = k.bar("sc", inc="sc")
        gvv = g[:].rearrange("p (c t) -> p t c", c=HCH)[:, 0:w, :]
        vv = k.rec("ve", lambda e: e.reduce_max(
            bca[:, 512:512 + w], gvv, AX.X, apply_absolute_value=True),
            wait=[("sc", vs)], inc="ve", drain=True)
        vv = cross_max(bca[:, 512:], w, rows[0:1, 1024:1024 + w],
                       [("ve", vv)])
        k.rec("ve", lambda e: e.tensor_single_scalar(
            rows[0:1, 1024:1024 + w], rows[0:1, 1024:1024 + w], 1e-5,
            ALU.max), wait=[("ve", vv)], drain=True)
        vv = k.rec("ve", lambda e: e.tensor_scalar_mul(
            rows[0:1, 2560:2560 + w], rows[0:1, 1024:1024 + w],
            0.5 / 127.0), inc="ve", drain=True)
        k.rec("sc", lambda e: e.activation(
            rows[0:1, 1024:1024 + w], rows[0:1, 1024:1024 + w], AF.Ln),
            wait=[("ve", vv)], drain=True)
        vs = k.rec("sc", lambda e: e.activation(
            rows[0:1, 1024:1024 + w], rows[0:1, 1024:1024 + w], AF.Exp,
            scale=-1.0, bias=LN127), inc="sc", drain=True)
        return bcast_to(hbc, rows[0:1, 1024:1024 + w], w,
                        [("sc", vs), ("ve", k.v("ve"))])

    def ffn(x_src, x1_dst, do_ln2, rq_src, wait0):
        wins = []
        for sqi in range(NB):
            wins.append((sqi, 0, 0, 511))
            for kk in range(1, 4):
                wins.append((sqi, 1, kk * 512, 512))
            wins.append((sqi, 2, None, 1))
        NW = len(wins)
        wb_free = {id(wba): 0, id(wbb): 0}
        pb_free = {}
        HB = [psA, psB, psE, psF, psG]
        v_xfree = {}
        v_xload = {}
        v_rpq = {}
        st8 = {"x1gd": 0,
               "y2gd": 0,
               "w1te": 0}

        def xload(wi):
            sqi, kind, t0, w = wins[wi]
            xcur = xa if wi % 2 == 0 else xb
            wt = list(wait0) if wi <= 1 else [("ve", v_xfree[wi - 2])]
            vx = k.dma("sy", xcur[:].rearrange("p (c t) -> p c t", c=NCH),
                       _drv(x_src)[:, :, sqi * S + t0:sqi * S + t0 + TT],
                       wait=wt)
            v_xload[wi] = vx
            k.rec("ve", lambda e, xcur=xcur, wi=wi: e.tensor_copy(
                xedge[:, (wi % 4) * 8:(wi % 4) * 8 + 8].rearrange(
                    "p (c t) -> p c t", t=1),
                xcur[:].rearrange("p (c t) -> p c t", c=NCH)
                [:, :, 511:512]), wait=[("ld", vx)], inc="ve", drain=True)

        def prep_rp(wi, gp_wait):
            sqi, kind, t0, w = wins[wi]
            xcur = xa if wi % 2 == 0 else xb
            if rq_src is None:
                v = ln_quant(xcur, TT,
                             lambda ch: rp[:, ch * 512:(ch + 1) * 512],
                             [("ld", v_xload[wi]), ("te", st8["w1te"])],
                             xc_wait=[("gp", gp_wait)], xcb=g)
                v_rpq[wi] = ("ve", v)
            else:
                v_rpq[wi] = ("ld", k.dma(
                    "sy", rp[:].rearrange("p (c t) -> p c t", c=NCH),
                    _drv(rq_src)[:, :, sqi * S + t0:sqi * S + t0 + TT],
                    wait=[("te", st8["w1te"]), ("ve", k.v("ve")),
                          ("gd", k.v("gd"))]))
            pb_free[id(psF)] = {"ve": k.v("ve"), "sc": k.v("sc")}
            pb_free[id(psG)] = {"ve": k.v("ve"), "sc": k.v("sc")}

        xload(0)
        if NW > 1 and wins[1][1] != 2:
            xload(1)
        prep_rp(0, 0)
        for wi, (sqi, kind, t0, w) in enumerate(wins):
            base = sqi * S
            xcur = xa if (wi % 2 == 0) else xb
            gw0 = (t0 if kind == 0 else t0 - 1) if kind != 2 else S - 1
            if kind != 2:
                for cg in range(8):
                    wb = wba if cg % 2 == 0 else wbb
                    vw = k.dma("sc", wb[:].rearrange("p (c t) -> p c t",
                                                     c=NCH),
                               _drv(wq1s)[:, :, cg * 512:(cg + 1) * 512],
                               wait=[("te", wb_free[id(wb)])])
                    banks = [HB[(cg * 4 + cl) % 5] for cl in range(4)]
                    vts = []
                    for cl in range(4):
                        hb = banks[cl]
                        pf = pb_free.get(id(hb), {"ve": 0, "sc": 0})
                        vt = None
                        for ic in range(NCH):
                            vt = k.rec("te", lambda e, ic=ic, cl=cl, hb=hb,
                                       wb=wb: e.matmul(
                                hb[:, 0:TT],
                                wb[:, ic * 512 + cl * 128:
                                   ic * 512 + cl * 128 + 128],
                                rp[:, ic * 512:(ic + 1) * 512],
                                start=(ic == 0), stop=(ic == NCH - 1)),
                                wait=[("sd", vw), v_rpq[wi],
                                      ("ve", pf["ve"]), ("sc", pf["sc"])]
                                if ic == 0 else None, inc="te")
                        vts.append(vt)
                    xw = [("gd", st8["x1gd"]), ("ve", k.v("ve"))] \
                        if cg == 0 else []
                    conv_group(cg, banks, vts, kind == 0, w, xw)
                    for cl in range(4):
                        pb_free[id(banks[cl])] = {"ve": k.v("ve"),
                                                  "sc": k.v("sc")}
                    wb_free[id(wb)] = k.v("te")
                st8["w1te"] = k.v("te")
            else:
                tail_col()
            vbc = hidden_rows(w)
            pb_free[id(psF)] = {"ve": k.v("ve"), "sc": k.v("sc")}
            pb_free[id(psG)] = {"ve": k.v("ve"), "sc": k.v("sc")}
            vR, vpool = round_hidden_grouped(w, vbc, rh_gd=st8["y2gd"])
            vbb_ = bcast_to(hbc[:, 512:1024], rows[0:1, 2560:2560 + w], w,
                            [("ve", vR)])
            nx = wi + 1
            if nx < NW and wins[nx][1] != 2:
                prep_rp(nx, vpool)
            for fc in range(NCH):
                wb = wba if fc % 2 == 0 else wbb
                vw = k.dma("gp", wb[:, 0:4096].rearrange(
                    "p (c t) -> p c t", c=HCH),
                    _drv(wq2s)[:, :, fc * 128:(fc + 1) * 128],
                    wait=[("te", wb_free[id(wb)])])
                ob_ = psA if fc % 2 == 0 else psB
                pf = pb_free.get(id(ob_), {"ve": 0, "sc": 0})
                vt = None
                for hc in range(HCH):
                    vt = k.rec("te", lambda e, hc=hc, ob_=ob_, wb=wb, w=w:
                               e.matmul(
                        ob_[:, 0:w], wb[:, hc * 128:(hc + 1) * 128],
                        Rh[:, hc * 512:hc * 512 + w],
                        start=(hc == 0), stop=(hc == HCH - 1)),
                        wait=[("gd", vw), ("ve", vR), ("ve", pf["ve"]),
                              ("sc", pf["sc"])]
                        if hc == 0 else None, inc="te")
                wb_free[id(wb)] = k.v("te")
                a2 = lambda a, b, fc=fc: ar2[:, fc * 512 + a:fc * 512 + b]
                k.rec("ve", lambda e, ob_=ob_, w=w, a2=a2: e.tensor_tensor(
                    a2(0, w), ob_[:, 0:w], hbc[:, 512:512 + w], ALU.mult),
                    wait=[("te", vt), ("ve", vbb_), ("gd", st8["x1gd"])],
                    drain=True)
                ecol = (wi - 1) % 4 * 8 + fc
                if kind == 0:
                    vv = k.rec("ve", lambda e, fc=fc, w=w, xcur=xcur, a2=a2:
                               e.tensor_tensor(
                        a2(0, w), a2(0, w),
                        xcur[:, fc * 512:fc * 512 + w], ALU.add),
                        inc="ve", drain=True)
                elif kind == 1:
                    k.rec("ve", lambda e, ecol=ecol, a2=a2:
                          e.tensor_tensor(
                        a2(0, 1), a2(0, 1),
                        xedge[:, ecol:ecol + 1], ALU.add), drain=True)
                    vv = k.rec("ve", lambda e, fc=fc, w=w, xcur=xcur, a2=a2:
                               e.tensor_tensor(
                        a2(1, w), a2(1, w),
                        xcur[:, fc * 512:fc * 512 + w - 1], ALU.add),
                        inc="ve", drain=True)
                else:
                    vv = k.rec("ve", lambda e, ecol=ecol, a2=a2:
                               e.tensor_tensor(
                        a2(0, 1), a2(0, 1),
                        xedge[:, ecol:ecol + 1], ALU.add),
                        inc="ve", drain=True)
                pb_free[id(ob_)] = {"ve": k.v("ve"), "sc": k.v("sc")}
            v_xfree[wi] = vv
            vv = k.bar("ve", inc="ve")
            vst = k.dma("gp",
                        _drv(x1_dst)[:, :, base + gw0:base + gw0 + w],
                        ar2[:].rearrange("p (c t) -> p c t", c=NCH)
                        [:, :, 0:w], wait=[("ve", vv)], slow=(w == 1))
            st8["x1gd"] = vst
            if do_ln2:
                vy = ln2_window(w)
                vst = k.dma("gp",
                            _drv(y2s)[:, :, base + gw0:base + gw0 + w],
                            y2st[:].rearrange("p (c t) -> p c t", c=NCH)
                            [:, :, 0:w], wait=[("ve", vy)], slow=(w == 1))
                st8["y2gd"] = vst
            if wi + 2 < NW and wins[wi + 2][1] != 2:
                xload(wi + 2)
        return k.v("gd")

    def ln2_window(w):
        vt = stat_sums(ar2, w, onesc, [("ve", k.v("ve"))])
        vv = k.rec("ve", lambda e: e.tensor_scalar_mul(
            rows[0:1, 0:w], psD[0:1, 0:w], 1.0 / D), wait=[("te", vt)],
            inc="ve", drain=True)
        vv = bcast(rows[0:1, 0:w], w, [("ve", vv)])
        for ch in range(NCH):
            k.rec("ve", lambda e, ch=ch: e.tensor_tensor(
                g[:, 4096 + ch * 512:4096 + ch * 512 + w],
                ar2[:, ch * 512:ch * 512 + w], bca[:, 0:w], ALU.subtract),
                wait=[("ve", vv)] if ch == 0 else None, drain=True)
        vv = k.bar("ve", inc="ve")
        for ch in range(NCH):
            k.rec("sc", lambda e, ch=ch: e.activation(
                y2sq[:, ch * 512:ch * 512 + w],
                g[:, 4096 + ch * 512:4096 + ch * 512 + w],
                AF.Square), wait=[("ve", vv)] if ch == 0 else None,
                drain=(ch == 0))
        vs = k.bar("sc", inc="sc")
        vt = stat_sums(y2sq, w, onescb, [("sc", vs)])
        vs = rstd_rows(psD[0:1, 0:w], w, 512, wait=[("te", vt)])
        vv = bcast(rows[0:1, 512:512 + w], w, [("sc", vs)])
        vlast = None
        for ch in range(NCH):
            vlast = k.rec("ve", lambda e, ch=ch: e.tensor_tensor(
                y2st[:, ch * 512:ch * 512 + w],
                g[:, 4096 + ch * 512:4096 + ch * 512 + w],
                bca[:, 0:w], ALU.mult),
                wait=[("ve", vv)] if ch == 0 else None, inc="ve",
                drain=True)
        return vlast

    def table_gen(h, v_iot):
        thc = th2pi[:, h:h + 1]
        k.rec("ve", lambda e: e.tensor_scalar(
            tgf[:], tgr[:], thc, None, ALU.mult),
            wait=[("sc", k.v("sc")), ("te", k.v("te")), ("ld", v_iot)],
            drain=True)
        k.rec("ve", lambda e: e.tensor_scalar(
            tgf2[:], tgf[:], MAGIC, -MAGIC, ALU.add, ALU.add), drain=True)
        vv = k.rec("ve", lambda e: e.tensor_tensor(
            tgf[:], tgf[:], tgf2[:], ALU.subtract), inc="ve", drain=True)
        k.rec("sc", lambda e: e.activation(
            sinf[:], tgf[:], AF.Sin, scale=PI2), wait=[("ve", vv)],
            drain=True)
        k.rec("ve", lambda e: e.tensor_single_scalar(
            tgf2[:], tgf[:], 0.25, ALU.is_ge), drain=True)
        vv = k.rec("ve", lambda e: e.scalar_tensor_tensor(
            tgf2[:], tgf2[:], -1.0, tgf[:], ALU.mult, ALU.add), inc="ve",
            drain=True)
        k.rec("sc", lambda e: e.activation(
            cosf[:], tgf2[:], AF.Sin, scale=PI2, bias=PIH),
            wait=[("ve", vv)], drain=True)
        return k.rec("ve", lambda e, h=h: e.tensor_scalar(
            tribh[:], trib[:], esm1c[:, h:h + 1], None, ALU.mult),
            wait=[("ve", v_const)], inc="ve", drain=True)

    def attention():
        v_y2ld = k.dma("sy", y2[:].rearrange("p (c t) -> p c t", c=NCH),
                       _drv(y2s), wait=[("gd", k.v("gd"))])
        v_iot = k.dma("sy", tgr[:], iot,
                      wait=[("ve", k.v("ve")), ("sc", k.v("sc")),
                            ("te", k.v("te"))])
        v_wqh_free = 0
        st = {"uh": k.v("ve"), "sq_sc": k.v("sc")}
        for h in range(NH):
            vwq = None
            for pi, po in enumerate((0, D, 2 * D, 3 * D)):
                vwq = k.dma("gp",
                            wqh[:].rearrange("p (a c t) -> p a c t", a=4,
                                             c=NCH)[:, pi],
                            _drv(qwT)[:, :, po + h * 128:po + h * 128 + 128],
                            wait=[("te", v_wqh_free)] if pi == 0 else None)
            vmask = table_gen(h, v_iot)
            if vmask > st["uh"]:
                st["uh"] = vmask
            for b in range(NB):
                v_wqh_free = attn_bh(h, b, vwq, v_y2ld, vmask, st)
        if AMODE in ("proj", "state", "nostats"):
            return k.v("sc")
        vs = k.rec("sc", lambda e: e.activation(
            rsqtm[:], psD[:, 0:32], AF.Ln, bias=1e-6, scale=1.0 / D),
            wait=[("te", k.v("te"))], inc="sc", drain=True)
        return k.rec("sc", lambda e: e.activation(
            rsqtm[:], rsqtm[:], AF.Exp, scale=-0.5), inc="sc", drain=True)

    def attn_bh(h, b, vwq, v_y2ld, vmask, st):
        v0 = [("gd", vwq), ("ld", v_y2ld), ("ve", st["uh"]),
              ("sc", st["sq_sc"])]
        for pi, name in enumerate("qkvu"):
            for tb in range(4):
                pb = psA if tb % 2 == 0 else psB
                vt = None
                for ic in range(NCH):
                    wt = None
                    if ic == 0:
                        wt = v0 + [("sc", k.v("sc"))]
                        v0 = []
                    vt = k.rec("te", lambda e, pi=pi, tb=tb, ic=ic, pb=pb:
                               e.matmul(
                        pb[:, 0:512],
                        wqh[:].rearrange("p (a c t) -> p a c t", a=4,
                                         c=NCH)[:, pi, ic],
                        y2[:, ic * T + b * S + tb * 512:
                           ic * T + b * S + (tb + 1) * 512],
                        start=(ic == 0), stop=(ic == NCH - 1)),
                        wait=wt, inc="te")
                bcol = qb[:, pi * 8 + h:pi * 8 + h + 1]
                if name in "qk":
                    dst = silq if name == "q" else silk
                    k.rec("sc", lambda e, dst=dst, tb=tb, pb=pb, bcol=bcol:
                          e.activation(dst[:, tb * 512:(tb + 1) * 512],
                                       pb[:, 0:512], AF.Silu, bias=bcol),
                          wait=[("te", vt)], inc="sc", drain=True)
                elif name == "u":
                    k.rec("sc", lambda e, tb=tb, pb=pb, bcol=bcol:
                          e.activation(uh[:, tb * 512:(tb + 1) * 512],
                                       pb[:, 0:512], AF.Identity,
                                       bias=bcol),
                          wait=[("te", vt)], inc="sc", drain=True)
                else:
                    vs = k.rec("sc", lambda e, tb=tb, pb=pb, bcol=bcol:
                               e.activation(vfm[:, 0:512], pb[:, 0:512],
                                            AF.Identity, bias=bcol),
                               wait=[("te", vt)], inc="sc", drain=True)
                    vt2 = None
                    for jj in range(4):
                        vt2 = k.rec("te", lambda e, jj=jj: e.transpose(
                            psT[:, jj * 128:(jj + 1) * 128],
                            vfm[:, jj * 128:(jj + 1) * 128], identb[:]),
                            wait=[("sc", vs), ("ve", k.v("ve"))]
                            if jj == 0 else None, inc="te")
                    vv = k.rec("ve", lambda e, tb=tb: e.tensor_copy(
                        vtm[:, tb * 512:(tb + 1) * 512], psT[:, 0:512]),
                        wait=[("te", vt2)], inc="ve", drain=True)
        v_projte = k.v("te")
        vs_pr = k.bar("sc", inc="sc")
        for dst, src, tb_ in ((qcb, silq, cosf), (qsb, silq, sinf),
                              (kcb, silk, cosf), (ksb, silk, sinf)):
            k.rec("ve", lambda e, d=dst, s=src, t=tb_: e.tensor_tensor(
                d[:], s[:], t[:], ALU.mult), wait=[("sc", vs_pr)],
                drain=True)
        vv_lrpe = k.bar("ve", inc="ve")
        vv_kt = None
        for half, (srcb, dstb) in enumerate(((kcb, kcT), (ksb, ksT))):
            for grp in range(4):
                vt = None
                for jj in range(4):
                    wt_t = [("ve", k.v("ve"))] if jj == 0 else None
                    vt = k.rec("te", lambda e, srcb=srcb, grp=grp, jj=jj:
                               e.transpose(
                        psT[:, jj * 128:(jj + 1) * 128],
                        srcb[:, grp * 512 + jj * 128:
                             grp * 512 + (jj + 1) * 128], identb[:]),
                        wait=wt_t, inc="te")
                vv_kt = k.rec("ve", lambda e, dstb=dstb, grp=grp:
                              e.tensor_copy(
                    dstb[:, grp * 512:(grp + 1) * 512], psT[:, 0:512]),
                    wait=[("te", vt)], inc="ve", drain=True)
        if AMODE == "proj":
            st["uh"] = k.v("ve")
            st["sq_sc"] = k.v("sc")
            return v_projte
        UB = [psF, psG, psA, psB]
        v_pfx = []
        vtU = []
        vv_a = None

        def bsb_ap(c):
            return bsbA[:, (c - 1) * 256:c * 256] if c <= 6 else bsbB[:]

        def pfx_step(c):
            nonlocal vv_a
            if c == 0:
                k.rec("ve", lambda e: e.tensor_copy(bsbf[:],
                                                    UB[0][:, 0:256]),
                      wait=[("te", vtU[0])], drain=True)
            else:
                k.rec("ve", lambda e, ub=UB[c % 4]: e.tensor_tensor(
                    bsbf[:], bsbf[:], ub[:, 0:256], ALU.add),
                    wait=[("te", vtU[c])], drain=True)
            if c <= 6:
                vv = k.rec("ve", lambda e, dst=bsb_ap(c + 1), h=h:
                           e.tensor_scalar(dst, bsbf[:], esm1c[:, h:h + 1],
                                           None, ALU.mult), inc="ve",
                           drain=True)
                v_pfx.append(vv)
            if c == 7:
                vv_a = k.rec("ve", lambda e: e.tensor_copy(asb[:],
                                                           bsbf[:]),
                             inc="ve", drain=True)

        for c in range(8):
            ub = UB[c % 4]
            wt = None
            if c == 0:
                wt = [("ve", vv_kt)]
            elif c in (2, 3):
                wt = [("sc", vs_pr)]
            elif c >= 4:
                wt = [("ve", v_pfx[c - 4])]
            vt = None
            for half, kT in enumerate((kcT, ksT)):
                for jb in range(2):
                    jc = c * 2 + jb
                    vt = k.rec("te", lambda e, ub=ub, kT=kT,
                               off=half * 128, jc=jc, s0=(jb == 0),
                               s1=(jb == 1): e.matmul(
                        ub[:, off:off + 128],
                        kT[:, jc * 128:(jc + 1) * 128],
                        vtm[:, jc * 128:(jc + 1) * 128],
                        start=s0, stop=s1),
                        wait=wt if (half == 0 and jb == 0) else None,
                        inc="te")
            vtU.append(vt)
            if c >= 2:
                pfx_step(c - 2)
        pfx_step(6)
        pfx_step(7)
        if AMODE == "state":
            st["uh"] = k.v("ve")
            st["sq_sc"] = k.v("sc")
            return v_projte
        vintra = {}
        vmaskv = {}
        vAVt = {}
        vsq = {}
        vstat = {}
        vout = {}
        OU = [psA, psB]
        IC = [psE, psC]

        def pre(c):
            ou = OU[c % 2]
            ic_ = IC[c % 2]
            if c <= 1:
                wt = [("ve", vv_a)]
            else:
                wt = [("ve", vout[c - 2]), ("sc", vsq[c - 2])]
            k.rec("te", lambda e, c=c, ou=ou: e.matmul(
                ou[:, 0:256], asb[:, 0:128], qcb[:, c * 256:(c + 1) * 256],
                start=True, stop=False), wait=wt, inc="te")
            k.rec("te", lambda e, c=c, ou=ou: e.matmul(
                ou[:, 0:256], asb[:, 128:256],
                qsb[:, c * 256:(c + 1) * 256], start=False, stop=False),
                inc="te")
            if c > 0:
                bs = bsb_ap(c)
                k.rec("te", lambda e, c=c, ou=ou, bs=bs: e.matmul(
                    ou[:, 0:256], bs[:, 0:128],
                    qcb[:, c * 256:(c + 1) * 256], start=False,
                    stop=False), wait=[("ve", v_pfx[c - 1])], inc="te")
                k.rec("te", lambda e, c=c, ou=ou, bs=bs: e.matmul(
                    ou[:, 0:256], bs[:, 128:256],
                    qsb[:, c * 256:(c + 1) * 256], start=False,
                    stop=False), inc="te")
            wti = [("ve", vmaskv[c - 2])] if c >= 2 else None
            k.rec("te", lambda e, c=c, ic_=ic_: e.matmul(
                ic_[:, 0:256], kcb[:, c * 256:c * 256 + 128],
                qcb[:, c * 256:(c + 1) * 256], start=True, stop=False),
                wait=wti, inc="te")
            k.rec("te", lambda e, c=c, ic_=ic_: e.matmul(
                ic_[:, 0:256], ksb[:, c * 256:c * 256 + 128],
                qsb[:, c * 256:(c + 1) * 256], start=False, stop=True),
                inc="te")
            k.rec("te", lambda e, c=c, ic_=ic_: e.matmul(
                ic_[:, 256:384], kcb[:, c * 256 + 128:c * 256 + 256],
                qcb[:, c * 256 + 128:(c + 1) * 256], start=True,
                stop=False), inc="te")
            vintra[c] = k.rec("te", lambda e, c=c, ic_=ic_: e.matmul(
                ic_[:, 256:384], ksb[:, c * 256 + 128:c * 256 + 256],
                qsb[:, c * 256 + 128:(c + 1) * 256], start=False,
                stop=True), inc="te")

        def maskev(c):
            ic_ = IC[c % 2]
            fp = fsb[:, (c % 2) * 384:(c % 2) * 384 + 384]
            wt = [("te", vintra[c])]
            if c >= 2:
                wt.append(("te", vAVt[c - 2]))
            k.rec("ve", lambda e, ic_=ic_, fp=fp: e.tensor_tensor(
                fp[:, 0:128], ic_[:, 0:128], tribh[:], ALU.mult),
                wait=wt, drain=True)
            k.rec("ve", lambda e, ic_=ic_, fp=fp, h=h: e.tensor_scalar(
                fp[:, 128:256], ic_[:, 128:256], esm1c[:, h:h + 1], None,
                ALU.mult), drain=True)
            vmaskv[c] = k.rec("ve", lambda e, ic_=ic_, fp=fp:
                              e.tensor_tensor(
                fp[:, 256:384], ic_[:, 256:384], tribh[:], ALU.mult),
                inc="ve", drain=True)

        def av(c):
            ou = OU[c % 2]
            fp = fsb[:, (c % 2) * 384:(c % 2) * 384 + 384]
            k.rec("te", lambda e, c=c, ou=ou, fp=fp: e.matmul(
                ou[:, 128:256], vtm[:, c * 256 + 128:c * 256 + 256],
                fp[:, 256:384], start=False, stop=False),
                wait=[("ve", vmaskv[c])], inc="te")
            vAVt[c] = k.rec("te", lambda e, c=c, ou=ou, fp=fp: e.matmul(
                ou[:, 0:256], vtm[:, c * 256:c * 256 + 128], fp[:, 0:256],
                start=False, stop=True), inc="te")

        def outev(c):
            ou = OU[c % 2]
            pc0 = h * T + b * S + c * 256
            if AMODE == "nostats":
                vsq[c] = k.v("sc")
                vout[c] = k.rec("ve", lambda e, c=c, ou=ou, pc0=pc0:
                                e.tensor_tensor(
                    pat[:, pc0:pc0 + 256], ou[:, 0:256],
                    uh[:, c * 256:(c + 1) * 256], ALU.mult),
                    wait=[("te", vAVt[c])], inc="ve", drain=True)
                return
            wt = [("te", vAVt[c])]
            if c >= 1:
                wt.append(("te", vstat[c - 1]))
            vsq[c] = k.rec("sc", lambda e, c=c, ou=ou: e.activation(
                sqat[:], ou[:, 0:256], AF.Square), wait=wt, inc="sc",
                drain=True)
            vout[c] = k.rec("ve", lambda e, c=c, ou=ou, pc0=pc0:
                            e.tensor_tensor(
                pat[:, pc0:pc0 + 256], ou[:, 0:256],
                uh[:, c * 256:(c + 1) * 256], ALU.mult),
                wait=[("sc", vsq[c])], inc="ve", drain=True)

        def stat(c):
            vt = None
            for jb in range(2):
                col = b * 16 + c * 2 + jb
                vt = k.rec("te", lambda e, c=c, jb=jb, col=col, h=h, b=b:
                           e.matmul(
                    psD[:, col:col + 1],
                    sqat[:, jb * 128:(jb + 1) * 128],
                    onescb[:, 0:1],
                    start=(h == 0 and b == 0 and c == 0 and jb == 0),
                    stop=(h == NH - 1 and b == NB - 1 and c == 7
                          and jb == 1)),
                    wait=[("sc", vsq[c])] if jb == 0 else None, inc="te")
            vstat[c] = vt

        pre(0)
        maskev(0)
        pre(1)
        maskev(1)
        for c in range(8):
            av(c)
            if c >= 1 and AMODE != "nostats":
                stat(c - 1)
            outev(c)
            if c + 2 < 8:
                pre(c + 2)
                maskev(c + 2)
        if AMODE != "nostats":
            stat(7)
        st["uh"] = vout[7]
        st["sq_sc"] = vsq[7]
        return v_projte

    def attn_out(v_rsq):
        k.dma("gp", owb[:].rearrange("p (c t) -> p c t", c=NCH),
              _drv(owT), wait=[("te", k.v("te")), ("ve", k.v("ve")),
                               ("sc", k.v("sc"))])
        v_st = 0
        v_xfree = ("sc", k.v("sc"))
        pb4 = {id(psA): 0, id(psB): 0}
        for tl in range(8):
            b, tb = tl // 4, tl % 4
            base = b * S + tb * 512
            vt2 = None
            for jj in range(4):
                col = b * 16 + tb * 4 + jj
                vt2 = k.rec("te", lambda e, col=col, jj=jj: e.transpose(
                    psG[0:1, jj * 128:jj * 128 + 128],
                    rsqtm[:, col:col + 1], identf[:]),
                    wait=[("sc", v_rsq)] if (tl == 0 and jj == 0) else None,
                    inc="te")
            vv = k.rec("ve", lambda e: e.tensor_copy(
                rows[0:1, 0:512], psG[0:1, 0:512]), wait=[("te", vt2)],
                inc="ve", drain=True)
            vv = bcast(rows[0:1, 0:512], 512, [("ve", vv)])
            for hc in range(NCH):
                k.rec("ve", lambda e, hc=hc, base=base: e.tensor_tensor(
                    zsb[:, hc * 512:(hc + 1) * 512],
                    pat[:, hc * T + base:hc * T + base + 512],
                    bca[:, 0:512], ALU.mult),
                    wait=[("ve", vv)] if hc == 0 else None, drain=True)
            vv = k.bar("ve", inc="ve")
            vx = k.dma("sy", x4a[:].rearrange("p (c t) -> p c t", c=NCH),
                       _drv(xs1)[:, :, base:base + 512], wait=[v_xfree])
            for fc in range(NCH):
                ob_ = psA if fc % 2 == 0 else psB
                vt = None
                for ic in range(NCH):
                    wt4 = None
                    if ic == 0:
                        wt4 = [("sc", pb4[id(ob_)])]
                        if fc == 0:
                            wt4 += [("ve", vv), ("gd", k.v("gd"))]
                    vt = k.rec("te", lambda e, fc=fc, ic=ic, ob_=ob_:
                               e.matmul(
                        ob_[:, 0:512],
                        owb[:, ic * 1024 + fc * 128:ic * 1024 + fc * 128
                            + 128],
                        zsb[:, ic * 512:(ic + 1) * 512],
                        start=(ic == 0), stop=(ic == NCH - 1)),
                        wait=wt4, inc="te")
                vs = k.rec("sc", lambda e, fc=fc, ob_=ob_: e.activation(
                    ar2[:, fc * 512:(fc + 1) * 512], ob_[:, 0:512],
                    AF.Identity, bias=ob[:, fc:fc + 1]),
                    wait=[("te", vt)], inc="sc", drain=True)
                pb4[id(ob_)] = vs
                vv2 = k.rec("ve", lambda e, fc=fc: e.tensor_tensor(
                    ar2[:, fc * 512:(fc + 1) * 512],
                    ar2[:, fc * 512:(fc + 1) * 512],
                    x4a[:, fc * 512:(fc + 1) * 512], ALU.add),
                    wait=[("sc", vs), ("ld", vx)], inc="ve", drain=True)
            v_xfree = ("ve", vv2)
            vv = k.bar("ve", inc="ve")
            vst = k.dma("gp", _drv(xs2)[:, :, base:base + 512],
                        ar2[:].rearrange("p (c t) -> p c t", c=NCH),
                        wait=[("ve", vv)])
            vq = ln_quant(ar2, 512,
                          lambda ch: rp[:, ch * 512:(ch + 1) * 512],
                          [("gd", vst), ("gd", v_st)],
                          xc_wait=[("gd", vst)], sqb=sq4)
            v_st = k.dma("gp", _drv(rq3)[:, :, base:base + 512],
                         rp[:].rearrange("p (c t) -> p c t", c=NCH),
                         wait=[("ve", vq)])
        return v_st

    wt0 = [("gd", v_prep)] if "prep" in phases else []
    last_stage = "none"
    if "ffna" in phases:
        ffn(xT, xs1, "attn" in phases, None, wt0)
        last_stage = "ffna"
    if "attn" in phases:
        v_rsq = attention()
        last_stage = "attn"
    if "attno" in phases:
        attn_out(v_rsq)
        last_stage = "attno"
    if "ffnc" in phases:
        ffn(xs2, y, False, rq3, [("gd", k.v("gd")), ("ve", k.v("ve")),
                                 ("sc", k.v("sc")), ("te", k.v("te"))])
        last_stage = "ffnc"

    if last_stage != "ffnc":
        src = {"ffna": xs1, "attn": xs1, "attno": xs2}.get(last_stage, xT)
        if debug_out is not None:
            src = {"xs1": xs1, "xs2": xs2, "y2s": None, "xT": xT}[debug_out]
        wt = [("gd", k.v("gd")), ("ve", k.v("ve")), ("sc", k.v("sc")),
              ("te", k.v("te")), ("ld", k.v("ld"))]
        for i in range(8):
            vx = k.dma("sy", xa[:].rearrange("p (c t) -> p c t", c=1),
                       _drv(src)[:, i:i + 1, :],
                       wait=wt if i == 0 else [("gd", k.v("gd"))])
            k.dma("gp", _drv(y)[:, i:i + 1, :],
                  xa[:].rearrange("p (c t) -> p c t", c=1),
                  wait=[("ld", vx)])

    fin = [("gd", k.v("gd")), ("ld", k.v("ld")), ("sd", k.v("sd")),
           ("vd", k.v("vd")), ("ve", k.v("ve")), ("sc", k.v("sc")),
           ("te", k.v("te"))]
    k.rec("sy", lambda e: e.nop(), wait=fin)

    build._k = k
    with contextlib.ExitStack() as st:
        sems = {}
        for s in ("ld", "gd", "sd", "vd", "te", "ve", "sc", "gp"):
            sems[s] = st.enter_context(nc.semaphore("sem_" + s))
        blk = st.enter_context(nc.Block())
        blk.sync(lambda e: k.emit(sems, "sy", e))
        blk.tensor(lambda e: k.emit(sems, "te", e))
        blk.vector(lambda e: k.emit(sems, "ve", e))
        blk.scalar(lambda e: k.emit(sems, "sc", e))
        blk.gpsimd(lambda e: k.emit(sems, "gp", e))
    return nc


def _inputs_for_core(ci, x, slope, qw, qbv, ow, obv, theta, w1, cwv, cbv,
                     w2, consts):
    xc = np.ascontiguousarray(
        x[2 * ci:2 * ci + 2].reshape(T, D).T.astype(np.float32))
    d = dict(xT=xc)
    d.update(consts)
    return d


def kernel(L0_attn_x, L0_attn_slope_rate, L0_attn_qkvu_w, L0_attn_qkvu_b,
           L0_attn_out_w, L0_attn_out_b, L0_attn_theta, L0_ffn_w1,
           L0_ffn_conv_w, L0_ffn_conv_b, L0_ffn_w2, _phases=None):
    x = np.asarray(L0_attn_x, dtype=np.float32)
    consts = host_consts(L0_attn_slope_rate, L0_attn_qkvu_w, L0_attn_qkvu_b,
                         L0_attn_out_w, L0_attn_out_b, L0_attn_theta,
                         L0_ffn_w1, L0_ffn_conv_w, L0_ffn_conv_b, L0_ffn_w2)
    nc = build(_phases or ("prep", "ffna", "attn", "attno", "ffnc"))
    in_maps = []
    for ci in range(8):
        xc = np.ascontiguousarray(
            x[2 * ci:2 * ci + 2].reshape(T, D).T.astype(np.float32))
        m = dict(xT=xc)
        m.update(consts)
        in_maps.append(m)
    res = run_bass_kernel_spmd(nc, in_maps, core_ids=list(range(8)))
    outs = []
    for r in res.results:
        yT = np.asarray(r["y"])
        outs.append(yT.reshape(D, NB, S).transpose(1, 2, 0))
    return np.concatenate(outs, axis=0).astype(np.float32)


def host_consts(slope, qw, qbv, ow, obv, theta, w1, cwv, cbv, w2):
    w1T = np.ascontiguousarray(np.asarray(w1, np.float32).T)
    w2T = np.ascontiguousarray(np.asarray(w2, np.float32).T)
    qwT = np.ascontiguousarray(np.asarray(qw, np.float32).T)
    owT = np.ascontiguousarray(np.asarray(ow, np.float32).T)
    qbT = np.ascontiguousarray(
        np.asarray(qbv, np.float32).reshape(32, 128).T)
    obT = np.ascontiguousarray(
        np.asarray(obv, np.float32).reshape(8, 128).T)
    cbT = np.ascontiguousarray(
        np.asarray(cbv, np.float32).reshape(32, 128).T)
    cwT = np.ascontiguousarray(
        np.asarray(cwv, np.float32).reshape(32, 128, 3)
        .transpose(1, 0, 2).reshape(128, 96))
    thT = np.ascontiguousarray(
        np.asarray(theta, np.float32).reshape(8, 128).T)
    slp = np.ascontiguousarray(
        np.asarray(slope, np.float32).reshape(1, 8))
    idn = np.eye(128, dtype=np.float32)
    tri = np.triu(np.ones((128, 128), np.float32))
    iot = np.broadcast_to(np.arange(2048, dtype=np.float32),
                          (128, 2048)).copy()
    return dict(w1T=w1T, w2T=w2T, qwT=qwT, owT=owT, qbT=qbT, obT=obT,
                cbT=cbT, cwT=cwT, thT=thT, slp=slp, idn=idn, tri=tri,
                iot=iot)


# version 13
# speedup vs baseline: 1.1168x; 1.1168x over previous
import contextlib
import numpy as np
import ml_dtypes
import concourse.bass as bass
import concourse.mybir as mybir
from concourse.bass_utils import run_bass_kernel_spmd

F32 = mybir.dt.float32
BF16 = mybir.dt.bfloat16
I32 = mybir.dt.int32
AF = mybir.ActivationFunctionType
ALU = mybir.AluOpType
AX = mybir.AxisListType
BF = ml_dtypes.bfloat16

NB = 2
S = 2048
T = NB * S
D = 1024
NCH = 8
HID = 4096
HCH = 32
NH = 8
TT = 512
CS = 256
AMODE = "full"
LN127 = float(np.log(np.float64(127.0)))
PI2 = float(2.0 * np.pi)
PIH = float(np.pi / 2.0)
MAGIC = float(3 << 22)
ENGS = ("sy", "te", "ve", "sc", "gp")
KB_ = 1024


class K:

    SELF = {"te": "te", "ve": "ve", "sc": "sc", "gp": "gp"}

    def __init__(self, nc):
        self.nc = nc
        self.recs = {e: [] for e in ENGS}
        self.cnt = {k: 0 for k in ("ld", "gd", "te", "ve", "sc", "gp")}
        self.pend = {e: [] for e in ENGS}

    def rec(self, eng, fn, wait=None, inc=None, drain=False):
        if fn is None:
            return self.bar(eng, wait=wait, inc=inc)
        sem = self.SELF.get(eng)
        w = self.pend[eng] + list(wait or ())
        self.pend[eng] = []
        if sem is None:
            self.recs[eng].append((fn, tuple(w), None))
            return self.cnt.get(eng)
        if drain and self.cnt[sem] > 0:
            w.append((sem, self.cnt[sem]))
        self.cnt[sem] += 1
        out = self.cnt[sem]
        self.recs[eng].append((fn, tuple(w), sem))
        return out

    def bar(self, eng, wait=None, inc=None):
        if wait:
            self.pend[eng].extend(wait)
        return self.cnt.get(self.SELF.get(eng, eng), 0)

    def dma(self, eng, dst, src, wait=None, slow=False):
        sem = "ld" if eng == "sy" else "gd"
        prev = self.cnt[sem]
        self.cnt[sem] += 16
        out = self.cnt[sem]
        w = self.pend[eng] + list(wait or ())
        self.pend[eng] = []
        if prev > 0:
            w.append((sem, prev))
        self.recs[eng].append(
            (lambda e, d=dst, s=src, sl=slow: e.dma_start(
                d, s, allow_slow_non_contiguous=sl),
             tuple(w), sem))
        return out

    def v(self, sem):
        return self.cnt[sem]

    def emit(self, sems, eng, e):
        for fn, waits, inc in self.recs[eng]:
            dedup = {}
            for s, val in waits:
                if val > dedup.get(s, -1):
                    dedup[s] = val
            for s, val in dedup.items():
                if val > 0:
                    e.wait_ge(sems[s], val)
            inst = fn(e)
            if inc is not None:
                inst.then_inc(sems[inc], 16 if inc in ("ld", "gd") else 1)


def _drv(ap):
    return ap.rearrange("(c p) t -> p c t", p=128)


def build(phases=("prep", "ffna", "attn", "attno", "ffnc"), debug_out=None,
          taps=False):
    nc = bass.Bass("TRN2", target_bir_lowering=False)
    k = K(nc)
    k.taps = taps
    k.tapped = {}

    xT = nc.dram_tensor("xT", [D, T], F32, kind="ExternalInput").ap()
    w1T = nc.dram_tensor("w1T", [D, HID], F32, kind="ExternalInput").ap()
    w2T = nc.dram_tensor("w2T", [HID, D], F32, kind="ExternalInput").ap()
    qwT = nc.dram_tensor("qwT", [D, 4 * D], F32, kind="ExternalInput").ap()
    owT = nc.dram_tensor("owT", [D, D], F32, kind="ExternalInput").ap()
    qbT = nc.dram_tensor("qbT", [128, 32], F32, kind="ExternalInput").ap()
    obT = nc.dram_tensor("obT", [128, 8], F32, kind="ExternalInput").ap()
    cbT = nc.dram_tensor("cbT", [128, 32], F32, kind="ExternalInput").ap()
    cwT = nc.dram_tensor("cwT", [128, 96], F32, kind="ExternalInput").ap()
    thT = nc.dram_tensor("thT", [128, 8], F32, kind="ExternalInput").ap()
    slp = nc.dram_tensor("slp", [1, 8], F32, kind="ExternalInput").ap()
    idn = nc.dram_tensor("idn", [128, 128], F32, kind="ExternalInput").ap()
    tri = nc.dram_tensor("tri", [128, 128], F32, kind="ExternalInput").ap()
    iot = nc.dram_tensor("iot", [128, S], F32, kind="ExternalInput").ap()

    y = nc.dram_tensor("y", [D, T], F32, kind="ExternalOutput").ap()
    if taps:
        d_rp = nc.dram_tensor("d_rp", [128, 4096], BF16,
                              kind="ExternalOutput").ap()
        d_g = nc.dram_tensor("d_g", [128, HCH * 512], F32,
                             kind="ExternalOutput").ap()
        d_rh = nc.dram_tensor("d_rh", [128, HCH * 512], BF16,
                              kind="ExternalOutput").ap()
        d_ar2 = nc.dram_tensor("d_ar2", [128, 4096], F32,
                               kind="ExternalOutput").ap()
        d_rows = nc.dram_tensor("d_rows", [1, 4096], F32,
                                kind="ExternalOutput").ap()
    xs1 = nc.dram_tensor("xs1", [D, T], F32, kind="Internal").ap()
    xs2 = nc.dram_tensor("xs2", [D, T], F32, kind="Internal").ap()
    y2s = nc.dram_tensor("y2s", [D, T], BF16, kind="Internal").ap()
    rq3 = nc.dram_tensor("rq3", [D, T], BF16, kind="Internal").ap()
    wq1s = nc.dram_tensor("wq1s", [D, HID], BF16, kind="Internal").ap()
    wq2s = nc.dram_tensor("wq2s", [HID, D], BF16, kind="Internal").ap()
    wq2g = nc.dram_tensor("wq2g", [128, 32768], BF16, kind="Internal").ap()

    SBASE = 16384
    A = lambda n, sh, dt, off: nc.alloc_sbuf_tensor_at(
        n, sh, dt, offset=off + SBASE)
    identf = A("identf", [128, 128], F32, 0)
    identb = A("identb", [128, 128], BF16, 512)
    trib = A("trib", [128, 128], F32, 768)
    tribh = A("tribh", [128, 128], F32, 1280)
    qb = A("qb", [128, 32], F32, 1792)
    ob = A("ob", [128, 8], F32, 1920)
    cb = A("cb", [128, 32], F32, 1952)
    cw = A("cw", [128, 96], F32, 2080)
    thetaT = A("thetaT", [128, 8], F32, 2464)
    th2pi = A("th2pi", [128, 8], F32, 2496)
    esm1c = A("esm1c", [128, 8], F32, 2528)
    onesc = A("onesc", [128, 1], F32, 2560)
    onescb = A("onescb", [128, 1], BF16, 2592)
    onesr = A("onesr", [1, 128], F32, 2624)
    slpr = A("slpr", [1, 8], F32, 4192)
    esm1r = A("esm1r", [1, 8], F32, 4224)
    wsum = A("wsum", [128, 1], F32, 4256)
    scol = A("scol", [128, 1], F32, 4288)
    mcol = A("mcol", [128, 1], F32, 4320)
    srow = A("srow", [1, 8], F32, 4352)
    hedge = A("hedge", [128, 64], F32, 4384)
    rsqtm = A("rsqtm", [128, 32], F32, 4640)
    ceps5 = A("ceps5", [128, 1], F32, 4768)
    ceps6 = A("ceps6", [128, 1], F32, 4800)
    cl127 = A("cl127", [128, 1], F32, 4832)
    zcol = A("zcol", [128, 1], F32, 4864)
    zcolb = A("zcolb", [128, 1], BF16, 4896)
    cpih = A("cpih", [128, 1], F32, 4928)
    xedge = A("xedge", [128, 32], F32, 4960)
    nc.const_aps.register(F32, PIH, cpih[:, 0:1])
    nc.const_aps.register(F32, 1e-5, ceps5[:, 0:1])
    nc.const_aps.register(F32, 1e-6, ceps6[:, 0:1])
    nc.const_aps.register(F32, LN127, cl127[:, 0:1])
    nc.const_aps.register(F32, 0.0, zcol[:, 0:1])
    nc.const_aps.register(F32, 1.0, onesc[:, 0:1])
    nc.const_aps.register(BF16, 0.0, zcolb[:, 0:1])
    nc.const_aps.register(BF16, 1.0, onescb[:, 0:1])
    rows = A("rows", [1, 3072], F32, 8 * KB_)
    hbc = A("hbc", [128, 1024], F32, 20 * KB_)
    bca = A("bca", [128, 1536], F32, 24 * KB_)
    xa = A("xa", [128, 4096], F32, 30 * KB_)
    xb = A("xb", [128, 4096], F32, 46 * KB_)
    wba = A("wba", [128, 4096], BF16, 62 * KB_)
    wbb = A("wbb", [128, 4096], BF16, 70 * KB_)
    rp = A("rp", [128, 4096], BF16, 78 * KB_)
    Rh = A("Rh", [128, HCH * 512], BF16, 86 * KB_)
    y2st = A("y2st", [128, 4096], BF16, 86 * KB_)
    y2sq = A("y2sq", [128, 4096], BF16, 94 * KB_)
    sq = A("sq", [128, 4096], BF16, 118 * KB_)
    g = A("g", [128, HCH * 512], F32, 126 * KB_)
    ar2 = A("ar2", [128, 4096], F32, 190 * KB_)
    wfull = A("wfull", [128, 32768], F32, 30 * KB_)
    p1xs = A("p1xs", [128, 4096], F32, 158 * KB_)
    p1wq = A("p1wq", [128, 4096], BF16, 8 * KB_)

    tgr = A("tgr", [128, S], F32, 8 * KB_)
    sinf = A("sinf", [128, S], BF16, 16 * KB_)
    cosf = A("cosf", [128, S], BF16, 20 * KB_)
    tgf = A("tgf", [128, S], F32, 158 * KB_)
    tgf2 = A("tgf2", [128, S], F32, 166 * KB_)
    asb = A("asb", [128, 256], BF16, 26752)
    bsbB = A("bsbB", [128, 256], BF16, 27264)
    bsbf = A("bsbf", [128, 256], F32, 27776)
    fsb = A("fsb", [128, 768], BF16, 28800)
    bsbA = A("bsbA", [128, 1536], BF16, 5088)
    sqat = A("sqat", [128, 256], BF16, 211968)
    y2 = A("y2", [128, NCH * T], BF16, 30 * KB_)
    pat = A("pat", [128, NH * T], BF16, 94 * KB_)
    silq = A("silq", [128, S], BF16, 158 * KB_)
    silk = A("silk", [128, S], BF16, 162 * KB_)
    qcb = A("qcb", [128, S], BF16, 166 * KB_)
    qsb = A("qsb", [128, S], BF16, 170 * KB_)
    kcb = A("kcb", [128, S], BF16, 174 * KB_)
    ksb = A("ksb", [128, S], BF16, 178 * KB_)
    kcT = A("kcT", [128, S], BF16, 182 * KB_)
    ksT = A("ksT", [128, S], BF16, 186 * KB_)
    vtm = A("vtm", [128, S], BF16, 190 * KB_)
    uh = A("uh", [128, S], BF16, 194 * KB_)
    wqh = A("wqh", [128, 4096], BF16, 198 * KB_)
    vfm = A("vfm", [128, 512], BF16, 206 * KB_)

    zsb = A("zsb", [128, 4096], BF16, 30 * KB_)
    owb = A("owb", [128, 8192], BF16, 38 * KB_)
    x4a = A("x4a", [128, 4096], F32, 54 * KB_)
    sq4 = A("sq4", [128, 4096], BF16, 70 * KB_)

    psA = nc.alloc_psum_tensor("psA", [128, 512], F32)
    psB = nc.alloc_psum_tensor("psB", [128, 512], F32)
    psC = nc.alloc_psum_tensor("psC", [128, 512], F32)
    psD = nc.alloc_psum_tensor("psD", [128, 512], F32)
    psE = nc.alloc_psum_tensor("psE", [128, 512], F32)
    psF = nc.alloc_psum_tensor("psF", [128, 512], F32)
    psG = nc.alloc_psum_tensor("psG", [128, 512], F32)
    psT = nc.alloc_psum_tensor("psT", [128, 1024], BF16)

    for dst, src in [(identf, idn), (trib, tri), (qb, qbT), (ob, obT),
                     (cb, cbT), (cw, cwT), (thetaT, thT),
                     (slpr, slp)]:
        k.dma("sy", dst[:], src)
    vld0 = k.v("ld")
    k.rec("ve", lambda e: e.memset(onesc[:], 1.0))
    k.rec("ve", lambda e: e.memset(ceps5[:], 1e-5))
    k.rec("ve", lambda e: e.memset(zcol[:], 0.0))
    k.rec("ve", lambda e: e.memset(zcolb[:], 0.0))
    k.rec("ve", lambda e: e.memset(ceps6[:], 1e-6))
    k.rec("ve", lambda e: e.memset(cl127[:], LN127))
    k.rec("ve", lambda e: e.memset(onescb[:], 1.0))
    k.rec("ve", lambda e: e.memset(onesr[:], 1.0))
    k.rec("ve", lambda e: e.memset(cpih[:], PIH))
    k.rec("ve", lambda e: e.memset(hedge[:], 0.0))
    k.rec("ve", lambda e: e.tensor_scalar_mul(th2pi[:], thetaT[:], 1.0 / PI2),
          wait=[("ld", vld0)], drain=True)
    k.rec("ve", lambda e: e.tensor_copy(identb[:], identf[:]), drain=True)
    v_ms = k.bar("ve", inc="ve")
    k.rec("sc", lambda e: e.activation(esm1r[:], slpr[:], AF.Exp),
          wait=[("ld", vld0), ("ve", v_ms)])
    vs = k.bar("sc", inc="sc")
    vv = k.rec("ve", lambda e: e.tensor_scalar_add(esm1r[:], esm1r[:], -1.0),
               wait=[("sc", vs)], inc="ve", drain=True)
    vt = k.rec("te", lambda e: e.matmul(psD[0:128, 0:8], onesr[0:1, :],
                                        esm1r[0:1, :], start=True, stop=True),
               wait=[("ve", vv)], inc="te")
    v_const = k.rec("ve", lambda e: e.tensor_copy(esm1c[:], psD[:, 0:8]),
                    wait=[("te", vt)], inc="ve", drain=True)

    def wprep(wsrc, wdst, nelem, gdst=None):
        nsl = nelem // (128 * 4096)
        wfv = wfull[:, 0:nsl * 4096]
        vld = k.dma("sy",
                    wfv.rearrange("p (c t) -> p c t",
                                  c=_drv(wsrc).shape[1]),
                    _drv(wsrc),
                    wait=[("ve", k.v("ve")), ("gd", k.v("gd"))])
        vv = k.rec("ve", lambda e: e.reduce_sum(
            wsum[:], wfv, AX.X, apply_absolute_value=True),
            wait=[("ld", vld)], inc="ve", drain=True)
        vt = k.rec("te", lambda e: e.matmul(
            psD[0:1, 0:1], wsum[:, 0:1], onesc[:, 0:1], start=True,
            stop=True), wait=[("ve", vv)], inc="te")
        k.rec("ve", lambda e, n=nelem: e.tensor_scalar(
            srow[0:1, 0:1], psD[0:1, 0:1], 1.0 / n, 1e-5, ALU.mult,
            ALU.max), wait=[("te", vt)], drain=True)
        vv = k.rec("ve", lambda e: e.reciprocal(srow[0:1, 1:2],
                                                srow[0:1, 0:1]),
                   inc="ve", drain=True)
        vt = k.rec("te", lambda e: e.matmul(
            psD[0:128, 4:6], onesr[0:1, :], srow[0:1, 0:2], start=True,
            stop=True), wait=[("ve", vv)], inc="te")
        k.rec("ve", lambda e: e.tensor_copy(mcol[:, 0:1], psD[:, 4:5]),
              wait=[("te", vt)], drain=True)
        k.rec("ve", lambda e: e.tensor_copy(scol[:, 0:1], psD[:, 5:6]),
              drain=True)
        vst = 0
        for i in range(nsl):
            sl = wfull[:, i * 4096:(i + 1) * 4096]
            k.rec("ve", lambda e, sl=sl: e.tensor_scalar(
                p1xs[:], sl, scol[:, 0:1], MAGIC, ALU.mult, ALU.add),
                wait=[("gd", vst)] if vst else None, drain=True)
            k.rec("ve", lambda e: e.tensor_scalar(
                p1xs[:], p1xs[:], -MAGIC, 1.0, ALU.add, ALU.min), drain=True)
            vv = k.rec("ve", lambda e: e.tensor_scalar(
                p1wq[:], p1xs[:], -1.0, mcol[:, 0:1], ALU.max, ALU.mult),
                inc="ve", drain=True)
            ncs = _drv(wdst).shape[1] * 4096 * 128 // nelem
            vst = k.dma("gp",
                        _drv(wdst)[:, i * ncs:(i + 1) * ncs, :],
                        p1wq[:].rearrange("p (c t) -> p c t", c=ncs),
                        wait=[("ve", vv)])
            if gdst is not None:
                vst = k.dma("gp", gdst[:, i * 4096:(i + 1) * 4096],
                            p1wq[:])
        return vst

    if "prep" in phases:
        wprep(w1T, wq1s, D * HID)
        v_prep = wprep(w2T, wq2s, HID * D, gdst=wq2g)
    else:
        v_prep = 0

    def bcast(row_ap, w, wait):
        vt = k.rec("te", lambda e: e.matmul(
            psC[0:128, 0:w], onesr[0:1, :], row_ap, start=True, stop=True),
            wait=wait, inc="te")
        return k.rec("ve", lambda e: e.tensor_copy(bca[:, 0:w], psC[:, 0:w]),
                     wait=[("te", vt)], inc="ve", drain=True)

    def bcast2(row_ap, w, wait):
        vt = k.rec("te", lambda e: e.matmul(
            psC[0:128, 0:w], onesr[0:1, :], row_ap, start=True, stop=True),
            wait=wait, inc="te")
        return k.rec("ve", lambda e: e.tensor_copy(
            bca[:, 512:512 + w], psC[:, 0:w]), wait=[("te", vt)], inc="ve",
            drain=True)

    def bcast_to(dst, row_ap, w, wait):
        vt = k.rec("te", lambda e: e.matmul(
            psC[0:128, 0:w], onesr[0:1, :], row_ap, start=True, stop=True),
            wait=wait, inc="te")
        return k.rec("ve", lambda e, dst=dst: e.tensor_copy(
            dst[:, 0:w], psC[:, 0:w]), wait=[("te", vt)], inc="ve",
            drain=True)

    def cross_max(src01, w, row_dst, wait):
        nblk = (w + 127) // 128
        vts = []
        for j in range(nblk):
            wj = min(128, w - j * 128)
            vt = k.rec("te", lambda e, j=j, wj=wj: e.transpose(
                psF[0:wj, j * 128:j * 128 + 128],
                src01[:, j * 128:j * 128 + wj], identf[:]),
                wait=wait if j == 0 else None, inc="te")
            vts.append((j, wj, vt))
        v_all_tr = vts[-1][2]
        if w == 1:
            return k.rec("ve", lambda e: e.reduce_max(
                row_dst, psF[0:1, 0:128], AX.X, apply_absolute_value=True),
                wait=[("te", v_all_tr)], inc="ve", drain=True)
        vvs = []
        for j, wj, vt in vts:
            vv = k.rec("ve", lambda e, j=j, wj=wj: e.reduce_max(
                bca[0:wj, 1024 + j:1025 + j],
                psF[0:wj, j * 128:j * 128 + 128], AX.X,
                apply_absolute_value=True),
                wait=[("te", v_all_tr)] if j == 0 else None,
                inc="ve", drain=True)
            vvs.append((j, wj, vv))
        vt2 = None
        for j, wj, vv in vvs:
            vt2 = k.rec("te", lambda e, j=j, wj=wj: e.transpose(
                psG[0:1, j * 128:j * 128 + wj],
                bca[0:wj, 1024 + j:1025 + j], identf[0:wj, 0:wj]),
                wait=[("ve", vv)], inc="te")
        return k.rec("ve", lambda e, w=w: e.tensor_copy(
            row_dst, psG[0:1, 0:w]), wait=[("te", vt2)], inc="ve",
            drain=True)

    def rstd_rows(sum_ps_w, w, dst_off, wait=None):
        vv = k.rec("ve", lambda e: e.tensor_scalar_mul(
            rows[0:1, dst_off:dst_off + w], sum_ps_w, 1.0 / D),
            wait=wait, inc="ve", drain=True)
        k.rec("sc", lambda e: e.activation(
            rows[0:1, dst_off:dst_off + w], rows[0:1, dst_off:dst_off + w],
            AF.Ln, bias=1e-5), wait=[("ve", vv)], drain=True)
        return k.rec("sc", lambda e: e.activation(
            rows[0:1, dst_off:dst_off + w], rows[0:1, dst_off:dst_off + w],
            AF.Exp, scale=-0.5), inc="sc", drain=True)

    def stat_sums(src, w, ones, wait, stop_bank=None):
        vt = None
        for ch in range(NCH):
            vt = k.rec("te", lambda e, ch=ch: e.matmul(
                psD[0:1, 0:w], ones[:, 0:1],
                src[:, ch * 512:ch * 512 + w],
                start=(ch == 0), stop=(ch == NCH - 1)),
                wait=wait if ch == 0 else None, inc="te")
        return vt

    def round_chunk(dst, src, w, bc_rs_off, bc_a_off, wait, scale_first,
                    alpha_mult):
        if scale_first:
            k.rec("ve", lambda e: e.tensor_tensor(
                src, src, bca[:, bc_rs_off:bc_rs_off + w], ALU.mult),
                wait=wait, drain=True)
            wait = None
        if alpha_mult:
            k.rec("ve", lambda e: e.tensor_scalar(
                src, src, MAGIC, -MAGIC, ALU.add, ALU.add), wait=wait,
                drain=True)
            return k.rec("ve", lambda e: e.tensor_tensor(
                dst, src, bca[:, bc_a_off:bc_a_off + w], ALU.mult),
                inc="ve", drain=True)
        return k.rec("ve", lambda e: e.tensor_scalar(
            dst, src, MAGIC, -MAGIC, ALU.add, ALU.add), wait=wait,
            inc="ve", drain=True)

    def ln_quant(src, w, rdst, wait, xc_wait=None, sqb=None, xcb=None):
        xcb = ar2 if xcb is None else xcb
        vt = stat_sums(src, w, onesc,
                       list(wait) + [("ve", k.v("ve")), ("sc", k.v("sc"))])
        vv = k.rec("ve", lambda e: e.tensor_scalar_mul(
            rows[0:1, 0:w], psD[0:1, 0:w], 1.0 / D), wait=[("te", vt)],
            inc="ve", drain=True)
        vv = bcast(rows[0:1, 0:w], w, [("ve", vv)])
        for ch in range(NCH):
            wtx = ([("ve", vv)] + list(xc_wait or ())) if ch == 0 else None
            k.rec("ve", lambda e, ch=ch: e.tensor_tensor(
                xcb[:, ch * 512:ch * 512 + w],
                src[:, ch * 512:ch * 512 + w], bca[:, 0:w], ALU.subtract),
                wait=wtx, drain=True)
        vv = k.bar("ve", inc="ve")
        sqb = sq if sqb is None else sqb
        for ch in range(NCH):
            k.rec("sc", lambda e, ch=ch, sqb=sqb: e.activation(
                sqb[:, ch * 512:ch * 512 + w],
                xcb[:, ch * 512:ch * 512 + w], AF.Square),
                wait=[("ve", vv), ("te", k.v("te"))] if ch == 0 else None,
                drain=(ch == 0))
        vs = k.bar("sc", inc="sc")
        vt = stat_sums(sqb, w, onescb, [("sc", vs)])
        vs = rstd_rows(psD[0:1, 0:w], w, 512, wait=[("te", vt)])
        xcv = xcb[:, 0:NCH * 512].rearrange(
            "p (c t) -> p t c", c=NCH)[:, 0:w, :]
        vv = k.rec("ve", lambda e: e.reduce_max(
            bca[:, 512:512 + w], xcv, AX.X, apply_absolute_value=True),
            inc="ve", drain=True)
        vv = cross_max(bca[:, 512:], w, rows[0:1, 1024:1024 + w],
                       [("ve", vv)])
        k.rec("ve", lambda e: e.tensor_tensor(
            rows[0:1, 1024:1024 + w], rows[0:1, 1024:1024 + w],
            rows[0:1, 512:512 + w], ALU.mult), wait=[("sc", vs)],
            drain=True)
        k.rec("ve", lambda e: e.tensor_single_scalar(
            rows[0:1, 1024:1024 + w], rows[0:1, 1024:1024 + w], 1e-5,
            ALU.max), drain=True)
        vv = k.rec("ve", lambda e: e.tensor_scalar_mul(
            rows[0:1, 2048:2048 + w], rows[0:1, 1024:1024 + w],
            1.0 / 127.0), inc="ve", drain=True)
        k.rec("sc", lambda e: e.activation(
            rows[0:1, 1024:1024 + w], rows[0:1, 1024:1024 + w], AF.Ln),
            wait=[("ve", vv)], drain=True)
        vs = k.rec("sc", lambda e: e.activation(
            rows[0:1, 1024:1024 + w], rows[0:1, 1024:1024 + w], AF.Exp,
            scale=-1.0, bias=LN127), inc="sc", drain=True)
        vv = k.rec("ve", lambda e: e.tensor_tensor(
            rows[0:1, 1536:1536 + w], rows[0:1, 512:512 + w],
            rows[0:1, 1024:1024 + w], ALU.mult), wait=[("sc", vs)],
            inc="ve", drain=True)
        vv = bcast(rows[0:1, 1536:1536 + w], w, [("ve", vv)])
        vv = bcast2(rows[0:1, 2048:2048 + w], w, [("ve", vv)])
        vlast = None
        for ch in range(NCH):
            vlast = round_chunk(
                rdst(ch), xcb[:, ch * 512:ch * 512 + w], w, 0, 512,
                [("ve", vv)] if ch == 0 else None, True, True)
        return vlast

    def conv_group(cg, banks, vts, isfirst, w, extra_wait):
        scr = lambda cl, a, b: ar2[:, cl * 512 + a: cl * 512 + b]
        vs1 = None
        for cl in range(4):
            c = cg * 4 + cl
            hb = banks[cl]
            cw2 = cw[:, c * 3 + 2:c * 3 + 3]
            cbc = cb[:, c:c + 1]
            wt = [("te", vts[cl])]
            if cl == 0:
                wt += extra_wait
            if isfirst:
                vs1 = k.rec("sc", lambda e, cl=cl, hb=hb, cw2=cw2, cbc=cbc:
                            e.activation(scr(cl, 0, 511), hb[:, 1:512],
                                         AF.Identity, bias=cbc, scale=cw2),
                            wait=wt, inc="sc", drain=(cl == 0))
            else:
                vs1 = k.rec("sc", lambda e, cl=cl, hb=hb, cw2=cw2, cbc=cbc:
                            e.activation(scr(cl, 0, 512), hb[:, 0:512],
                                         AF.Identity, bias=cbc, scale=cw2),
                            wait=wt, inc="sc", drain=(cl == 0))
        for cl in range(4):
            c = cg * 4 + cl
            hb = banks[cl]
            cw1 = cw[:, c * 3 + 1:c * 3 + 2]
            he1 = hedge[:, c * 2 + 1:c * 2 + 2]
            wt = [("sc", vs1)] if cl == 0 else None
            if isfirst:
                k.rec("ve", lambda e, cl=cl, hb=hb, cw1=cw1:
                      e.scalar_tensor_tensor(
                    scr(cl, 0, 511), hb[:, 0:511], cw1, scr(cl, 0, 511),
                    ALU.mult, ALU.add), wait=wt, drain=(cl == 0))
            else:
                k.rec("ve", lambda e, cl=cl, hb=hb, cw1=cw1:
                      e.scalar_tensor_tensor(
                    scr(cl, 1, 512), hb[:, 0:511], cw1, scr(cl, 1, 512),
                    ALU.mult, ALU.add), wait=wt, drain=(cl == 0))
                k.rec("ve", lambda e, cl=cl, cw1=cw1, he1=he1:
                      e.scalar_tensor_tensor(
                    scr(cl, 0, 1), he1, cw1, scr(cl, 0, 1),
                    ALU.mult, ALU.add))
        for cl in range(4):
            c = cg * 4 + cl
            hb = banks[cl]
            cw0 = cw[:, c * 3 + 0:c * 3 + 1]
            he0 = hedge[:, c * 2:c * 2 + 1]
            he1 = hedge[:, c * 2 + 1:c * 2 + 2]
            gv = lambda a, b, c=c: g[:, c * 512 + a:c * 512 + b]
            if isfirst:
                k.rec("ve", lambda e, cl=cl, hb=hb, cw0=cw0, gv=gv:
                      e.scalar_tensor_tensor(
                    gv(1, 511), hb[:, 0:510], cw0, scr(cl, 1, 511),
                    ALU.mult, ALU.add), drain=(cl == 0))
                k.rec("ve", lambda e, cl=cl, gv=gv: e.tensor_copy(
                    gv(0, 1), scr(cl, 0, 1)))
            else:
                k.rec("ve", lambda e, cl=cl, hb=hb, cw0=cw0, gv=gv:
                      e.scalar_tensor_tensor(
                    gv(2, 512), hb[:, 0:510], cw0, scr(cl, 2, 512),
                    ALU.mult, ALU.add), drain=(cl == 0))
                k.rec("ve", lambda e, cl=cl, cw0=cw0, he0=he0, gv=gv:
                      e.scalar_tensor_tensor(
                    gv(0, 1), he0, cw0, scr(cl, 0, 1), ALU.mult, ALU.add))
                k.rec("ve", lambda e, cl=cl, cw0=cw0, he1=he1, gv=gv:
                      e.scalar_tensor_tensor(
                    gv(1, 2), he1, cw0, scr(cl, 1, 2), ALU.mult, ALU.add))
        vcar = None
        for cl in range(4):
            c = cg * 4 + cl
            hb = banks[cl]
            vcar = k.rec("ve", lambda e, c=c, hb=hb: e.tensor_copy(
                hedge[:, c * 2:c * 2 + 2], hb[:, 510:512]), inc="ve",
                drain=(cl == 0))
        vsil = None
        ncol = 511 if isfirst else 512
        for cl in range(4):
            c = cg * 4 + cl
            vsil = k.rec("sc", lambda e, c=c, ncol=ncol: e.activation(
                g[:, c * 512:c * 512 + ncol], g[:, c * 512:c * 512 + ncol],
                AF.Silu), wait=[("ve", vcar)] if cl == 0 else None,
                inc="sc", drain=(cl == 0))
        return vcar, vsil

    def round_hidden_grouped(w, v_rows, rh_gd=0):
        vR = None
        vpool = None
        vR_list = []
        for gi in range(8):
            h0 = gi * 4
            for cl in range(4):
                hc = h0 + cl
                vpool = k.rec("gp", lambda e, hc=hc, w=w: e.tensor_tensor(
                    g[:, hc * 512:hc * 512 + w],
                    g[:, hc * 512:hc * 512 + w], hbc[:, 0:w], ALU.mult),
                    wait=([("ve", v_rows), ("sc", k.v("sc"))]
                          if (gi == 0 and cl == 0) else None),
                    inc="gp", drain=(cl == 0))
            for cl in range(4):
                hc = h0 + cl
                wtr = None
                if cl == 0:
                    wtr = [("gp", vpool)]
                    if gi == 0 and rh_gd:
                        wtr.append(("gd", rh_gd))
                vR = k.rec("ve", lambda e, hc=hc, w=w: e.tensor_scalar(
                    Rh[:, hc * 512:hc * 512 + w],
                    g[:, hc * 512:hc * 512 + w], MAGIC, -MAGIC, ALU.add,
                    ALU.add), wait=wtr, inc="ve", drain=(cl == 0))
            vR_list.append(vR)
        return vR_list, vpool

    def conv_chunk(c, hb, v_te, isfirst, w):
        gv = lambda a, b: g[:, c * 512 + a:c * 512 + b]
        cw2 = cw[:, c * 3 + 2:c * 3 + 3]
        cw1 = cw[:, c * 3 + 1:c * 3 + 2]
        cw0 = cw[:, c * 3 + 0:c * 3 + 1]
        cbc = cb[:, c:c + 1]
        he0 = hedge[:, c * 2:c * 2 + 1]
        he1 = hedge[:, c * 2 + 1:c * 2 + 2]
        scr = bca[:, 1024:1536]
        wt = [("te", v_te)]
        if isfirst:
            k.rec("ve", lambda e: e.tensor_scalar(
                scr[:, 0:511], hb[:, 1:512], cw2, cbc, ALU.mult, ALU.add),
                wait=wt, drain=True)
            k.rec("ve", lambda e: e.scalar_tensor_tensor(
                scr[:, 0:511], hb[:, 0:511], cw1, scr[:, 0:511], ALU.mult,
                ALU.add), drain=True)
            k.rec("ve", lambda e: e.scalar_tensor_tensor(
                gv(1, 511), hb[:, 0:510], cw0, scr[:, 1:511], ALU.mult,
                ALU.add), drain=True)
            k.rec("ve", lambda e: e.tensor_copy(gv(0, 1), scr[:, 0:1]),
                  drain=True)
        else:
            k.rec("ve", lambda e: e.tensor_scalar(
                scr[:, 0:512], hb[:, 0:512], cw2, cbc, ALU.mult, ALU.add),
                wait=wt, drain=True)
            k.rec("ve", lambda e: e.scalar_tensor_tensor(
                scr[:, 1:512], hb[:, 0:511], cw1, scr[:, 1:512], ALU.mult,
                ALU.add), drain=True)
            k.rec("ve", lambda e: e.scalar_tensor_tensor(
                scr[:, 0:1], he1, cw1, scr[:, 0:1], ALU.mult, ALU.add),
                drain=True)
            k.rec("ve", lambda e: e.scalar_tensor_tensor(
                gv(2, 512), hb[:, 0:510], cw0, scr[:, 2:512], ALU.mult,
                ALU.add), drain=True)
            k.rec("ve", lambda e: e.scalar_tensor_tensor(
                gv(0, 1), he0, cw0, scr[:, 0:1], ALU.mult, ALU.add),
                drain=True)
            k.rec("ve", lambda e: e.scalar_tensor_tensor(
                gv(1, 2), he1, cw0, scr[:, 1:2], ALU.mult, ALU.add),
                drain=True)
        k.rec("ve", lambda e: e.tensor_copy(he0, hb[:, 510:511]), drain=True)
        vv = k.rec("ve", lambda e: e.tensor_copy(he1, hb[:, 511:512]),
                   inc="ve", drain=True)
        k.rec("sc", lambda e, w=w: e.activation(gv(0, w), gv(0, w),
                                                AF.Silu),
              wait=[("ve", vv)], drain=True)

    def tail_col():
        he0v = hedge[:].rearrange("p (c e) -> p c e", e=2)[:, :, 0:1]
        he1v = hedge[:].rearrange("p (c e) -> p c e", e=2)[:, :, 1:2]
        cw0v = cw[:].rearrange("p (c e) -> p c e", e=3)[:, :, 0:1]
        cw1v = cw[:].rearrange("p (c e) -> p c e", e=3)[:, :, 1:2]
        cbv = cb[:].rearrange("p (c e) -> p c e", e=1)
        gtv = g[:].rearrange("p (c t) -> p c t", t=512)[:, :, 0:1]
        scr = bca[:, 1504:1536].rearrange("p (c e) -> p c e", e=1)
        k.rec("ve", lambda e: e.tensor_tensor(scr, he0v, cw0v, ALU.mult),
              drain=True)
        k.rec("ve", lambda e: e.tensor_tensor(gtv, he1v, cw1v, ALU.mult),
              drain=True)
        k.rec("ve", lambda e: e.tensor_tensor(gtv, gtv, scr, ALU.add),
              drain=True)
        vv = k.rec("ve", lambda e: e.tensor_tensor(gtv, gtv, cbv, ALU.add),
                   inc="ve", drain=True)
        k.rec("sc", lambda e: e.activation(
            g[:].rearrange("p (c t) -> p c t", t=512)[:, :, 0:1],
            g[:].rearrange("p (c t) -> p c t", t=512)[:, :, 0:1], AF.Silu),
            wait=[("ve", vv)], drain=True)

    def hidden_rows(w):
        vs = k.bar("sc", inc="sc")
        gvv = g[:].rearrange("p (c t) -> p t c", c=HCH)[:, 0:w, :]
        vv = k.rec("ve", lambda e: e.reduce_max(
            bca[:, 512:512 + w], gvv, AX.X, apply_absolute_value=True),
            wait=[("sc", vs)], inc="ve", drain=True)
        vv = cross_max(bca[:, 512:], w, rows[0:1, 1024:1024 + w],
                       [("ve", vv)])
        k.rec("ve", lambda e: e.tensor_single_scalar(
            rows[0:1, 1024:1024 + w], rows[0:1, 1024:1024 + w], 1e-5,
            ALU.max), wait=[("ve", vv)], drain=True)
        vv = k.rec("ve", lambda e: e.tensor_scalar_mul(
            rows[0:1, 2560:2560 + w], rows[0:1, 1024:1024 + w],
            0.5 / 127.0), inc="ve", drain=True)
        k.rec("sc", lambda e: e.activation(
            rows[0:1, 1024:1024 + w], rows[0:1, 1024:1024 + w], AF.Ln),
            wait=[("ve", vv)], drain=True)
        vs = k.rec("sc", lambda e: e.activation(
            rows[0:1, 1024:1024 + w], rows[0:1, 1024:1024 + w], AF.Exp,
            scale=-1.0, bias=LN127), inc="sc", drain=True)
        return bcast_to(hbc, rows[0:1, 1024:1024 + w], w,
                        [("sc", vs), ("ve", k.v("ve"))])

    def ffn(x_src, x1_dst, do_ln2, rq_src, wait0):
        wins = []
        for sqi in range(NB):
            wins.append((sqi, 0, 0, 511))
            for kk in range(1, 4):
                wins.append((sqi, 1, kk * 512, 512))
            wins.append((sqi, 2, None, 1))
        NW = len(wins)
        wb_free = {id(wba): 0, id(wbb): 0}
        pb_free = {}
        HB = [psA, psB, psE, psF, psG]
        v_xfree = {}
        v_xload = {}
        v_rpq = {}
        st8 = {"x1gd": 0,
               "y2gd": 0,
               "w1te": 0}

        def xload(wi):
            sqi, kind, t0, w = wins[wi]
            xcur = xa if wi % 2 == 0 else xb
            wt = list(wait0) if wi <= 1 else [("ve", v_xfree[wi - 2])]
            vx = k.dma("sy", xcur[:].rearrange("p (c t) -> p c t", c=NCH),
                       _drv(x_src)[:, :, sqi * S + t0:sqi * S + t0 + TT],
                       wait=wt)
            v_xload[wi] = vx
            k.rec("ve", lambda e, xcur=xcur, wi=wi: e.tensor_copy(
                xedge[:, (wi % 4) * 8:(wi % 4) * 8 + 8].rearrange(
                    "p (c t) -> p c t", t=1),
                xcur[:].rearrange("p (c t) -> p c t", c=NCH)
                [:, :, 511:512]), wait=[("ld", vx)], inc="ve", drain=True)

        def prep_rp(wi, gp_wait):
            sqi, kind, t0, w = wins[wi]
            xcur = xa if wi % 2 == 0 else xb
            if rq_src is None:
                v = ln_quant(xcur, TT,
                             lambda ch: rp[:, ch * 512:(ch + 1) * 512],
                             [("ld", v_xload[wi]), ("te", st8["w1te"])],
                             xc_wait=[("gp", gp_wait)], xcb=g)
                v_rpq[wi] = ("ve", v)
            else:
                v_rpq[wi] = ("ld", k.dma(
                    "sy", rp[:].rearrange("p (c t) -> p c t", c=NCH),
                    _drv(rq_src)[:, :, sqi * S + t0:sqi * S + t0 + TT],
                    wait=[("te", st8["w1te"]), ("ve", k.v("ve")),
                          ("gd", k.v("gd"))]))
            pb_free[id(psF)] = {"ve": k.v("ve"), "sc": k.v("sc")}
            pb_free[id(psG)] = {"ve": k.v("ve"), "sc": k.v("sc")}

        xload(0)
        if NW > 1 and wins[1][1] != 2:
            xload(1)
        prep_rp(0, 0)
        for wi, (sqi, kind, t0, w) in enumerate(wins):
            base = sqi * S
            xcur = xa if (wi % 2 == 0) else xb
            gw0 = (t0 if kind == 0 else t0 - 1) if kind != 2 else S - 1
            if kind != 2:
                for cg in range(8):
                    wb = wba if cg % 2 == 0 else wbb
                    vw = k.dma("sy", wb[:].rearrange("p (c t) -> p c t",
                                                     c=NCH),
                               _drv(wq1s)[:, :, cg * 512:(cg + 1) * 512],
                               wait=[("te", wb_free[id(wb)])])
                    banks = [HB[(cg * 4 + cl) % 5] for cl in range(4)]
                    vts = []
                    for cl in range(4):
                        hb = banks[cl]
                        pf = pb_free.get(id(hb), {"ve": 0, "sc": 0})
                        vt = None
                        for ic in range(NCH):
                            vt = k.rec("te", lambda e, ic=ic, cl=cl, hb=hb,
                                       wb=wb: e.matmul(
                                hb[:, 0:TT],
                                wb[:, ic * 512 + cl * 128:
                                   ic * 512 + cl * 128 + 128],
                                rp[:, ic * 512:(ic + 1) * 512],
                                start=(ic == 0), stop=(ic == NCH - 1)),
                                wait=[("ld", vw), v_rpq[wi],
                                      ("ve", pf["ve"]), ("sc", pf["sc"])]
                                if ic == 0 else None, inc="te")
                        vts.append(vt)
                    xw = [("gd", st8["x1gd"]), ("ve", k.v("ve"))] \
                        if cg == 0 else []
                    conv_group(cg, banks, vts, kind == 0, w, xw)
                    for cl in range(4):
                        pb_free[id(banks[cl])] = {"ve": k.v("ve"),
                                                  "sc": k.v("sc")}
                    wb_free[id(wb)] = k.v("te")
                st8["w1te"] = k.v("te")
            else:
                tail_col()
            vbc = hidden_rows(w)
            pb_free[id(psF)] = {"ve": k.v("ve"), "sc": k.v("sc")}
            pb_free[id(psG)] = {"ve": k.v("ve"), "sc": k.v("sc")}
            vR_g, vpool = round_hidden_grouped(w, vbc, rh_gd=st8["y2gd"])
            vR = vR_g[-1]
            nx = wi + 1

            def res_fc(fc, ob_, vt, vbb_):
                a2 = lambda a, b, fc=fc: ar2[:, fc * 512 + a:fc * 512 + b]
                k.rec("ve", lambda e, ob_=ob_, w=w, a2=a2: e.tensor_tensor(
                    a2(0, w), ob_[:, 0:w], hbc[:, 512:512 + w], ALU.mult),
                    wait=[("te", vt), ("ve", vbb_), ("gd", st8["x1gd"])],
                    drain=True)
                ecol = (wi - 1) % 4 * 8 + fc
                if kind == 0:
                    vv = k.rec("ve", lambda e, fc=fc, w=w, xcur=xcur, a2=a2:
                               e.tensor_tensor(
                        a2(0, w), a2(0, w),
                        xcur[:, fc * 512:fc * 512 + w], ALU.add),
                        inc="ve", drain=True)
                elif kind == 1:
                    k.rec("ve", lambda e, ecol=ecol, a2=a2:
                          e.tensor_tensor(
                        a2(0, 1), a2(0, 1),
                        xedge[:, ecol:ecol + 1], ALU.add), drain=True)
                    vv = k.rec("ve", lambda e, fc=fc, w=w, xcur=xcur, a2=a2:
                               e.tensor_tensor(
                        a2(1, w), a2(1, w),
                        xcur[:, fc * 512:fc * 512 + w - 1], ALU.add),
                        inc="ve", drain=True)
                else:
                    vv = k.rec("ve", lambda e, ecol=ecol, a2=a2:
                               e.tensor_tensor(
                        a2(0, 1), a2(0, 1),
                        xedge[:, ecol:ecol + 1], ALU.add),
                        inc="ve", drain=True)
                pb_free[id(ob_)] = {"ve": k.v("ve"), "sc": k.v("sc")}
                return vv

            if kind != 2:
                B5 = [psA, psB, psE, psF, psG]
                vt5 = [None] * 5
                for gi in range(8):
                    wb = wba if gi % 2 == 0 else wbb
                    vw = k.dma("sy", wb[:, 0:4096],
                               wq2g[:, gi * 4096:(gi + 1) * 4096],
                               wait=[("te", wb_free[id(wb)])])
                    for fc in range(5):
                        ob_ = B5[fc]
                        pf = pb_free.get(id(ob_), {"ve": 0, "sc": 0})
                        for cl in range(4):
                            wt = []
                            if fc == 0 and cl == 0:
                                wt += [("ld", vw), ("ve", vR_g[gi])]
                            if gi == 0 and cl == 0:
                                wt += [("ve", pf["ve"]), ("sc", pf["sc"])]
                            vt5[fc] = k.rec("te", lambda e, gi=gi, fc=fc,
                                            cl=cl, ob_=ob_, wb=wb, w=w:
                                            e.matmul(
                                ob_[:, 0:w],
                                wb[:, cl * 1024 + fc * 128:
                                   cl * 1024 + fc * 128 + 128],
                                Rh[:, (gi * 4 + cl) * 512:
                                   (gi * 4 + cl) * 512 + w],
                                start=(gi == 0 and cl == 0),
                                stop=(gi == 7 and cl == 3)),
                                wait=wt or None, inc="te")
                    wb_free[id(wb)] = k.v("te")
                vbb_ = bcast_to(hbc[:, 512:1024],
                                rows[0:1, 2560:2560 + w], w, [("ve", vR)])
                for fc in range(5):
                    vv = res_fc(fc, B5[fc], vt5[fc], vbb_)
                if nx < NW and wins[nx][1] != 2:
                    prep_rp(nx, vpool)
                for j, fc in enumerate((5, 6, 7)):
                    wb = wba if j % 2 == 0 else wbb
                    vw = k.dma("sy", wb[:, 0:4096].rearrange(
                        "p (c t) -> p c t", c=HCH),
                        _drv(wq2s)[:, :, fc * 128:(fc + 1) * 128],
                        wait=[("te", wb_free[id(wb)])])
                    ob_ = (psA, psB, psE)[j]
                    pf = pb_free.get(id(ob_), {"ve": 0, "sc": 0})
                    vt = None
                    for hc in range(HCH):
                        vt = k.rec("te", lambda e, hc=hc, ob_=ob_, wb=wb,
                                   w=w: e.matmul(
                            ob_[:, 0:w], wb[:, hc * 128:(hc + 1) * 128],
                            Rh[:, hc * 512:hc * 512 + w],
                            start=(hc == 0), stop=(hc == HCH - 1)),
                            wait=[("ld", vw), ("ve", pf["ve"]),
                                  ("sc", pf["sc"])] if hc == 0 else None,
                            inc="te")
                    wb_free[id(wb)] = k.v("te")
                    vv = res_fc(fc, ob_, vt, vbb_)
            else:
                vbb_ = bcast_to(hbc[:, 512:1024],
                                rows[0:1, 2560:2560 + w], w, [("ve", vR)])
                if nx < NW and wins[nx][1] != 2:
                    prep_rp(nx, vpool)
                for fc in range(NCH):
                    wb = wba if fc % 2 == 0 else wbb
                    vw = k.dma("sy", wb[:, 0:4096].rearrange(
                        "p (c t) -> p c t", c=HCH),
                        _drv(wq2s)[:, :, fc * 128:(fc + 1) * 128],
                        wait=[("te", wb_free[id(wb)])])
                    ob_ = psA if fc % 2 == 0 else psB
                    pf = pb_free.get(id(ob_), {"ve": 0, "sc": 0})
                    vt = None
                    for hc in range(HCH):
                        vt = k.rec("te", lambda e, hc=hc, ob_=ob_, wb=wb,
                                   w=w: e.matmul(
                            ob_[:, 0:w], wb[:, hc * 128:(hc + 1) * 128],
                            Rh[:, hc * 512:hc * 512 + w],
                            start=(hc == 0), stop=(hc == HCH - 1)),
                            wait=[("ld", vw), ("ve", vR), ("ve", pf["ve"]),
                                  ("sc", pf["sc"])]
                            if hc == 0 else None, inc="te")
                    wb_free[id(wb)] = k.v("te")
                    vv = res_fc(fc, ob_, vt, vbb_)
            v_xfree[wi] = vv
            vv = k.bar("ve", inc="ve")
            vst = k.dma("gp",
                        _drv(x1_dst)[:, :, base + gw0:base + gw0 + w],
                        ar2[:].rearrange("p (c t) -> p c t", c=NCH)
                        [:, :, 0:w], wait=[("ve", vv)], slow=(w == 1))
            st8["x1gd"] = vst
            if do_ln2:
                vy = ln2_window(w)
                vst = k.dma("gp",
                            _drv(y2s)[:, :, base + gw0:base + gw0 + w],
                            y2st[:].rearrange("p (c t) -> p c t", c=NCH)
                            [:, :, 0:w], wait=[("ve", vy)], slow=(w == 1))
                st8["y2gd"] = vst
            if wi + 2 < NW and wins[wi + 2][1] != 2:
                xload(wi + 2)
        return k.v("gd")

    def ln2_window(w):
        vt = stat_sums(ar2, w, onesc, [("ve", k.v("ve"))])
        vv = k.rec("ve", lambda e: e.tensor_scalar_mul(
            rows[0:1, 0:w], psD[0:1, 0:w], 1.0 / D), wait=[("te", vt)],
            inc="ve", drain=True)
        vv = bcast(rows[0:1, 0:w], w, [("ve", vv)])
        for ch in range(NCH):
            k.rec("ve", lambda e, ch=ch: e.tensor_tensor(
                g[:, 4096 + ch * 512:4096 + ch * 512 + w],
                ar2[:, ch * 512:ch * 512 + w], bca[:, 0:w], ALU.subtract),
                wait=[("ve", vv)] if ch == 0 else None, drain=True)
        vv = k.bar("ve", inc="ve")
        for ch in range(NCH):
            k.rec("sc", lambda e, ch=ch: e.activation(
                y2sq[:, ch * 512:ch * 512 + w],
                g[:, 4096 + ch * 512:4096 + ch * 512 + w],
                AF.Square), wait=[("ve", vv)] if ch == 0 else None,
                drain=(ch == 0))
        vs = k.bar("sc", inc="sc")
        vt = stat_sums(y2sq, w, onescb, [("sc", vs)])
        vs = rstd_rows(psD[0:1, 0:w], w, 512, wait=[("te", vt)])
        vv = bcast(rows[0:1, 512:512 + w], w, [("sc", vs)])
        vlast = None
        for ch in range(NCH):
            vlast = k.rec("ve", lambda e, ch=ch: e.tensor_tensor(
                y2st[:, ch * 512:ch * 512 + w],
                g[:, 4096 + ch * 512:4096 + ch * 512 + w],
                bca[:, 0:w], ALU.mult),
                wait=[("ve", vv)] if ch == 0 else None, inc="ve",
                drain=True)
        return vlast

    def table_gen(h, v_iot):
        thc = th2pi[:, h:h + 1]
        k.rec("ve", lambda e: e.tensor_scalar(
            tgf[:], tgr[:], thc, None, ALU.mult),
            wait=[("sc", k.v("sc")), ("te", k.v("te")), ("ld", v_iot)],
            drain=True)
        k.rec("ve", lambda e: e.tensor_scalar(
            tgf2[:], tgf[:], MAGIC, -MAGIC, ALU.add, ALU.add), drain=True)
        vv = k.rec("ve", lambda e: e.tensor_tensor(
            tgf[:], tgf[:], tgf2[:], ALU.subtract), inc="ve", drain=True)
        k.rec("sc", lambda e: e.activation(
            sinf[:], tgf[:], AF.Sin, scale=PI2), wait=[("ve", vv)],
            drain=True)
        k.rec("ve", lambda e: e.tensor_single_scalar(
            tgf2[:], tgf[:], 0.25, ALU.is_ge), drain=True)
        vv = k.rec("ve", lambda e: e.scalar_tensor_tensor(
            tgf2[:], tgf2[:], -1.0, tgf[:], ALU.mult, ALU.add), inc="ve",
            drain=True)
        k.rec("sc", lambda e: e.activation(
            cosf[:], tgf2[:], AF.Sin, scale=PI2, bias=PIH),
            wait=[("ve", vv)], drain=True)
        return k.rec("ve", lambda e, h=h: e.tensor_scalar(
            tribh[:], trib[:], esm1c[:, h:h + 1], None, ALU.mult),
            wait=[("ve", v_const)], inc="ve", drain=True)

    def attention():
        v_y2ld = k.dma("sy", y2[:].rearrange("p (c t) -> p c t", c=NCH),
                       _drv(y2s), wait=[("gd", k.v("gd"))])
        v_iot = k.dma("sy", tgr[:], iot,
                      wait=[("ve", k.v("ve")), ("sc", k.v("sc")),
                            ("te", k.v("te"))])
        v_wqh_free = 0
        st = {"uh": k.v("ve"), "sq_sc": k.v("sc")}
        for h in range(NH):
            vwq = None
            for pi, po in enumerate((0, D, 2 * D, 3 * D)):
                vwq = k.dma("gp",
                            wqh[:].rearrange("p (a c t) -> p a c t", a=4,
                                             c=NCH)[:, pi],
                            _drv(qwT)[:, :, po + h * 128:po + h * 128 + 128],
                            wait=[("te", v_wqh_free)] if pi == 0 else None)
            vmask = table_gen(h, v_iot)
            if vmask > st["uh"]:
                st["uh"] = vmask
            for b in range(NB):
                v_wqh_free = attn_bh(h, b, vwq, v_y2ld, vmask, st)
        if AMODE in ("proj", "state", "nostats"):
            return k.v("sc")
        vs = k.rec("sc", lambda e: e.activation(
            rsqtm[:], psD[:, 0:32], AF.Ln, bias=1e-6, scale=1.0 / D),
            wait=[("te", k.v("te"))], inc="sc", drain=True)
        return k.rec("sc", lambda e: e.activation(
            rsqtm[:], rsqtm[:], AF.Exp, scale=-0.5), inc="sc", drain=True)

    def attn_bh(h, b, vwq, v_y2ld, vmask, st):
        v0 = [("gd", vwq), ("ld", v_y2ld), ("ve", st["uh"]),
              ("sc", st["sq_sc"])]
        for pi, name in enumerate("qkvu"):
            for tb in range(4):
                pb = psA if tb % 2 == 0 else psB
                vt = None
                for ic in range(NCH):
                    wt = None
                    if ic == 0:
                        wt = v0 + [("sc", k.v("sc"))]
                        v0 = []
                    vt = k.rec("te", lambda e, pi=pi, tb=tb, ic=ic, pb=pb:
                               e.matmul(
                        pb[:, 0:512],
                        wqh[:].rearrange("p (a c t) -> p a c t", a=4,
                                         c=NCH)[:, pi, ic],
                        y2[:, ic * T + b * S + tb * 512:
                           ic * T + b * S + (tb + 1) * 512],
                        start=(ic == 0), stop=(ic == NCH - 1)),
                        wait=wt, inc="te")
                bcol = qb[:, pi * 8 + h:pi * 8 + h + 1]
                if name in "qk":
                    dst = silq if name == "q" else silk
                    k.rec("sc", lambda e, dst=dst, tb=tb, pb=pb, bcol=bcol:
                          e.activation(dst[:, tb * 512:(tb + 1) * 512],
                                       pb[:, 0:512], AF.Silu, bias=bcol),
                          wait=[("te", vt)], inc="sc", drain=True)
                elif name == "u":
                    k.rec("sc", lambda e, tb=tb, pb=pb, bcol=bcol:
                          e.activation(uh[:, tb * 512:(tb + 1) * 512],
                                       pb[:, 0:512], AF.Identity,
                                       bias=bcol),
                          wait=[("te", vt)], inc="sc", drain=True)
                else:
                    vs = k.rec("sc", lambda e, tb=tb, pb=pb, bcol=bcol:
                               e.activation(vfm[:, 0:512], pb[:, 0:512],
                                            AF.Identity, bias=bcol),
                               wait=[("te", vt)], inc="sc", drain=True)
                    vt2 = None
                    for jj in range(4):
                        vt2 = k.rec("te", lambda e, jj=jj: e.transpose(
                            psT[:, jj * 128:(jj + 1) * 128],
                            vfm[:, jj * 128:(jj + 1) * 128], identb[:]),
                            wait=[("sc", vs), ("ve", k.v("ve"))]
                            if jj == 0 else None, inc="te")
                    vv = k.rec("ve", lambda e, tb=tb: e.tensor_copy(
                        vtm[:, tb * 512:(tb + 1) * 512], psT[:, 0:512]),
                        wait=[("te", vt2)], inc="ve", drain=True)
        v_projte = k.v("te")
        vs_pr = k.bar("sc", inc="sc")
        for dst, src, tb_ in ((qcb, silq, cosf), (qsb, silq, sinf),
                              (kcb, silk, cosf), (ksb, silk, sinf)):
            k.rec("ve", lambda e, d=dst, s=src, t=tb_: e.tensor_tensor(
                d[:], s[:], t[:], ALU.mult), wait=[("sc", vs_pr)],
                drain=True)
        vv_lrpe = k.bar("ve", inc="ve")
        vv_kt = None
        for half, (srcb, dstb) in enumerate(((kcb, kcT), (ksb, ksT))):
            for grp in range(4):
                vt = None
                for jj in range(4):
                    wt_t = [("ve", k.v("ve"))] if jj == 0 else None
                    vt = k.rec("te", lambda e, srcb=srcb, grp=grp, jj=jj:
                               e.transpose(
                        psT[:, jj * 128:(jj + 1) * 128],
                        srcb[:, grp * 512 + jj * 128:
                             grp * 512 + (jj + 1) * 128], identb[:]),
                        wait=wt_t, inc="te")
                vv_kt = k.rec("ve", lambda e, dstb=dstb, grp=grp:
                              e.tensor_copy(
                    dstb[:, grp * 512:(grp + 1) * 512], psT[:, 0:512]),
                    wait=[("te", vt)], inc="ve", drain=True)
        if AMODE == "proj":
            st["uh"] = k.v("ve")
            st["sq_sc"] = k.v("sc")
            return v_projte
        UB = [psF, psG, psA, psB]
        v_pfx = []
        vtU = []
        vv_a = None

        def bsb_ap(c):
            return bsbA[:, (c - 1) * 256:c * 256] if c <= 6 else bsbB[:]

        def pfx_step(c):
            nonlocal vv_a
            if c == 0:
                k.rec("ve", lambda e: e.tensor_copy(bsbf[:],
                                                    UB[0][:, 0:256]),
                      wait=[("te", vtU[0])], drain=True)
            else:
                k.rec("ve", lambda e, ub=UB[c % 4]: e.tensor_tensor(
                    bsbf[:], bsbf[:], ub[:, 0:256], ALU.add),
                    wait=[("te", vtU[c])], drain=True)
            if c <= 6:
                vv = k.rec("ve", lambda e, dst=bsb_ap(c + 1), h=h:
                           e.tensor_scalar(dst, bsbf[:], esm1c[:, h:h + 1],
                                           None, ALU.mult), inc="ve",
                           drain=True)
                v_pfx.append(vv)
            if c == 7:
                vv_a = k.rec("ve", lambda e: e.tensor_copy(asb[:],
                                                           bsbf[:]),
                             inc="ve", drain=True)

        for c in range(8):
            ub = UB[c % 4]
            wt = None
            if c == 0:
                wt = [("ve", vv_kt)]
            elif c in (2, 3):
                wt = [("sc", vs_pr)]
            elif c >= 4:
                wt = [("ve", v_pfx[c - 4])]
            vt = None
            for half, kT in enumerate((kcT, ksT)):
                for jb in range(2):
                    jc = c * 2 + jb
                    vt = k.rec("te", lambda e, ub=ub, kT=kT,
                               off=half * 128, jc=jc, s0=(jb == 0),
                               s1=(jb == 1): e.matmul(
                        ub[:, off:off + 128],
                        kT[:, jc * 128:(jc + 1) * 128],
                        vtm[:, jc * 128:(jc + 1) * 128],
                        start=s0, stop=s1),
                        wait=wt if (half == 0 and jb == 0) else None,
                        inc="te")
            vtU.append(vt)
            if c >= 2:
                pfx_step(c - 2)
        pfx_step(6)
        pfx_step(7)
        if AMODE == "state":
            st["uh"] = k.v("ve")
            st["sq_sc"] = k.v("sc")
            return v_projte
        vintra = {}
        vmaskv = {}
        vAVt = {}
        vsq = {}
        vstat = {}
        vout = {}
        OU = [psA, psB]
        IC = [psE, psC]

        def pre(c):
            ou = OU[c % 2]
            ic_ = IC[c % 2]
            if c <= 1:
                wt = [("ve", vv_a)]
            else:
                wt = [("ve", vout[c - 2]), ("sc", vsq[c - 2])]
            k.rec("te", lambda e, c=c, ou=ou: e.matmul(
                ou[:, 0:256], asb[:, 0:128], qcb[:, c * 256:(c + 1) * 256],
                start=True, stop=False), wait=wt, inc="te")
            k.rec("te", lambda e, c=c, ou=ou: e.matmul(
                ou[:, 0:256], asb[:, 128:256],
                qsb[:, c * 256:(c + 1) * 256], start=False, stop=False),
                inc="te")
            if c > 0:
                bs = bsb_ap(c)
                k.rec("te", lambda e, c=c, ou=ou, bs=bs: e.matmul(
                    ou[:, 0:256], bs[:, 0:128],
                    qcb[:, c * 256:(c + 1) * 256], start=False,
                    stop=False), wait=[("ve", v_pfx[c - 1])], inc="te")
                k.rec("te", lambda e, c=c, ou=ou, bs=bs: e.matmul(
                    ou[:, 0:256], bs[:, 128:256],
                    qsb[:, c * 256:(c + 1) * 256], start=False,
                    stop=False), inc="te")
            wti = [("ve", vmaskv[c - 2])] if c >= 2 else None
            k.rec("te", lambda e, c=c, ic_=ic_: e.matmul(
                ic_[:, 0:256], kcb[:, c * 256:c * 256 + 128],
                qcb[:, c * 256:(c + 1) * 256], start=True, stop=False),
                wait=wti, inc="te")
            k.rec("te", lambda e, c=c, ic_=ic_: e.matmul(
                ic_[:, 0:256], ksb[:, c * 256:c * 256 + 128],
                qsb[:, c * 256:(c + 1) * 256], start=False, stop=True),
                inc="te")
            k.rec("te", lambda e, c=c, ic_=ic_: e.matmul(
                ic_[:, 256:384], kcb[:, c * 256 + 128:c * 256 + 256],
                qcb[:, c * 256 + 128:(c + 1) * 256], start=True,
                stop=False), inc="te")
            vintra[c] = k.rec("te", lambda e, c=c, ic_=ic_: e.matmul(
                ic_[:, 256:384], ksb[:, c * 256 + 128:c * 256 + 256],
                qsb[:, c * 256 + 128:(c + 1) * 256], start=False,
                stop=True), inc="te")

        def maskev(c):
            ic_ = IC[c % 2]
            fp = fsb[:, (c % 2) * 384:(c % 2) * 384 + 384]
            wt = [("te", vintra[c])]
            if c >= 2:
                wt.append(("te", vAVt[c - 2]))
            k.rec("ve", lambda e, ic_=ic_, fp=fp: e.tensor_tensor(
                fp[:, 0:128], ic_[:, 0:128], tribh[:], ALU.mult),
                wait=wt, drain=True)
            k.rec("ve", lambda e, ic_=ic_, fp=fp, h=h: e.tensor_scalar(
                fp[:, 128:256], ic_[:, 128:256], esm1c[:, h:h + 1], None,
                ALU.mult), drain=True)
            vmaskv[c] = k.rec("ve", lambda e, ic_=ic_, fp=fp:
                              e.tensor_tensor(
                fp[:, 256:384], ic_[:, 256:384], tribh[:], ALU.mult),
                inc="ve", drain=True)

        def av(c):
            ou = OU[c % 2]
            fp = fsb[:, (c % 2) * 384:(c % 2) * 384 + 384]
            k.rec("te", lambda e, c=c, ou=ou, fp=fp: e.matmul(
                ou[:, 128:256], vtm[:, c * 256 + 128:c * 256 + 256],
                fp[:, 256:384], start=False, stop=False),
                wait=[("ve", vmaskv[c])], inc="te")
            vAVt[c] = k.rec("te", lambda e, c=c, ou=ou, fp=fp: e.matmul(
                ou[:, 0:256], vtm[:, c * 256:c * 256 + 128], fp[:, 0:256],
                start=False, stop=True), inc="te")

        def outev(c):
            ou = OU[c % 2]
            pc0 = h * T + b * S + c * 256
            if AMODE == "nostats":
                vsq[c] = k.v("sc")
                vout[c] = k.rec("ve", lambda e, c=c, ou=ou, pc0=pc0:
                                e.tensor_tensor(
                    pat[:, pc0:pc0 + 256], ou[:, 0:256],
                    uh[:, c * 256:(c + 1) * 256], ALU.mult),
                    wait=[("te", vAVt[c])], inc="ve", drain=True)
                return
            wt = [("te", vAVt[c])]
            if c >= 1:
                wt.append(("te", vstat[c - 1]))
            vsq[c] = k.rec("sc", lambda e, c=c, ou=ou: e.activation(
                sqat[:], ou[:, 0:256], AF.Square), wait=wt, inc="sc",
                drain=True)
            vout[c] = k.rec("ve", lambda e, c=c, ou=ou, pc0=pc0:
                            e.tensor_tensor(
                pat[:, pc0:pc0 + 256], ou[:, 0:256],
                uh[:, c * 256:(c + 1) * 256], ALU.mult),
                wait=[("sc", vsq[c])], inc="ve", drain=True)

        def stat(c):
            vt = None
            for jb in range(2):
                col = b * 16 + c * 2 + jb
                vt = k.rec("te", lambda e, c=c, jb=jb, col=col, h=h, b=b:
                           e.matmul(
                    psD[:, col:col + 1],
                    sqat[:, jb * 128:(jb + 1) * 128],
                    onescb[:, 0:1],
                    start=(h == 0 and b == 0 and c == 0 and jb == 0),
                    stop=(h == NH - 1 and b == NB - 1 and c == 7
                          and jb == 1)),
                    wait=[("sc", vsq[c])] if jb == 0 else None, inc="te")
            vstat[c] = vt

        pre(0)
        maskev(0)
        pre(1)
        maskev(1)
        for c in range(8):
            av(c)
            if c >= 1 and AMODE != "nostats":
                stat(c - 1)
            outev(c)
            if c + 2 < 8:
                pre(c + 2)
                maskev(c + 2)
        if AMODE != "nostats":
            stat(7)
        st["uh"] = vout[7]
        st["sq_sc"] = vsq[7]
        return v_projte

    def attn_out(v_rsq):
        k.dma("gp", owb[:].rearrange("p (c t) -> p c t", c=NCH),
              _drv(owT), wait=[("te", k.v("te")), ("ve", k.v("ve")),
                               ("sc", k.v("sc"))])
        v_st = 0
        v_xfree = ("sc", k.v("sc"))
        pb4 = {id(psA): 0, id(psB): 0}
        for tl in range(8):
            b, tb = tl // 4, tl % 4
            base = b * S + tb * 512
            vt2 = None
            for jj in range(4):
                col = b * 16 + tb * 4 + jj
                vt2 = k.rec("te", lambda e, col=col, jj=jj: e.transpose(
                    psG[0:1, jj * 128:jj * 128 + 128],
                    rsqtm[:, col:col + 1], identf[:]),
                    wait=[("sc", v_rsq)] if (tl == 0 and jj == 0) else None,
                    inc="te")
            vv = k.rec("ve", lambda e: e.tensor_copy(
                rows[0:1, 0:512], psG[0:1, 0:512]), wait=[("te", vt2)],
                inc="ve", drain=True)
            vv = bcast(rows[0:1, 0:512], 512, [("ve", vv)])
            for hc in range(NCH):
                k.rec("ve", lambda e, hc=hc, base=base: e.tensor_tensor(
                    zsb[:, hc * 512:(hc + 1) * 512],
                    pat[:, hc * T + base:hc * T + base + 512],
                    bca[:, 0:512], ALU.mult),
                    wait=[("ve", vv)] if hc == 0 else None, drain=True)
            vv = k.bar("ve", inc="ve")
            vx = k.dma("sy", x4a[:].rearrange("p (c t) -> p c t", c=NCH),
                       _drv(xs1)[:, :, base:base + 512], wait=[v_xfree])
            for fc in range(NCH):
                ob_ = psA if fc % 2 == 0 else psB
                vt = None
                for ic in range(NCH):
                    wt4 = None
                    if ic == 0:
                        wt4 = [("sc", pb4[id(ob_)])]
                        if fc == 0:
                            wt4 += [("ve", vv), ("gd", k.v("gd"))]
                    vt = k.rec("te", lambda e, fc=fc, ic=ic, ob_=ob_:
                               e.matmul(
                        ob_[:, 0:512],
                        owb[:, ic * 1024 + fc * 128:ic * 1024 + fc * 128
                            + 128],
                        zsb[:, ic * 512:(ic + 1) * 512],
                        start=(ic == 0), stop=(ic == NCH - 1)),
                        wait=wt4, inc="te")
                vs = k.rec("sc", lambda e, fc=fc, ob_=ob_: e.activation(
                    ar2[:, fc * 512:(fc + 1) * 512], ob_[:, 0:512],
                    AF.Identity, bias=ob[:, fc:fc + 1]),
                    wait=[("te", vt)], inc="sc", drain=True)
                pb4[id(ob_)] = vs
                vv2 = k.rec("ve", lambda e, fc=fc: e.tensor_tensor(
                    ar2[:, fc * 512:(fc + 1) * 512],
                    ar2[:, fc * 512:(fc + 1) * 512],
                    x4a[:, fc * 512:(fc + 1) * 512], ALU.add),
                    wait=[("sc", vs), ("ld", vx)], inc="ve", drain=True)
            v_xfree = ("ve", vv2)
            vv = k.bar("ve", inc="ve")
            vst = k.dma("gp", _drv(xs2)[:, :, base:base + 512],
                        ar2[:].rearrange("p (c t) -> p c t", c=NCH),
                        wait=[("ve", vv)])
            vq = ln_quant(ar2, 512,
                          lambda ch: rp[:, ch * 512:(ch + 1) * 512],
                          [("gd", vst), ("gd", v_st)],
                          xc_wait=[("gd", vst)], sqb=sq4)
            v_st = k.dma("gp", _drv(rq3)[:, :, base:base + 512],
                         rp[:].rearrange("p (c t) -> p c t", c=NCH),
                         wait=[("ve", vq)])
        return v_st

    wt0 = [("gd", v_prep)] if "prep" in phases else []
    last_stage = "none"
    if "ffna" in phases:
        ffn(xT, xs1, "attn" in phases, None, wt0)
        last_stage = "ffna"
    if "attn" in phases:
        v_rsq = attention()
        last_stage = "attn"
    if "attno" in phases:
        attn_out(v_rsq)
        last_stage = "attno"
    if "ffnc" in phases:
        ffn(xs2, y, False, rq3, [("gd", k.v("gd")), ("ve", k.v("ve")),
                                 ("sc", k.v("sc")), ("te", k.v("te"))])
        last_stage = "ffnc"

    if last_stage != "ffnc":
        src = {"ffna": xs1, "attn": xs1, "attno": xs2}.get(last_stage, xT)
        if debug_out is not None:
            src = {"xs1": xs1, "xs2": xs2, "y2s": None, "xT": xT}[debug_out]
        wt = [("gd", k.v("gd")), ("ve", k.v("ve")), ("sc", k.v("sc")),
              ("te", k.v("te")), ("ld", k.v("ld"))]
        for i in range(8):
            vx = k.dma("sy", xa[:].rearrange("p (c t) -> p c t", c=1),
                       _drv(src)[:, i:i + 1, :],
                       wait=wt if i == 0 else [("gd", k.v("gd"))])
            k.dma("gp", _drv(y)[:, i:i + 1, :],
                  xa[:].rearrange("p (c t) -> p c t", c=1),
                  wait=[("ld", vx)])

    fin = [("gd", k.v("gd")), ("ld", k.v("ld")), ("ve", k.v("ve")),
           ("sc", k.v("sc")), ("te", k.v("te"))]
    k.rec("sy", lambda e: e.nop(), wait=fin)

    build._k = k
    with contextlib.ExitStack() as st:
        sems = {}
        for s in ("ld", "gd", "te", "ve", "sc", "gp"):
            sems[s] = st.enter_context(nc.semaphore("sem_" + s))
        blk = st.enter_context(nc.Block())
        blk.sync(lambda e: k.emit(sems, "sy", e))
        blk.tensor(lambda e: k.emit(sems, "te", e))
        blk.vector(lambda e: k.emit(sems, "ve", e))
        blk.scalar(lambda e: k.emit(sems, "sc", e))
        blk.gpsimd(lambda e: k.emit(sems, "gp", e))
    return nc


def _inputs_for_core(ci, x, slope, qw, qbv, ow, obv, theta, w1, cwv, cbv,
                     w2, consts):
    xc = np.ascontiguousarray(
        x[2 * ci:2 * ci + 2].reshape(T, D).T.astype(np.float32))
    d = dict(xT=xc)
    d.update(consts)
    return d


def kernel(L0_attn_x, L0_attn_slope_rate, L0_attn_qkvu_w, L0_attn_qkvu_b,
           L0_attn_out_w, L0_attn_out_b, L0_attn_theta, L0_ffn_w1,
           L0_ffn_conv_w, L0_ffn_conv_b, L0_ffn_w2, _phases=None):
    x = np.asarray(L0_attn_x, dtype=np.float32)
    consts = host_consts(L0_attn_slope_rate, L0_attn_qkvu_w, L0_attn_qkvu_b,
                         L0_attn_out_w, L0_attn_out_b, L0_attn_theta,
                         L0_ffn_w1, L0_ffn_conv_w, L0_ffn_conv_b, L0_ffn_w2)
    nc = build(_phases or ("prep", "ffna", "attn", "attno", "ffnc"))
    in_maps = []
    for ci in range(8):
        xc = np.ascontiguousarray(
            x[2 * ci:2 * ci + 2].reshape(T, D).T.astype(np.float32))
        m = dict(xT=xc)
        m.update(consts)
        in_maps.append(m)
    res = run_bass_kernel_spmd(nc, in_maps, core_ids=list(range(8)))
    outs = []
    for r in res.results:
        yT = np.asarray(r["y"])
        outs.append(yT.reshape(D, NB, S).transpose(1, 2, 0))
    return np.concatenate(outs, axis=0).astype(np.float32)


def host_consts(slope, qw, qbv, ow, obv, theta, w1, cwv, cbv, w2):
    w1T = np.ascontiguousarray(np.asarray(w1, np.float32).T)
    w2T = np.ascontiguousarray(np.asarray(w2, np.float32).T)
    qwT = np.ascontiguousarray(np.asarray(qw, np.float32).T)
    owT = np.ascontiguousarray(np.asarray(ow, np.float32).T)
    qbT = np.ascontiguousarray(
        np.asarray(qbv, np.float32).reshape(32, 128).T)
    obT = np.ascontiguousarray(
        np.asarray(obv, np.float32).reshape(8, 128).T)
    cbT = np.ascontiguousarray(
        np.asarray(cbv, np.float32).reshape(32, 128).T)
    cwT = np.ascontiguousarray(
        np.asarray(cwv, np.float32).reshape(32, 128, 3)
        .transpose(1, 0, 2).reshape(128, 96))
    thT = np.ascontiguousarray(
        np.asarray(theta, np.float32).reshape(8, 128).T)
    slp = np.ascontiguousarray(
        np.asarray(slope, np.float32).reshape(1, 8))
    idn = np.eye(128, dtype=np.float32)
    tri = np.triu(np.ones((128, 128), np.float32))
    iot = np.broadcast_to(np.arange(2048, dtype=np.float32),
                          (128, 2048)).copy()
    return dict(w1T=w1T, w2T=w2T, qwT=qwT, owT=owT, qbT=qbT, obT=obT,
                cbT=cbT, cwT=cwT, thT=thT, slp=slp, idn=idn, tri=tri,
                iot=iot)


# version 14
# speedup vs baseline: 1.1558x; 1.0350x over previous
import contextlib
import numpy as np
import ml_dtypes
import concourse.bass as bass
import concourse.mybir as mybir
from concourse.bass_utils import run_bass_kernel_spmd

F32 = mybir.dt.float32
BF16 = mybir.dt.bfloat16
I32 = mybir.dt.int32
AF = mybir.ActivationFunctionType
ALU = mybir.AluOpType
AX = mybir.AxisListType
BF = ml_dtypes.bfloat16

NB = 2
S = 2048
T = NB * S
D = 1024
NCH = 8
HID = 4096
HCH = 32
NH = 8
TT = 512
CS = 256
AMODE = "full"
LN127 = float(np.log(np.float64(127.0)))
PI2 = float(2.0 * np.pi)
PIH = float(np.pi / 2.0)
MAGIC = float(3 << 22)
ENGS = ("sy", "te", "ve", "sc", "gp")
KB_ = 1024


class K:

    SELF = {"te": "te", "ve": "ve", "sc": "sc", "gp": "gp"}

    def __init__(self, nc):
        self.nc = nc
        self.recs = {e: [] for e in ENGS}
        self.cnt = {k: 0 for k in ("ld", "gd", "te", "ve", "sc", "gp")}
        self.pend = {e: [] for e in ENGS}

    def rec(self, eng, fn, wait=None, inc=None, drain=False):
        if fn is None:
            return self.bar(eng, wait=wait, inc=inc)
        sem = self.SELF.get(eng)
        w = self.pend[eng] + list(wait or ())
        self.pend[eng] = []
        if sem is None:
            self.recs[eng].append((fn, tuple(w), None))
            return self.cnt.get(eng)
        if drain and self.cnt[sem] > 0:
            w.append((sem, self.cnt[sem]))
        self.cnt[sem] += 1
        out = self.cnt[sem]
        self.recs[eng].append((fn, tuple(w), sem))
        return out

    def bar(self, eng, wait=None, inc=None):
        if wait:
            self.pend[eng].extend(wait)
        return self.cnt.get(self.SELF.get(eng, eng), 0)

    def dma(self, eng, dst, src, wait=None, slow=False):
        sem = "ld" if eng == "sy" else "gd"
        prev = self.cnt[sem]
        self.cnt[sem] += 16
        out = self.cnt[sem]
        w = self.pend[eng] + list(wait or ())
        self.pend[eng] = []
        if prev > 0:
            w.append((sem, prev))
        self.recs[eng].append(
            (lambda e, d=dst, s=src, sl=slow: e.dma_start(
                d, s, allow_slow_non_contiguous=sl),
             tuple(w), sem))
        return out

    def v(self, sem):
        return self.cnt[sem]

    def emit(self, sems, eng, e):
        for fn, waits, inc in self.recs[eng]:
            dedup = {}
            for s, val in waits:
                if val > dedup.get(s, -1):
                    dedup[s] = val
            for s, val in dedup.items():
                if val > 0:
                    e.wait_ge(sems[s], val)
            inst = fn(e)
            if inc is not None:
                inst.then_inc(sems[inc], 16 if inc in ("ld", "gd") else 1)


def _drv(ap):
    return ap.rearrange("(c p) t -> p c t", p=128)


def build(phases=("prep", "ffna", "attn", "attno", "ffnc"), debug_out=None,
          taps=False):
    nc = bass.Bass("TRN2", target_bir_lowering=False)
    k = K(nc)
    k.taps = taps
    k.tapped = {}

    xT = nc.dram_tensor("xT", [D, T], F32, kind="ExternalInput").ap()
    w1T = nc.dram_tensor("w1T", [D, HID], F32, kind="ExternalInput").ap()
    w2T = nc.dram_tensor("w2T", [HID, D], F32, kind="ExternalInput").ap()
    qwT = nc.dram_tensor("qwT", [D, 4 * D], F32, kind="ExternalInput").ap()
    owT = nc.dram_tensor("owT", [D, D], F32, kind="ExternalInput").ap()
    qbT = nc.dram_tensor("qbT", [128, 32], F32, kind="ExternalInput").ap()
    obT = nc.dram_tensor("obT", [128, 8], F32, kind="ExternalInput").ap()
    cbT = nc.dram_tensor("cbT", [128, 32], F32, kind="ExternalInput").ap()
    cwT = nc.dram_tensor("cwT", [128, 96], F32, kind="ExternalInput").ap()
    thT = nc.dram_tensor("thT", [128, 8], F32, kind="ExternalInput").ap()
    slp = nc.dram_tensor("slp", [1, 8], F32, kind="ExternalInput").ap()
    idn = nc.dram_tensor("idn", [128, 128], F32, kind="ExternalInput").ap()
    tri = nc.dram_tensor("tri", [128, 128], F32, kind="ExternalInput").ap()
    iot = nc.dram_tensor("iot", [128, S], F32, kind="ExternalInput").ap()

    y = nc.dram_tensor("y", [D, T], F32, kind="ExternalOutput").ap()
    if taps:
        d_rp = nc.dram_tensor("d_rp", [128, 4096], BF16,
                              kind="ExternalOutput").ap()
        d_g = nc.dram_tensor("d_g", [128, HCH * 512], F32,
                             kind="ExternalOutput").ap()
        d_rh = nc.dram_tensor("d_rh", [128, HCH * 512], BF16,
                              kind="ExternalOutput").ap()
        d_ar2 = nc.dram_tensor("d_ar2", [128, 4096], F32,
                               kind="ExternalOutput").ap()
        d_rows = nc.dram_tensor("d_rows", [1, 4096], F32,
                                kind="ExternalOutput").ap()
    xs1 = nc.dram_tensor("xs1", [D, T], F32, kind="Internal").ap()
    xs2 = nc.dram_tensor("xs2", [D, T], F32, kind="Internal").ap()
    y2s = nc.dram_tensor("y2s", [D, T], BF16, kind="Internal").ap()
    rq3 = nc.dram_tensor("rq3", [D, T], BF16, kind="Internal").ap()
    wq1s = nc.dram_tensor("wq1s", [D, HID], BF16, kind="Internal").ap()
    wq2s = nc.dram_tensor("wq2s", [HID, D], BF16, kind="Internal").ap()
    wq2g = nc.dram_tensor("wq2g", [128, 32768], BF16, kind="Internal").ap()

    SBASE = 16384
    A = lambda n, sh, dt, off: nc.alloc_sbuf_tensor_at(
        n, sh, dt, offset=off + SBASE)
    identf = A("identf", [128, 128], F32, 0)
    identb = A("identb", [128, 128], BF16, 512)
    trib = A("trib", [128, 128], F32, 768)
    tribh = A("tribh", [128, 128], F32, 1280)
    qb = A("qb", [128, 32], F32, 1792)
    ob = A("ob", [128, 8], F32, 1920)
    cb = A("cb", [128, 32], F32, 1952)
    cw = A("cw", [128, 96], F32, 2080)
    thetaT = A("thetaT", [128, 8], F32, 2464)
    th2pi = A("th2pi", [128, 8], F32, 2496)
    esm1c = A("esm1c", [128, 8], F32, 2528)
    onesc = A("onesc", [128, 1], F32, 2560)
    onescb = A("onescb", [128, 1], BF16, 2592)
    onesr = A("onesr", [1, 128], F32, 2624)
    slpr = A("slpr", [1, 8], F32, 4192)
    esm1r = A("esm1r", [1, 8], F32, 4224)
    wsum = A("wsum", [128, 1], F32, 4256)
    scol = A("scol", [128, 1], F32, 4288)
    mcol = A("mcol", [128, 1], F32, 4320)
    srow = A("srow", [1, 8], F32, 4352)
    hedge = A("hedge", [128, 64], F32, 4384)
    rsqtm = A("rsqtm", [128, 32], F32, 4640)
    ceps5 = A("ceps5", [128, 1], F32, 4768)
    ceps6 = A("ceps6", [128, 1], F32, 4800)
    cl127 = A("cl127", [128, 1], F32, 4832)
    zcol = A("zcol", [128, 1], F32, 4864)
    zcolb = A("zcolb", [128, 1], BF16, 4896)
    cpih = A("cpih", [128, 1], F32, 4928)
    xedge = A("xedge", [128, 32], F32, 4960)
    nc.const_aps.register(F32, PIH, cpih[:, 0:1])
    nc.const_aps.register(F32, 1e-5, ceps5[:, 0:1])
    nc.const_aps.register(F32, 1e-6, ceps6[:, 0:1])
    nc.const_aps.register(F32, LN127, cl127[:, 0:1])
    nc.const_aps.register(F32, 0.0, zcol[:, 0:1])
    nc.const_aps.register(F32, 1.0, onesc[:, 0:1])
    nc.const_aps.register(BF16, 0.0, zcolb[:, 0:1])
    nc.const_aps.register(BF16, 1.0, onescb[:, 0:1])
    rows = A("rows", [1, 3072], F32, 8 * KB_)
    hbc = A("hbc", [128, 1024], F32, 20 * KB_)
    bca = A("bca", [128, 1536], F32, 24 * KB_)
    xa = A("xa", [128, 4096], F32, 30 * KB_)
    xb = A("xb", [128, 4096], F32, 46 * KB_)
    wba = A("wba", [128, 4096], BF16, 62 * KB_)
    wbb = A("wbb", [128, 4096], BF16, 70 * KB_)
    rp = A("rp", [128, 4096], BF16, 78 * KB_)
    Rh = A("Rh", [128, HCH * 512], BF16, 86 * KB_)
    y2st = A("y2st", [128, 4096], BF16, 86 * KB_)
    y2sq = A("y2sq", [128, 4096], BF16, 94 * KB_)
    sq = A("sq", [128, 4096], BF16, 118 * KB_)
    g = A("g", [128, HCH * 512], F32, 126 * KB_)
    ar2 = A("ar2", [128, 4096], F32, 190 * KB_)
    wfull = A("wfull", [128, 32768], F32, 30 * KB_)
    p1xs = A("p1xs", [128, 4096], F32, 158 * KB_)
    p1wq = A("p1wq", [128, 4096], BF16, 8 * KB_)

    tgr = A("tgr", [128, S], F32, 8 * KB_)
    sinf = A("sinf", [128, S], BF16, 16 * KB_)
    cosf = A("cosf", [128, S], BF16, 20 * KB_)
    tgf = A("tgf", [128, S], F32, 158 * KB_)
    tgf2 = A("tgf2", [128, S], F32, 166 * KB_)
    asb = A("asb", [128, 256], BF16, 26752)
    bsbB = A("bsbB", [128, 256], BF16, 27264)
    bsbf = A("bsbf", [128, 256], F32, 27776)
    fsb = A("fsb", [128, 768], BF16, 28800)
    bsbA = A("bsbA", [128, 1536], BF16, 5088)
    sqat = A("sqat", [128, 256], BF16, 211968)
    y2 = A("y2", [128, NCH * T], BF16, 30 * KB_)
    pat = A("pat", [128, NH * T], BF16, 94 * KB_)
    silq = A("silq", [128, S], BF16, 158 * KB_)
    silk = A("silk", [128, S], BF16, 162 * KB_)
    qcb = A("qcb", [128, S], BF16, 166 * KB_)
    qsb = A("qsb", [128, S], BF16, 170 * KB_)
    kcb = A("kcb", [128, S], BF16, 174 * KB_)
    ksb = A("ksb", [128, S], BF16, 178 * KB_)
    kcT = A("kcT", [128, S], BF16, 182 * KB_)
    ksT = A("ksT", [128, S], BF16, 186 * KB_)
    vtm = A("vtm", [128, S], BF16, 190 * KB_)
    uh = A("uh", [128, S], BF16, 194 * KB_)
    wqh = A("wqh", [128, 4096], BF16, 198 * KB_)
    vfm = A("vfm", [128, 512], BF16, 206 * KB_)

    zsb = A("zsb", [128, 4096], BF16, 30 * KB_)
    owb = A("owb", [128, 8192], BF16, 38 * KB_)
    x4a = A("x4a", [128, 4096], F32, 54 * KB_)
    sq4 = A("sq4", [128, 4096], BF16, 70 * KB_)

    psA = nc.alloc_psum_tensor("psA", [128, 512], F32)
    psB = nc.alloc_psum_tensor("psB", [128, 512], F32)
    psC = nc.alloc_psum_tensor("psC", [128, 512], F32)
    psD = nc.alloc_psum_tensor("psD", [128, 512], F32)
    psE = nc.alloc_psum_tensor("psE", [128, 512], F32)
    psF = nc.alloc_psum_tensor("psF", [128, 512], F32)
    psG = nc.alloc_psum_tensor("psG", [128, 512], F32)
    psT = nc.alloc_psum_tensor("psT", [128, 1024], BF16)

    for dst, src in [(identf, idn), (trib, tri), (qb, qbT), (ob, obT),
                     (cb, cbT), (cw, cwT), (thetaT, thT),
                     (slpr, slp)]:
        k.dma("sy", dst[:], src)
    vld0 = k.v("ld")
    k.rec("ve", lambda e: e.memset(onesc[:], 1.0))
    k.rec("ve", lambda e: e.memset(ceps5[:], 1e-5))
    k.rec("ve", lambda e: e.memset(zcol[:], 0.0))
    k.rec("ve", lambda e: e.memset(zcolb[:], 0.0))
    k.rec("ve", lambda e: e.memset(ceps6[:], 1e-6))
    k.rec("ve", lambda e: e.memset(cl127[:], LN127))
    k.rec("ve", lambda e: e.memset(onescb[:], 1.0))
    k.rec("ve", lambda e: e.memset(onesr[:], 1.0))
    k.rec("ve", lambda e: e.memset(cpih[:], PIH))
    k.rec("ve", lambda e: e.memset(hedge[:], 0.0))
    k.rec("ve", lambda e: e.tensor_scalar_mul(th2pi[:], thetaT[:], 1.0 / PI2),
          wait=[("ld", vld0)], drain=True)
    k.rec("ve", lambda e: e.tensor_copy(identb[:], identf[:]), drain=True)
    v_ms = k.bar("ve", inc="ve")
    k.rec("sc", lambda e: e.activation(esm1r[:], slpr[:], AF.Exp),
          wait=[("ld", vld0), ("ve", v_ms)])
    vs = k.bar("sc", inc="sc")
    vv = k.rec("ve", lambda e: e.tensor_scalar_add(esm1r[:], esm1r[:], -1.0),
               wait=[("sc", vs)], inc="ve", drain=True)
    vt = k.rec("te", lambda e: e.matmul(psD[0:128, 0:8], onesr[0:1, :],
                                        esm1r[0:1, :], start=True, stop=True),
               wait=[("ve", vv)], inc="te")
    v_const = k.rec("ve", lambda e: e.tensor_copy(esm1c[:], psD[:, 0:8]),
                    wait=[("te", vt)], inc="ve", drain=True)

    def wprep(wsrc, wdst, nelem, gdst=None):
        nsl = nelem // (128 * 4096)
        wfv = wfull[:, 0:nsl * 4096]
        vld = k.dma("sy",
                    wfv.rearrange("p (c t) -> p c t",
                                  c=_drv(wsrc).shape[1]),
                    _drv(wsrc),
                    wait=[("ve", k.v("ve")), ("gd", k.v("gd"))])
        vv = k.rec("ve", lambda e: e.reduce_sum(
            wsum[:], wfv, AX.X, apply_absolute_value=True),
            wait=[("ld", vld)], inc="ve", drain=True)
        vt = k.rec("te", lambda e: e.matmul(
            psD[0:1, 0:1], wsum[:, 0:1], onesc[:, 0:1], start=True,
            stop=True), wait=[("ve", vv)], inc="te")
        k.rec("ve", lambda e, n=nelem: e.tensor_scalar(
            srow[0:1, 0:1], psD[0:1, 0:1], 1.0 / n, 1e-5, ALU.mult,
            ALU.max), wait=[("te", vt)], drain=True)
        vv = k.rec("ve", lambda e: e.reciprocal(srow[0:1, 1:2],
                                                srow[0:1, 0:1]),
                   inc="ve", drain=True)
        vt = k.rec("te", lambda e: e.matmul(
            psD[0:128, 4:6], onesr[0:1, :], srow[0:1, 0:2], start=True,
            stop=True), wait=[("ve", vv)], inc="te")
        k.rec("ve", lambda e: e.tensor_copy(mcol[:, 0:1], psD[:, 4:5]),
              wait=[("te", vt)], drain=True)
        k.rec("ve", lambda e: e.tensor_copy(scol[:, 0:1], psD[:, 5:6]),
              drain=True)
        vst = 0
        for i in range(nsl):
            sl = wfull[:, i * 4096:(i + 1) * 4096]
            k.rec("ve", lambda e, sl=sl: e.tensor_scalar(
                p1xs[:], sl, scol[:, 0:1], MAGIC, ALU.mult, ALU.add),
                wait=[("gd", vst)] if vst else None, drain=True)
            k.rec("ve", lambda e: e.tensor_scalar(
                p1xs[:], p1xs[:], -MAGIC, 1.0, ALU.add, ALU.min), drain=True)
            vv = k.rec("ve", lambda e: e.tensor_scalar(
                p1wq[:], p1xs[:], -1.0, mcol[:, 0:1], ALU.max, ALU.mult),
                inc="ve", drain=True)
            ncs = _drv(wdst).shape[1] * 4096 * 128 // nelem
            vst = k.dma("gp",
                        _drv(wdst)[:, i * ncs:(i + 1) * ncs, :],
                        p1wq[:].rearrange("p (c t) -> p c t", c=ncs),
                        wait=[("ve", vv)])
            if gdst is not None:
                vst = k.dma("gp", gdst[:, i * 4096:(i + 1) * 4096],
                            p1wq[:])
        return vst

    if "prep" in phases:
        wprep(w1T, wq1s, D * HID)
        v_prep = wprep(w2T, wq2s, HID * D, gdst=wq2g)
    else:
        v_prep = 0

    def bcast(row_ap, w, wait):
        vt = k.rec("te", lambda e: e.matmul(
            psC[0:128, 0:w], onesr[0:1, :], row_ap, start=True, stop=True),
            wait=wait, inc="te")
        return k.rec("ve", lambda e: e.tensor_copy(bca[:, 0:w], psC[:, 0:w]),
                     wait=[("te", vt)], inc="ve", drain=True)

    def bcast2(row_ap, w, wait):
        vt = k.rec("te", lambda e: e.matmul(
            psC[0:128, 0:w], onesr[0:1, :], row_ap, start=True, stop=True),
            wait=wait, inc="te")
        return k.rec("ve", lambda e: e.tensor_copy(
            bca[:, 512:512 + w], psC[:, 0:w]), wait=[("te", vt)], inc="ve",
            drain=True)

    def bcast_to(dst, row_ap, w, wait):
        vt = k.rec("te", lambda e: e.matmul(
            psC[0:128, 0:w], onesr[0:1, :], row_ap, start=True, stop=True),
            wait=wait, inc="te")
        return k.rec("ve", lambda e, dst=dst: e.tensor_copy(
            dst[:, 0:w], psC[:, 0:w]), wait=[("te", vt)], inc="ve",
            drain=True)

    def cross_max(src01, w, row_dst, wait):
        nblk = (w + 127) // 128
        vts = []
        for j in range(nblk):
            wj = min(128, w - j * 128)
            vt = k.rec("te", lambda e, j=j, wj=wj: e.transpose(
                psF[0:wj, j * 128:j * 128 + 128],
                src01[:, j * 128:j * 128 + wj], identf[:]),
                wait=wait if j == 0 else None, inc="te")
            vts.append((j, wj, vt))
        v_all_tr = vts[-1][2]
        if w == 1:
            return k.rec("ve", lambda e: e.reduce_max(
                row_dst, psF[0:1, 0:128], AX.X, apply_absolute_value=True),
                wait=[("te", v_all_tr)], inc="ve", drain=True)
        vvs = []
        for j, wj, vt in vts:
            vv = k.rec("ve", lambda e, j=j, wj=wj: e.reduce_max(
                bca[0:wj, 1024 + j:1025 + j],
                psF[0:wj, j * 128:j * 128 + 128], AX.X,
                apply_absolute_value=True),
                wait=[("te", v_all_tr)] if j == 0 else None,
                inc="ve", drain=True)
            vvs.append((j, wj, vv))
        vt2 = None
        for j, wj, vv in vvs:
            vt2 = k.rec("te", lambda e, j=j, wj=wj: e.transpose(
                psG[0:1, j * 128:j * 128 + wj],
                bca[0:wj, 1024 + j:1025 + j], identf[0:wj, 0:wj]),
                wait=[("ve", vv)], inc="te")
        return k.rec("ve", lambda e, w=w: e.tensor_copy(
            row_dst, psG[0:1, 0:w]), wait=[("te", vt2)], inc="ve",
            drain=True)

    def rstd_rows(sum_ps_w, w, dst_off, wait=None):
        vv = k.rec("ve", lambda e: e.tensor_scalar_mul(
            rows[0:1, dst_off:dst_off + w], sum_ps_w, 1.0 / D),
            wait=wait, inc="ve", drain=True)
        k.rec("sc", lambda e: e.activation(
            rows[0:1, dst_off:dst_off + w], rows[0:1, dst_off:dst_off + w],
            AF.Ln, bias=1e-5), wait=[("ve", vv)], drain=True)
        return k.rec("sc", lambda e: e.activation(
            rows[0:1, dst_off:dst_off + w], rows[0:1, dst_off:dst_off + w],
            AF.Exp, scale=-0.5), inc="sc", drain=True)

    def stat_sums(src, w, ones, wait, stop_bank=None):
        vt = None
        for ch in range(NCH):
            vt = k.rec("te", lambda e, ch=ch: e.matmul(
                psD[0:1, 0:w], ones[:, 0:1],
                src[:, ch * 512:ch * 512 + w],
                start=(ch == 0), stop=(ch == NCH - 1)),
                wait=wait if ch == 0 else None, inc="te")
        return vt

    def round_chunk(dst, src, w, bc_rs_off, bc_a_off, wait, scale_first,
                    alpha_mult):
        if scale_first:
            k.rec("ve", lambda e: e.tensor_tensor(
                src, src, bca[:, bc_rs_off:bc_rs_off + w], ALU.mult),
                wait=wait, drain=True)
            wait = None
        if alpha_mult:
            k.rec("ve", lambda e: e.tensor_scalar(
                src, src, MAGIC, -MAGIC, ALU.add, ALU.add), wait=wait,
                drain=True)
            return k.rec("ve", lambda e: e.tensor_tensor(
                dst, src, bca[:, bc_a_off:bc_a_off + w], ALU.mult),
                inc="ve", drain=True)
        return k.rec("ve", lambda e: e.tensor_scalar(
            dst, src, MAGIC, -MAGIC, ALU.add, ALU.add), wait=wait,
            inc="ve", drain=True)

    def ln_quant(src, w, rdst, wait, xc_wait=None, sqb=None, xcb=None):
        xcb = ar2 if xcb is None else xcb
        vt = stat_sums(src, w, onesc,
                       list(wait) + [("ve", k.v("ve")), ("sc", k.v("sc"))])
        vv = k.rec("ve", lambda e: e.tensor_scalar_mul(
            rows[0:1, 0:w], psD[0:1, 0:w], 1.0 / D), wait=[("te", vt)],
            inc="ve", drain=True)
        vv = bcast(rows[0:1, 0:w], w, [("ve", vv)])
        for ch in range(NCH):
            wtx = ([("ve", vv)] + list(xc_wait or ())) if ch == 0 else None
            k.rec("ve", lambda e, ch=ch: e.tensor_tensor(
                xcb[:, ch * 512:ch * 512 + w],
                src[:, ch * 512:ch * 512 + w], bca[:, 0:w], ALU.subtract),
                wait=wtx, drain=True)
        vv = k.bar("ve", inc="ve")
        sqb = sq if sqb is None else sqb
        for ch in range(NCH):
            k.rec("sc", lambda e, ch=ch, sqb=sqb: e.activation(
                sqb[:, ch * 512:ch * 512 + w],
                xcb[:, ch * 512:ch * 512 + w], AF.Square),
                wait=[("ve", vv), ("te", k.v("te"))] if ch == 0 else None,
                drain=(ch == 0))
        vs = k.bar("sc", inc="sc")
        vt = stat_sums(sqb, w, onescb, [("sc", vs)])
        vs = rstd_rows(psD[0:1, 0:w], w, 512, wait=[("te", vt)])
        xcv = xcb[:, 0:NCH * 512].rearrange(
            "p (c t) -> p t c", c=NCH)[:, 0:w, :]
        vv = k.rec("ve", lambda e: e.reduce_max(
            bca[:, 512:512 + w], xcv, AX.X, apply_absolute_value=True),
            inc="ve", drain=True)
        vv = cross_max(bca[:, 512:], w, rows[0:1, 1024:1024 + w],
                       [("ve", vv)])
        k.rec("ve", lambda e: e.tensor_tensor(
            rows[0:1, 1024:1024 + w], rows[0:1, 1024:1024 + w],
            rows[0:1, 512:512 + w], ALU.mult), wait=[("sc", vs)],
            drain=True)
        k.rec("ve", lambda e: e.tensor_single_scalar(
            rows[0:1, 1024:1024 + w], rows[0:1, 1024:1024 + w], 1e-5,
            ALU.max), drain=True)
        vv = k.rec("ve", lambda e: e.tensor_scalar_mul(
            rows[0:1, 2048:2048 + w], rows[0:1, 1024:1024 + w],
            1.0 / 127.0), inc="ve", drain=True)
        k.rec("sc", lambda e: e.activation(
            rows[0:1, 1024:1024 + w], rows[0:1, 1024:1024 + w], AF.Ln),
            wait=[("ve", vv)], drain=True)
        vs = k.rec("sc", lambda e: e.activation(
            rows[0:1, 1024:1024 + w], rows[0:1, 1024:1024 + w], AF.Exp,
            scale=-1.0, bias=LN127), inc="sc", drain=True)
        vv = k.rec("ve", lambda e: e.tensor_tensor(
            rows[0:1, 1536:1536 + w], rows[0:1, 512:512 + w],
            rows[0:1, 1024:1024 + w], ALU.mult), wait=[("sc", vs)],
            inc="ve", drain=True)
        vv = bcast(rows[0:1, 1536:1536 + w], w, [("ve", vv)])
        vv = bcast2(rows[0:1, 2048:2048 + w], w, [("ve", vv)])
        vlast = None
        for ch in range(NCH):
            vlast = round_chunk(
                rdst(ch), xcb[:, ch * 512:ch * 512 + w], w, 0, 512,
                [("ve", vv)] if ch == 0 else None, True, True)
        return vlast

    def conv_group(cg, banks, vts, isfirst, w, extra_wait):
        scr = lambda cl, a, b: ar2[:, cl * 512 + a: cl * 512 + b]
        vs1 = None
        for cl in range(4):
            c = cg * 4 + cl
            hb = banks[cl]
            cw2 = cw[:, c * 3 + 2:c * 3 + 3]
            cbc = cb[:, c:c + 1]
            wt = [("te", vts[cl])]
            if cl == 0:
                wt += extra_wait
            if isfirst:
                vs1 = k.rec("sc", lambda e, cl=cl, hb=hb, cw2=cw2, cbc=cbc:
                            e.activation(scr(cl, 0, 511), hb[:, 1:512],
                                         AF.Identity, bias=cbc, scale=cw2),
                            wait=wt, inc="sc", drain=(cl == 0))
            else:
                vs1 = k.rec("sc", lambda e, cl=cl, hb=hb, cw2=cw2, cbc=cbc:
                            e.activation(scr(cl, 0, 512), hb[:, 0:512],
                                         AF.Identity, bias=cbc, scale=cw2),
                            wait=wt, inc="sc", drain=(cl == 0))
        for cl in range(4):
            c = cg * 4 + cl
            hb = banks[cl]
            cw1 = cw[:, c * 3 + 1:c * 3 + 2]
            he1 = hedge[:, c * 2 + 1:c * 2 + 2]
            wt = [("sc", vs1)] if cl == 0 else None
            if isfirst:
                k.rec("ve", lambda e, cl=cl, hb=hb, cw1=cw1:
                      e.scalar_tensor_tensor(
                    scr(cl, 0, 511), hb[:, 0:511], cw1, scr(cl, 0, 511),
                    ALU.mult, ALU.add), wait=wt, drain=(cl == 0))
            else:
                k.rec("ve", lambda e, cl=cl, hb=hb, cw1=cw1:
                      e.scalar_tensor_tensor(
                    scr(cl, 1, 512), hb[:, 0:511], cw1, scr(cl, 1, 512),
                    ALU.mult, ALU.add), wait=wt, drain=(cl == 0))
                k.rec("ve", lambda e, cl=cl, cw1=cw1, he1=he1:
                      e.scalar_tensor_tensor(
                    scr(cl, 0, 1), he1, cw1, scr(cl, 0, 1),
                    ALU.mult, ALU.add))
        for cl in range(4):
            c = cg * 4 + cl
            hb = banks[cl]
            cw0 = cw[:, c * 3 + 0:c * 3 + 1]
            he0 = hedge[:, c * 2:c * 2 + 1]
            he1 = hedge[:, c * 2 + 1:c * 2 + 2]
            gv = lambda a, b, c=c: g[:, c * 512 + a:c * 512 + b]
            if isfirst:
                k.rec("ve", lambda e, cl=cl, hb=hb, cw0=cw0, gv=gv:
                      e.scalar_tensor_tensor(
                    gv(1, 511), hb[:, 0:510], cw0, scr(cl, 1, 511),
                    ALU.mult, ALU.add), drain=(cl == 0))
                k.rec("ve", lambda e, cl=cl, gv=gv: e.tensor_copy(
                    gv(0, 1), scr(cl, 0, 1)))
            else:
                k.rec("ve", lambda e, cl=cl, hb=hb, cw0=cw0, gv=gv:
                      e.scalar_tensor_tensor(
                    gv(2, 512), hb[:, 0:510], cw0, scr(cl, 2, 512),
                    ALU.mult, ALU.add), drain=(cl == 0))
                k.rec("ve", lambda e, cl=cl, cw0=cw0, he0=he0, gv=gv:
                      e.scalar_tensor_tensor(
                    gv(0, 1), he0, cw0, scr(cl, 0, 1), ALU.mult, ALU.add))
                k.rec("ve", lambda e, cl=cl, cw0=cw0, he1=he1, gv=gv:
                      e.scalar_tensor_tensor(
                    gv(1, 2), he1, cw0, scr(cl, 1, 2), ALU.mult, ALU.add))
        vcar = None
        for cl in range(4):
            c = cg * 4 + cl
            hb = banks[cl]
            vcar = k.rec("ve", lambda e, c=c, hb=hb: e.tensor_copy(
                hedge[:, c * 2:c * 2 + 2], hb[:, 510:512]), inc="ve",
                drain=(cl == 0))
        vsil = None
        ncol = 511 if isfirst else 512
        for cl in range(4):
            c = cg * 4 + cl
            vsil = k.rec("sc", lambda e, c=c, ncol=ncol: e.activation(
                g[:, c * 512:c * 512 + ncol], g[:, c * 512:c * 512 + ncol],
                AF.Silu), wait=[("ve", vcar)] if cl == 0 else None,
                inc="sc", drain=(cl == 0))
        return vcar, vsil

    def round_hidden_grouped(w, v_rows, rh_gd=0):
        vR = None
        vpool = None
        vR_list = []
        for gi in range(8):
            h0 = gi * 4
            for cl in range(4):
                hc = h0 + cl
                vpool = k.rec("gp", lambda e, hc=hc, w=w: e.tensor_tensor(
                    g[:, hc * 512:hc * 512 + w],
                    g[:, hc * 512:hc * 512 + w], hbc[:, 0:w], ALU.mult),
                    wait=([("ve", v_rows), ("sc", k.v("sc"))]
                          if (gi == 0 and cl == 0) else None),
                    inc="gp", drain=(cl == 0))
            for cl in range(4):
                hc = h0 + cl
                wtr = None
                if cl == 0:
                    wtr = [("gp", vpool)]
                    if gi == 0 and rh_gd:
                        wtr.append(("gd", rh_gd))
                vR = k.rec("ve", lambda e, hc=hc, w=w: e.tensor_scalar(
                    Rh[:, hc * 512:hc * 512 + w],
                    g[:, hc * 512:hc * 512 + w], MAGIC, -MAGIC, ALU.add,
                    ALU.add), wait=wtr, inc="ve", drain=(cl == 0))
            vR_list.append(vR)
        return vR_list, vpool

    def conv_chunk(c, hb, v_te, isfirst, w):
        gv = lambda a, b: g[:, c * 512 + a:c * 512 + b]
        cw2 = cw[:, c * 3 + 2:c * 3 + 3]
        cw1 = cw[:, c * 3 + 1:c * 3 + 2]
        cw0 = cw[:, c * 3 + 0:c * 3 + 1]
        cbc = cb[:, c:c + 1]
        he0 = hedge[:, c * 2:c * 2 + 1]
        he1 = hedge[:, c * 2 + 1:c * 2 + 2]
        scr = bca[:, 1024:1536]
        wt = [("te", v_te)]
        if isfirst:
            k.rec("ve", lambda e: e.tensor_scalar(
                scr[:, 0:511], hb[:, 1:512], cw2, cbc, ALU.mult, ALU.add),
                wait=wt, drain=True)
            k.rec("ve", lambda e: e.scalar_tensor_tensor(
                scr[:, 0:511], hb[:, 0:511], cw1, scr[:, 0:511], ALU.mult,
                ALU.add), drain=True)
            k.rec("ve", lambda e: e.scalar_tensor_tensor(
                gv(1, 511), hb[:, 0:510], cw0, scr[:, 1:511], ALU.mult,
                ALU.add), drain=True)
            k.rec("ve", lambda e: e.tensor_copy(gv(0, 1), scr[:, 0:1]),
                  drain=True)
        else:
            k.rec("ve", lambda e: e.tensor_scalar(
                scr[:, 0:512], hb[:, 0:512], cw2, cbc, ALU.mult, ALU.add),
                wait=wt, drain=True)
            k.rec("ve", lambda e: e.scalar_tensor_tensor(
                scr[:, 1:512], hb[:, 0:511], cw1, scr[:, 1:512], ALU.mult,
                ALU.add), drain=True)
            k.rec("ve", lambda e: e.scalar_tensor_tensor(
                scr[:, 0:1], he1, cw1, scr[:, 0:1], ALU.mult, ALU.add),
                drain=True)
            k.rec("ve", lambda e: e.scalar_tensor_tensor(
                gv(2, 512), hb[:, 0:510], cw0, scr[:, 2:512], ALU.mult,
                ALU.add), drain=True)
            k.rec("ve", lambda e: e.scalar_tensor_tensor(
                gv(0, 1), he0, cw0, scr[:, 0:1], ALU.mult, ALU.add),
                drain=True)
            k.rec("ve", lambda e: e.scalar_tensor_tensor(
                gv(1, 2), he1, cw0, scr[:, 1:2], ALU.mult, ALU.add),
                drain=True)
        k.rec("ve", lambda e: e.tensor_copy(he0, hb[:, 510:511]), drain=True)
        vv = k.rec("ve", lambda e: e.tensor_copy(he1, hb[:, 511:512]),
                   inc="ve", drain=True)
        k.rec("sc", lambda e, w=w: e.activation(gv(0, w), gv(0, w),
                                                AF.Silu),
              wait=[("ve", vv)], drain=True)

    def tail_col():
        he0v = hedge[:].rearrange("p (c e) -> p c e", e=2)[:, :, 0:1]
        he1v = hedge[:].rearrange("p (c e) -> p c e", e=2)[:, :, 1:2]
        cw0v = cw[:].rearrange("p (c e) -> p c e", e=3)[:, :, 0:1]
        cw1v = cw[:].rearrange("p (c e) -> p c e", e=3)[:, :, 1:2]
        cbv = cb[:].rearrange("p (c e) -> p c e", e=1)
        gtv = g[:].rearrange("p (c t) -> p c t", t=512)[:, :, 0:1]
        scr = bca[:, 1504:1536].rearrange("p (c e) -> p c e", e=1)
        k.rec("ve", lambda e: e.tensor_tensor(scr, he0v, cw0v, ALU.mult),
              drain=True)
        k.rec("ve", lambda e: e.tensor_tensor(gtv, he1v, cw1v, ALU.mult),
              drain=True)
        k.rec("ve", lambda e: e.tensor_tensor(gtv, gtv, scr, ALU.add),
              drain=True)
        vv = k.rec("ve", lambda e: e.tensor_tensor(gtv, gtv, cbv, ALU.add),
                   inc="ve", drain=True)
        k.rec("sc", lambda e: e.activation(
            g[:].rearrange("p (c t) -> p c t", t=512)[:, :, 0:1],
            g[:].rearrange("p (c t) -> p c t", t=512)[:, :, 0:1], AF.Silu),
            wait=[("ve", vv)], drain=True)

    def hidden_rows(w, pmax=None):
        vs = k.bar("sc", inc="sc")
        if pmax is None:
            gvv = g[:].rearrange("p (c t) -> p t c", c=HCH)[:, 0:w, :]
            vv = k.rec("ve", lambda e: e.reduce_max(
                bca[:, 512:512 + w], gvv, AX.X, apply_absolute_value=True),
                wait=[("sc", vs)], inc="ve", drain=True)
        else:
            vv = pmax
        vv = cross_max(bca[:, 512:], w, rows[0:1, 1024:1024 + w],
                       [("ve", vv)])
        k.rec("ve", lambda e: e.tensor_single_scalar(
            rows[0:1, 1024:1024 + w], rows[0:1, 1024:1024 + w], 1e-5,
            ALU.max), wait=[("ve", vv)], drain=True)
        vv = k.rec("ve", lambda e: e.tensor_scalar_mul(
            rows[0:1, 2560:2560 + w], rows[0:1, 1024:1024 + w],
            0.5 / 127.0), inc="ve", drain=True)
        k.rec("sc", lambda e: e.activation(
            rows[0:1, 1024:1024 + w], rows[0:1, 1024:1024 + w], AF.Ln),
            wait=[("ve", vv)], drain=True)
        vs = k.rec("sc", lambda e: e.activation(
            rows[0:1, 1024:1024 + w], rows[0:1, 1024:1024 + w], AF.Exp,
            scale=-1.0, bias=LN127), inc="sc", drain=True)
        return bcast_to(hbc, rows[0:1, 1024:1024 + w], w,
                        [("sc", vs), ("ve", k.v("ve"))])

    def ffn(x_src, x1_dst, do_ln2, rq_src, wait0):
        wins = []
        for sqi in range(NB):
            wins.append((sqi, 0, 0, 511))
            for kk in range(1, 4):
                wins.append((sqi, 1, kk * 512, 512))
            wins.append((sqi, 2, None, 1))
        NW = len(wins)
        wb_free = {id(wba): 0, id(wbb): 0}
        pb_free = {}
        HB = [psA, psB, psE, psF, psG]
        v_xfree = {}
        v_xload = {}
        v_rpq = {}
        st8 = {"x1gd": 0,
               "y2gd": 0,
               "w1te": 0}

        def xload(wi):
            sqi, kind, t0, w = wins[wi]
            xcur = xa if wi % 2 == 0 else xb
            wt = list(wait0) if wi <= 1 else [("ve", v_xfree[wi - 2])]
            vx = k.dma("sy", xcur[:].rearrange("p (c t) -> p c t", c=NCH),
                       _drv(x_src)[:, :, sqi * S + t0:sqi * S + t0 + TT],
                       wait=wt)
            v_xload[wi] = vx
            k.rec("ve", lambda e, xcur=xcur, wi=wi: e.tensor_copy(
                xedge[:, (wi % 4) * 8:(wi % 4) * 8 + 8].rearrange(
                    "p (c t) -> p c t", t=1),
                xcur[:].rearrange("p (c t) -> p c t", c=NCH)
                [:, :, 511:512]), wait=[("ld", vx)], inc="ve", drain=True)

        def prep_rp(wi, gp_wait):
            sqi, kind, t0, w = wins[wi]
            xcur = xa if wi % 2 == 0 else xb
            if rq_src is None:
                v = ln_quant(xcur, TT,
                             lambda ch: rp[:, ch * 512:(ch + 1) * 512],
                             [("ld", v_xload[wi]), ("te", st8["w1te"])],
                             xc_wait=[("gp", gp_wait)], xcb=g)
                v_rpq[wi] = ("ve", v)
            else:
                v_rpq[wi] = ("ld", k.dma(
                    "sy", rp[:].rearrange("p (c t) -> p c t", c=NCH),
                    _drv(rq_src)[:, :, sqi * S + t0:sqi * S + t0 + TT],
                    wait=[("te", st8["w1te"]), ("ve", k.v("ve")),
                          ("gd", k.v("gd"))]))
            pb_free[id(psF)] = {"ve": k.v("ve"), "sc": k.v("sc")}
            pb_free[id(psG)] = {"ve": k.v("ve"), "sc": k.v("sc")}

        xload(0)
        if NW > 1 and wins[1][1] != 2:
            xload(1)
        prep_rp(0, 0)
        for wi, (sqi, kind, t0, w) in enumerate(wins):
            base = sqi * S
            xcur = xa if (wi % 2 == 0) else xb
            gw0 = (t0 if kind == 0 else t0 - 1) if kind != 2 else S - 1
            if kind != 2:
                for cg in range(8):
                    wb = wba if cg % 2 == 0 else wbb
                    vw = k.dma("sy", wb[:].rearrange("p (c t) -> p c t",
                                                     c=NCH),
                               _drv(wq1s)[:, :, cg * 512:(cg + 1) * 512],
                               wait=[("te", wb_free[id(wb)])])
                    banks = [HB[(cg * 4 + cl) % 5] for cl in range(4)]
                    vts = []
                    for cl in range(4):
                        hb = banks[cl]
                        pf = pb_free.get(id(hb), {"ve": 0, "sc": 0})
                        vt = None
                        for ic in range(NCH):
                            vt = k.rec("te", lambda e, ic=ic, cl=cl, hb=hb,
                                       wb=wb: e.matmul(
                                hb[:, 0:TT],
                                wb[:, ic * 512 + cl * 128:
                                   ic * 512 + cl * 128 + 128],
                                rp[:, ic * 512:(ic + 1) * 512],
                                start=(ic == 0), stop=(ic == NCH - 1)),
                                wait=[("ld", vw), v_rpq[wi],
                                      ("ve", pf["ve"]), ("sc", pf["sc"])]
                                if ic == 0 else None, inc="te")
                        vts.append(vt)
                    xw = [("gd", st8["x1gd"]), ("ve", k.v("ve"))] \
                        if cg == 0 else []
                    conv_group(cg, banks, vts, kind == 0, w, xw)
                    for cl in range(4):
                        pb_free[id(banks[cl])] = {"ve": k.v("ve"),
                                                  "sc": k.v("sc")}
                    wb_free[id(wb)] = k.v("te")
                    gslc = g[:, cg * 2048:(cg + 1) * 2048].rearrange(
                        "p (c t) -> p t c", c=4)[:, 0:w, :]
                    if cg == 0:
                        pmax = k.rec("ve", lambda e, gslc=gslc, w=w:
                                     e.reduce_max(
                            bca[:, 512:512 + w], gslc, AX.X,
                            apply_absolute_value=True),
                            wait=[("sc", k.v("sc"))], inc="ve", drain=True)
                    else:
                        k.rec("ve", lambda e, gslc=gslc, w=w: e.reduce_max(
                            bca[:, 1024:1024 + w], gslc, AX.X,
                            apply_absolute_value=True),
                            wait=[("sc", k.v("sc"))], drain=True)
                        pmax = k.rec("ve", lambda e, w=w: e.tensor_tensor(
                            bca[:, 512:512 + w], bca[:, 512:512 + w],
                            bca[:, 1024:1024 + w], ALU.max),
                            inc="ve", drain=True)
                st8["w1te"] = k.v("te")
            else:
                tail_col()
                pmax = None
            vbc = hidden_rows(w, pmax=pmax)
            pb_free[id(psF)] = {"ve": k.v("ve"), "sc": k.v("sc")}
            pb_free[id(psG)] = {"ve": k.v("ve"), "sc": k.v("sc")}
            vR_g, vpool = round_hidden_grouped(w, vbc, rh_gd=st8["y2gd"])
            vR = vR_g[-1]
            nx = wi + 1

            def res_fc(fc, ob_, vt, vbb_):
                a2 = lambda a, b, fc=fc: ar2[:, fc * 512 + a:fc * 512 + b]
                k.rec("ve", lambda e, ob_=ob_, w=w, a2=a2: e.tensor_tensor(
                    a2(0, w), ob_[:, 0:w], hbc[:, 512:512 + w], ALU.mult),
                    wait=[("te", vt), ("ve", vbb_), ("gd", st8["x1gd"])],
                    drain=True)
                ecol = (wi - 1) % 4 * 8 + fc
                if kind == 0:
                    vv = k.rec("ve", lambda e, fc=fc, w=w, xcur=xcur, a2=a2:
                               e.tensor_tensor(
                        a2(0, w), a2(0, w),
                        xcur[:, fc * 512:fc * 512 + w], ALU.add),
                        inc="ve", drain=True)
                elif kind == 1:
                    k.rec("ve", lambda e, ecol=ecol, a2=a2:
                          e.tensor_tensor(
                        a2(0, 1), a2(0, 1),
                        xedge[:, ecol:ecol + 1], ALU.add), drain=True)
                    vv = k.rec("ve", lambda e, fc=fc, w=w, xcur=xcur, a2=a2:
                               e.tensor_tensor(
                        a2(1, w), a2(1, w),
                        xcur[:, fc * 512:fc * 512 + w - 1], ALU.add),
                        inc="ve", drain=True)
                else:
                    vv = k.rec("ve", lambda e, ecol=ecol, a2=a2:
                               e.tensor_tensor(
                        a2(0, 1), a2(0, 1),
                        xedge[:, ecol:ecol + 1], ALU.add),
                        inc="ve", drain=True)
                pb_free[id(ob_)] = {"ve": k.v("ve"), "sc": k.v("sc")}
                return vv

            if kind != 2:
                B5 = [psA, psB, psE, psF, psG]
                vt5 = [None] * 5
                for gi in range(8):
                    wb = wba if gi % 2 == 0 else wbb
                    vw = k.dma("sy", wb[:, 0:4096],
                               wq2g[:, gi * 4096:(gi + 1) * 4096],
                               wait=[("te", wb_free[id(wb)])])
                    for fc in range(5):
                        ob_ = B5[fc]
                        pf = pb_free.get(id(ob_), {"ve": 0, "sc": 0})
                        for cl in range(4):
                            wt = []
                            if fc == 0 and cl == 0:
                                wt += [("ld", vw), ("ve", vR_g[gi])]
                            if gi == 0 and cl == 0:
                                wt += [("ve", pf["ve"]), ("sc", pf["sc"])]
                            vt5[fc] = k.rec("te", lambda e, gi=gi, fc=fc,
                                            cl=cl, ob_=ob_, wb=wb, w=w:
                                            e.matmul(
                                ob_[:, 0:w],
                                wb[:, cl * 1024 + fc * 128:
                                   cl * 1024 + fc * 128 + 128],
                                Rh[:, (gi * 4 + cl) * 512:
                                   (gi * 4 + cl) * 512 + w],
                                start=(gi == 0 and cl == 0),
                                stop=(gi == 7 and cl == 3)),
                                wait=wt or None, inc="te")
                    wb_free[id(wb)] = k.v("te")
                vbb_ = bcast_to(hbc[:, 512:1024],
                                rows[0:1, 2560:2560 + w], w, [("ve", vR)])
                for fc in range(5):
                    vv = res_fc(fc, B5[fc], vt5[fc], vbb_)
                if nx < NW and wins[nx][1] != 2:
                    prep_rp(nx, vpool)
                for j, fc in enumerate((5, 6, 7)):
                    wb = wba if j % 2 == 0 else wbb
                    vw = k.dma("sy", wb[:, 0:4096].rearrange(
                        "p (c t) -> p c t", c=HCH),
                        _drv(wq2s)[:, :, fc * 128:(fc + 1) * 128],
                        wait=[("te", wb_free[id(wb)])])
                    ob_ = (psA, psB, psE)[j]
                    pf = pb_free.get(id(ob_), {"ve": 0, "sc": 0})
                    vt = None
                    for hc in range(HCH):
                        vt = k.rec("te", lambda e, hc=hc, ob_=ob_, wb=wb,
                                   w=w: e.matmul(
                            ob_[:, 0:w], wb[:, hc * 128:(hc + 1) * 128],
                            Rh[:, hc * 512:hc * 512 + w],
                            start=(hc == 0), stop=(hc == HCH - 1)),
                            wait=[("ld", vw), ("ve", pf["ve"]),
                                  ("sc", pf["sc"])] if hc == 0 else None,
                            inc="te")
                    wb_free[id(wb)] = k.v("te")
                    vv = res_fc(fc, ob_, vt, vbb_)
            else:
                vbb_ = bcast_to(hbc[:, 512:1024],
                                rows[0:1, 2560:2560 + w], w, [("ve", vR)])
                if nx < NW and wins[nx][1] != 2:
                    prep_rp(nx, vpool)
                for fc in range(NCH):
                    wb = wba if fc % 2 == 0 else wbb
                    vw = k.dma("sy", wb[:, 0:4096].rearrange(
                        "p (c t) -> p c t", c=HCH),
                        _drv(wq2s)[:, :, fc * 128:(fc + 1) * 128],
                        wait=[("te", wb_free[id(wb)])])
                    ob_ = psA if fc % 2 == 0 else psB
                    pf = pb_free.get(id(ob_), {"ve": 0, "sc": 0})
                    vt = None
                    for hc in range(HCH):
                        vt = k.rec("te", lambda e, hc=hc, ob_=ob_, wb=wb,
                                   w=w: e.matmul(
                            ob_[:, 0:w], wb[:, hc * 128:(hc + 1) * 128],
                            Rh[:, hc * 512:hc * 512 + w],
                            start=(hc == 0), stop=(hc == HCH - 1)),
                            wait=[("ld", vw), ("ve", vR), ("ve", pf["ve"]),
                                  ("sc", pf["sc"])]
                            if hc == 0 else None, inc="te")
                    wb_free[id(wb)] = k.v("te")
                    vv = res_fc(fc, ob_, vt, vbb_)
            v_xfree[wi] = vv
            vv = k.bar("ve", inc="ve")
            vst = k.dma("gp",
                        _drv(x1_dst)[:, :, base + gw0:base + gw0 + w],
                        ar2[:].rearrange("p (c t) -> p c t", c=NCH)
                        [:, :, 0:w], wait=[("ve", vv)], slow=(w == 1))
            st8["x1gd"] = vst
            if do_ln2:
                vy = ln2_window(w)
                vst = k.dma("gp",
                            _drv(y2s)[:, :, base + gw0:base + gw0 + w],
                            y2st[:].rearrange("p (c t) -> p c t", c=NCH)
                            [:, :, 0:w], wait=[("ve", vy)], slow=(w == 1))
                st8["y2gd"] = vst
            if wi + 2 < NW and wins[wi + 2][1] != 2:
                xload(wi + 2)
        return k.v("gd")

    def ln2_window(w):
        vt = stat_sums(ar2, w, onesc, [("ve", k.v("ve"))])
        vv = k.rec("ve", lambda e: e.tensor_scalar_mul(
            rows[0:1, 0:w], psD[0:1, 0:w], 1.0 / D), wait=[("te", vt)],
            inc="ve", drain=True)
        vv = bcast(rows[0:1, 0:w], w, [("ve", vv)])
        for ch in range(NCH):
            k.rec("ve", lambda e, ch=ch: e.tensor_tensor(
                g[:, 4096 + ch * 512:4096 + ch * 512 + w],
                ar2[:, ch * 512:ch * 512 + w], bca[:, 0:w], ALU.subtract),
                wait=[("ve", vv)] if ch == 0 else None, drain=True)
        vv = k.bar("ve", inc="ve")
        for ch in range(NCH):
            k.rec("sc", lambda e, ch=ch: e.activation(
                y2sq[:, ch * 512:ch * 512 + w],
                g[:, 4096 + ch * 512:4096 + ch * 512 + w],
                AF.Square), wait=[("ve", vv)] if ch == 0 else None,
                drain=(ch == 0))
        vs = k.bar("sc", inc="sc")
        vt = stat_sums(y2sq, w, onescb, [("sc", vs)])
        vs = rstd_rows(psD[0:1, 0:w], w, 512, wait=[("te", vt)])
        vv = bcast(rows[0:1, 512:512 + w], w, [("sc", vs)])
        vlast = None
        for ch in range(NCH):
            vlast = k.rec("ve", lambda e, ch=ch: e.tensor_tensor(
                y2st[:, ch * 512:ch * 512 + w],
                g[:, 4096 + ch * 512:4096 + ch * 512 + w],
                bca[:, 0:w], ALU.mult),
                wait=[("ve", vv)] if ch == 0 else None, inc="ve",
                drain=True)
        return vlast

    def table_gen(h, v_iot):
        thc = th2pi[:, h:h + 1]
        k.rec("ve", lambda e: e.tensor_scalar(
            tgf[:], tgr[:], thc, None, ALU.mult),
            wait=[("sc", k.v("sc")), ("te", k.v("te")), ("ld", v_iot)],
            drain=True)
        k.rec("ve", lambda e: e.tensor_scalar(
            tgf2[:], tgf[:], MAGIC, -MAGIC, ALU.add, ALU.add), drain=True)
        vv = k.rec("ve", lambda e: e.tensor_tensor(
            tgf[:], tgf[:], tgf2[:], ALU.subtract), inc="ve", drain=True)
        k.rec("sc", lambda e: e.activation(
            sinf[:], tgf[:], AF.Sin, scale=PI2), wait=[("ve", vv)],
            drain=True)
        k.rec("ve", lambda e: e.tensor_single_scalar(
            tgf2[:], tgf[:], 0.25, ALU.is_ge), drain=True)
        vv = k.rec("ve", lambda e: e.scalar_tensor_tensor(
            tgf2[:], tgf2[:], -1.0, tgf[:], ALU.mult, ALU.add), inc="ve",
            drain=True)
        k.rec("sc", lambda e: e.activation(
            cosf[:], tgf2[:], AF.Sin, scale=PI2, bias=PIH),
            wait=[("ve", vv)], drain=True)
        return k.rec("ve", lambda e, h=h: e.tensor_scalar(
            tribh[:], trib[:], esm1c[:, h:h + 1], None, ALU.mult),
            wait=[("ve", v_const)], inc="ve", drain=True)

    def attention():
        v_y2ld = k.dma("sy", y2[:].rearrange("p (c t) -> p c t", c=NCH),
                       _drv(y2s), wait=[("gd", k.v("gd"))])
        v_iot = k.dma("sy", tgr[:], iot,
                      wait=[("ve", k.v("ve")), ("sc", k.v("sc")),
                            ("te", k.v("te"))])
        v_wqh_free = 0
        st = {"uh": k.v("ve"), "sq_sc": k.v("sc")}
        for h in range(NH):
            vwq = None
            for pi, po in enumerate((0, D, 2 * D, 3 * D)):
                vwq = k.dma("gp",
                            wqh[:].rearrange("p (a c t) -> p a c t", a=4,
                                             c=NCH)[:, pi],
                            _drv(qwT)[:, :, po + h * 128:po + h * 128 + 128],
                            wait=[("te", v_wqh_free)] if pi == 0 else None)
            vmask = table_gen(h, v_iot)
            if vmask > st["uh"]:
                st["uh"] = vmask
            for b in range(NB):
                v_wqh_free = attn_bh(h, b, vwq, v_y2ld, vmask, st)
        if AMODE in ("proj", "state", "nostats"):
            return k.v("sc")
        vs = k.rec("sc", lambda e: e.activation(
            rsqtm[:], psD[:, 0:32], AF.Ln, bias=1e-6, scale=1.0 / D),
            wait=[("te", k.v("te"))], inc="sc", drain=True)
        return k.rec("sc", lambda e: e.activation(
            rsqtm[:], rsqtm[:], AF.Exp, scale=-0.5), inc="sc", drain=True)

    def attn_bh(h, b, vwq, v_y2ld, vmask, st):
        v0 = [("gd", vwq), ("ld", v_y2ld), ("ve", st["uh"]),
              ("sc", st["sq_sc"])]
        for pi, name in enumerate("qkvu"):
            for tb in range(4):
                pb = psA if tb % 2 == 0 else psB
                vt = None
                for ic in range(NCH):
                    wt = None
                    if ic == 0:
                        wt = v0 + [("sc", k.v("sc"))]
                        v0 = []
                    vt = k.rec("te", lambda e, pi=pi, tb=tb, ic=ic, pb=pb:
                               e.matmul(
                        pb[:, 0:512],
                        wqh[:].rearrange("p (a c t) -> p a c t", a=4,
                                         c=NCH)[:, pi, ic],
                        y2[:, ic * T + b * S + tb * 512:
                           ic * T + b * S + (tb + 1) * 512],
                        start=(ic == 0), stop=(ic == NCH - 1)),
                        wait=wt, inc="te")
                bcol = qb[:, pi * 8 + h:pi * 8 + h + 1]
                if name in "qk":
                    dst = silq if name == "q" else silk
                    k.rec("sc", lambda e, dst=dst, tb=tb, pb=pb, bcol=bcol:
                          e.activation(dst[:, tb * 512:(tb + 1) * 512],
                                       pb[:, 0:512], AF.Silu, bias=bcol),
                          wait=[("te", vt)], inc="sc", drain=True)
                elif name == "u":
                    k.rec("sc", lambda e, tb=tb, pb=pb, bcol=bcol:
                          e.activation(uh[:, tb * 512:(tb + 1) * 512],
                                       pb[:, 0:512], AF.Identity,
                                       bias=bcol),
                          wait=[("te", vt)], inc="sc", drain=True)
                else:
                    vs = k.rec("sc", lambda e, tb=tb, pb=pb, bcol=bcol:
                               e.activation(vfm[:, 0:512], pb[:, 0:512],
                                            AF.Identity, bias=bcol),
                               wait=[("te", vt)], inc="sc", drain=True)
                    vt2 = None
                    for jj in range(4):
                        vt2 = k.rec("te", lambda e, jj=jj: e.transpose(
                            psT[:, jj * 128:(jj + 1) * 128],
                            vfm[:, jj * 128:(jj + 1) * 128], identb[:]),
                            wait=[("sc", vs), ("ve", k.v("ve"))]
                            if jj == 0 else None, inc="te")
                    vv = k.rec("ve", lambda e, tb=tb: e.tensor_copy(
                        vtm[:, tb * 512:(tb + 1) * 512], psT[:, 0:512]),
                        wait=[("te", vt2)], inc="ve", drain=True)
        v_projte = k.v("te")
        vs_pr = k.bar("sc", inc="sc")
        for dst, src, tb_ in ((qcb, silq, cosf), (qsb, silq, sinf),
                              (kcb, silk, cosf), (ksb, silk, sinf)):
            k.rec("ve", lambda e, d=dst, s=src, t=tb_: e.tensor_tensor(
                d[:], s[:], t[:], ALU.mult), wait=[("sc", vs_pr)],
                drain=True)
        vv_lrpe = k.bar("ve", inc="ve")
        vv_kt = None
        for half, (srcb, dstb) in enumerate(((kcb, kcT), (ksb, ksT))):
            for grp in range(4):
                vt = None
                for jj in range(4):
                    wt_t = [("ve", k.v("ve"))] if jj == 0 else None
                    vt = k.rec("te", lambda e, srcb=srcb, grp=grp, jj=jj:
                               e.transpose(
                        psT[:, jj * 128:(jj + 1) * 128],
                        srcb[:, grp * 512 + jj * 128:
                             grp * 512 + (jj + 1) * 128], identb[:]),
                        wait=wt_t, inc="te")
                vv_kt = k.rec("ve", lambda e, dstb=dstb, grp=grp:
                              e.tensor_copy(
                    dstb[:, grp * 512:(grp + 1) * 512], psT[:, 0:512]),
                    wait=[("te", vt)], inc="ve", drain=True)
        if AMODE == "proj":
            st["uh"] = k.v("ve")
            st["sq_sc"] = k.v("sc")
            return v_projte
        UB = [psF, psG, psA, psB]
        v_pfx = []
        vtU = []
        vv_a = None

        def bsb_ap(c):
            return bsbA[:, (c - 1) * 256:c * 256] if c <= 6 else bsbB[:]

        def pfx_step(c):
            nonlocal vv_a
            if c == 0:
                k.rec("ve", lambda e: e.tensor_copy(bsbf[:],
                                                    UB[0][:, 0:256]),
                      wait=[("te", vtU[0])], drain=True)
            else:
                k.rec("ve", lambda e, ub=UB[c % 4]: e.tensor_tensor(
                    bsbf[:], bsbf[:], ub[:, 0:256], ALU.add),
                    wait=[("te", vtU[c])], drain=True)
            if c <= 6:
                vv = k.rec("ve", lambda e, dst=bsb_ap(c + 1), h=h:
                           e.tensor_scalar(dst, bsbf[:], esm1c[:, h:h + 1],
                                           None, ALU.mult), inc="ve",
                           drain=True)
                v_pfx.append(vv)
            if c == 7:
                vv_a = k.rec("ve", lambda e: e.tensor_copy(asb[:],
                                                           bsbf[:]),
                             inc="ve", drain=True)

        for c in range(8):
            ub = UB[c % 4]
            wt = None
            if c == 0:
                wt = [("ve", vv_kt)]
            elif c in (2, 3):
                wt = [("sc", vs_pr)]
            elif c >= 4:
                wt = [("ve", v_pfx[c - 4])]
            vt = None
            for half, kT in enumerate((kcT, ksT)):
                for jb in range(2):
                    jc = c * 2 + jb
                    vt = k.rec("te", lambda e, ub=ub, kT=kT,
                               off=half * 128, jc=jc, s0=(jb == 0),
                               s1=(jb == 1): e.matmul(
                        ub[:, off:off + 128],
                        kT[:, jc * 128:(jc + 1) * 128],
                        vtm[:, jc * 128:(jc + 1) * 128],
                        start=s0, stop=s1),
                        wait=wt if (half == 0 and jb == 0) else None,
                        inc="te")
            vtU.append(vt)
            if c >= 2:
                pfx_step(c - 2)
        pfx_step(6)
        pfx_step(7)
        if AMODE == "state":
            st["uh"] = k.v("ve")
            st["sq_sc"] = k.v("sc")
            return v_projte
        vintra = {}
        vmaskv = {}
        vAVt = {}
        vsq = {}
        vstat = {}
        vout = {}
        OU = [psA, psB]
        IC = [psE, psC]

        def pre(c):
            ou = OU[c % 2]
            ic_ = IC[c % 2]
            if c <= 1:
                wt = [("ve", vv_a)]
            else:
                wt = [("ve", vout[c - 2]), ("sc", vsq[c - 2])]
            k.rec("te", lambda e, c=c, ou=ou: e.matmul(
                ou[:, 0:256], asb[:, 0:128], qcb[:, c * 256:(c + 1) * 256],
                start=True, stop=False), wait=wt, inc="te")
            k.rec("te", lambda e, c=c, ou=ou: e.matmul(
                ou[:, 0:256], asb[:, 128:256],
                qsb[:, c * 256:(c + 1) * 256], start=False, stop=False),
                inc="te")
            if c > 0:
                bs = bsb_ap(c)
                k.rec("te", lambda e, c=c, ou=ou, bs=bs: e.matmul(
                    ou[:, 0:256], bs[:, 0:128],
                    qcb[:, c * 256:(c + 1) * 256], start=False,
                    stop=False), wait=[("ve", v_pfx[c - 1])], inc="te")
                k.rec("te", lambda e, c=c, ou=ou, bs=bs: e.matmul(
                    ou[:, 0:256], bs[:, 128:256],
                    qsb[:, c * 256:(c + 1) * 256], start=False,
                    stop=False), inc="te")
            wti = [("ve", vmaskv[c - 2])] if c >= 2 else None
            k.rec("te", lambda e, c=c, ic_=ic_: e.matmul(
                ic_[:, 0:256], kcb[:, c * 256:c * 256 + 128],
                qcb[:, c * 256:(c + 1) * 256], start=True, stop=False),
                wait=wti, inc="te")
            k.rec("te", lambda e, c=c, ic_=ic_: e.matmul(
                ic_[:, 0:256], ksb[:, c * 256:c * 256 + 128],
                qsb[:, c * 256:(c + 1) * 256], start=False, stop=True),
                inc="te")
            k.rec("te", lambda e, c=c, ic_=ic_: e.matmul(
                ic_[:, 256:384], kcb[:, c * 256 + 128:c * 256 + 256],
                qcb[:, c * 256 + 128:(c + 1) * 256], start=True,
                stop=False), inc="te")
            vintra[c] = k.rec("te", lambda e, c=c, ic_=ic_: e.matmul(
                ic_[:, 256:384], ksb[:, c * 256 + 128:c * 256 + 256],
                qsb[:, c * 256 + 128:(c + 1) * 256], start=False,
                stop=True), inc="te")

        def maskev(c):
            ic_ = IC[c % 2]
            fp = fsb[:, (c % 2) * 384:(c % 2) * 384 + 384]
            wt = [("te", vintra[c])]
            if c >= 2:
                wt.append(("te", vAVt[c - 2]))
            k.rec("ve", lambda e, ic_=ic_, fp=fp: e.tensor_tensor(
                fp[:, 0:128], ic_[:, 0:128], tribh[:], ALU.mult),
                wait=wt, drain=True)
            k.rec("ve", lambda e, ic_=ic_, fp=fp, h=h: e.tensor_scalar(
                fp[:, 128:256], ic_[:, 128:256], esm1c[:, h:h + 1], None,
                ALU.mult), drain=True)
            vmaskv[c] = k.rec("ve", lambda e, ic_=ic_, fp=fp:
                              e.tensor_tensor(
                fp[:, 256:384], ic_[:, 256:384], tribh[:], ALU.mult),
                inc="ve", drain=True)

        def av(c):
            ou = OU[c % 2]
            fp = fsb[:, (c % 2) * 384:(c % 2) * 384 + 384]
            k.rec("te", lambda e, c=c, ou=ou, fp=fp: e.matmul(
                ou[:, 128:256], vtm[:, c * 256 + 128:c * 256 + 256],
                fp[:, 256:384], start=False, stop=False),
                wait=[("ve", vmaskv[c])], inc="te")
            vAVt[c] = k.rec("te", lambda e, c=c, ou=ou, fp=fp: e.matmul(
                ou[:, 0:256], vtm[:, c * 256:c * 256 + 128], fp[:, 0:256],
                start=False, stop=True), inc="te")

        def outev(c):
            ou = OU[c % 2]
            pc0 = h * T + b * S + c * 256
            if AMODE == "nostats":
                vsq[c] = k.v("sc")
                vout[c] = k.rec("ve", lambda e, c=c, ou=ou, pc0=pc0:
                                e.tensor_tensor(
                    pat[:, pc0:pc0 + 256], ou[:, 0:256],
                    uh[:, c * 256:(c + 1) * 256], ALU.mult),
                    wait=[("te", vAVt[c])], inc="ve", drain=True)
                return
            wt = [("te", vAVt[c])]
            if c >= 1:
                wt.append(("te", vstat[c - 1]))
            vsq[c] = k.rec("sc", lambda e, c=c, ou=ou: e.activation(
                sqat[:], ou[:, 0:256], AF.Square), wait=wt, inc="sc",
                drain=True)
            vout[c] = k.rec("ve", lambda e, c=c, ou=ou, pc0=pc0:
                            e.tensor_tensor(
                pat[:, pc0:pc0 + 256], ou[:, 0:256],
                uh[:, c * 256:(c + 1) * 256], ALU.mult),
                wait=[("sc", vsq[c])], inc="ve", drain=True)

        def stat(c):
            vt = None
            for jb in range(2):
                col = b * 16 + c * 2 + jb
                vt = k.rec("te", lambda e, c=c, jb=jb, col=col, h=h, b=b:
                           e.matmul(
                    psD[:, col:col + 1],
                    sqat[:, jb * 128:(jb + 1) * 128],
                    onescb[:, 0:1],
                    start=(h == 0 and b == 0 and c == 0 and jb == 0),
                    stop=(h == NH - 1 and b == NB - 1 and c == 7
                          and jb == 1)),
                    wait=[("sc", vsq[c])] if jb == 0 else None, inc="te")
            vstat[c] = vt

        pre(0)
        maskev(0)
        pre(1)
        maskev(1)
        for c in range(8):
            av(c)
            if c >= 1 and AMODE != "nostats":
                stat(c - 1)
            outev(c)
            if c + 2 < 8:
                pre(c + 2)
                maskev(c + 2)
        if AMODE != "nostats":
            stat(7)
        st["uh"] = vout[7]
        st["sq_sc"] = vsq[7]
        return v_projte

    def attn_out(v_rsq):
        k.dma("gp", owb[:].rearrange("p (c t) -> p c t", c=NCH),
              _drv(owT), wait=[("te", k.v("te")), ("ve", k.v("ve")),
                               ("sc", k.v("sc"))])
        v_st = 0
        v_xfree = ("sc", k.v("sc"))
        pb4 = {id(psA): 0, id(psB): 0}
        for tl in range(8):
            b, tb = tl // 4, tl % 4
            base = b * S + tb * 512
            vt2 = None
            for jj in range(4):
                col = b * 16 + tb * 4 + jj
                vt2 = k.rec("te", lambda e, col=col, jj=jj: e.transpose(
                    psG[0:1, jj * 128:jj * 128 + 128],
                    rsqtm[:, col:col + 1], identf[:]),
                    wait=[("sc", v_rsq)] if (tl == 0 and jj == 0) else None,
                    inc="te")
            vv = k.rec("ve", lambda e: e.tensor_copy(
                rows[0:1, 0:512], psG[0:1, 0:512]), wait=[("te", vt2)],
                inc="ve", drain=True)
            vv = bcast(rows[0:1, 0:512], 512, [("ve", vv)])
            for hc in range(NCH):
                k.rec("ve", lambda e, hc=hc, base=base: e.tensor_tensor(
                    zsb[:, hc * 512:(hc + 1) * 512],
                    pat[:, hc * T + base:hc * T + base + 512],
                    bca[:, 0:512], ALU.mult),
                    wait=[("ve", vv)] if hc == 0 else None, drain=True)
            vv = k.bar("ve", inc="ve")
            vx = k.dma("sy", x4a[:].rearrange("p (c t) -> p c t", c=NCH),
                       _drv(xs1)[:, :, base:base + 512], wait=[v_xfree])
            for fc in range(NCH):
                ob_ = psA if fc % 2 == 0 else psB
                vt = None
                for ic in range(NCH):
                    wt4 = None
                    if ic == 0:
                        wt4 = [("sc", pb4[id(ob_)])]
                        if fc == 0:
                            wt4 += [("ve", vv), ("gd", k.v("gd"))]
                    vt = k.rec("te", lambda e, fc=fc, ic=ic, ob_=ob_:
                               e.matmul(
                        ob_[:, 0:512],
                        owb[:, ic * 1024 + fc * 128:ic * 1024 + fc * 128
                            + 128],
                        zsb[:, ic * 512:(ic + 1) * 512],
                        start=(ic == 0), stop=(ic == NCH - 1)),
                        wait=wt4, inc="te")
                vs = k.rec("sc", lambda e, fc=fc, ob_=ob_: e.activation(
                    ar2[:, fc * 512:(fc + 1) * 512], ob_[:, 0:512],
                    AF.Identity, bias=ob[:, fc:fc + 1]),
                    wait=[("te", vt)], inc="sc", drain=True)
                pb4[id(ob_)] = vs
                vv2 = k.rec("ve", lambda e, fc=fc: e.tensor_tensor(
                    ar2[:, fc * 512:(fc + 1) * 512],
                    ar2[:, fc * 512:(fc + 1) * 512],
                    x4a[:, fc * 512:(fc + 1) * 512], ALU.add),
                    wait=[("sc", vs), ("ld", vx)], inc="ve", drain=True)
            v_xfree = ("ve", vv2)
            vv = k.bar("ve", inc="ve")
            vst = k.dma("gp", _drv(xs2)[:, :, base:base + 512],
                        ar2[:].rearrange("p (c t) -> p c t", c=NCH),
                        wait=[("ve", vv)])
            vq = ln_quant(ar2, 512,
                          lambda ch: rp[:, ch * 512:(ch + 1) * 512],
                          [("gd", vst), ("gd", v_st)],
                          xc_wait=[("gd", vst)], sqb=sq4)
            v_st = k.dma("gp", _drv(rq3)[:, :, base:base + 512],
                         rp[:].rearrange("p (c t) -> p c t", c=NCH),
                         wait=[("ve", vq)])
        return v_st

    wt0 = [("gd", v_prep)] if "prep" in phases else []
    last_stage = "none"
    if "ffna" in phases:
        ffn(xT, xs1, "attn" in phases, None, wt0)
        last_stage = "ffna"
    if "attn" in phases:
        v_rsq = attention()
        last_stage = "attn"
    if "attno" in phases:
        attn_out(v_rsq)
        last_stage = "attno"
    if "ffnc" in phases:
        ffn(xs2, y, False, rq3, [("gd", k.v("gd")), ("ve", k.v("ve")),
                                 ("sc", k.v("sc")), ("te", k.v("te"))])
        last_stage = "ffnc"

    if last_stage != "ffnc":
        src = {"ffna": xs1, "attn": xs1, "attno": xs2}.get(last_stage, xT)
        if debug_out is not None:
            src = {"xs1": xs1, "xs2": xs2, "y2s": None, "xT": xT}[debug_out]
        wt = [("gd", k.v("gd")), ("ve", k.v("ve")), ("sc", k.v("sc")),
              ("te", k.v("te")), ("ld", k.v("ld"))]
        for i in range(8):
            vx = k.dma("sy", xa[:].rearrange("p (c t) -> p c t", c=1),
                       _drv(src)[:, i:i + 1, :],
                       wait=wt if i == 0 else [("gd", k.v("gd"))])
            k.dma("gp", _drv(y)[:, i:i + 1, :],
                  xa[:].rearrange("p (c t) -> p c t", c=1),
                  wait=[("ld", vx)])

    fin = [("gd", k.v("gd")), ("ld", k.v("ld")), ("ve", k.v("ve")),
           ("sc", k.v("sc")), ("te", k.v("te"))]
    k.rec("sy", lambda e: e.nop(), wait=fin)

    build._k = k
    with contextlib.ExitStack() as st:
        sems = {}
        for s in ("ld", "gd", "te", "ve", "sc", "gp"):
            sems[s] = st.enter_context(nc.semaphore("sem_" + s))
        blk = st.enter_context(nc.Block())
        blk.sync(lambda e: k.emit(sems, "sy", e))
        blk.tensor(lambda e: k.emit(sems, "te", e))
        blk.vector(lambda e: k.emit(sems, "ve", e))
        blk.scalar(lambda e: k.emit(sems, "sc", e))
        blk.gpsimd(lambda e: k.emit(sems, "gp", e))
    return nc


def _inputs_for_core(ci, x, slope, qw, qbv, ow, obv, theta, w1, cwv, cbv,
                     w2, consts):
    xc = np.ascontiguousarray(
        x[2 * ci:2 * ci + 2].reshape(T, D).T.astype(np.float32))
    d = dict(xT=xc)
    d.update(consts)
    return d


def kernel(L0_attn_x, L0_attn_slope_rate, L0_attn_qkvu_w, L0_attn_qkvu_b,
           L0_attn_out_w, L0_attn_out_b, L0_attn_theta, L0_ffn_w1,
           L0_ffn_conv_w, L0_ffn_conv_b, L0_ffn_w2, _phases=None):
    x = np.asarray(L0_attn_x, dtype=np.float32)
    consts = host_consts(L0_attn_slope_rate, L0_attn_qkvu_w, L0_attn_qkvu_b,
                         L0_attn_out_w, L0_attn_out_b, L0_attn_theta,
                         L0_ffn_w1, L0_ffn_conv_w, L0_ffn_conv_b, L0_ffn_w2)
    nc = build(_phases or ("prep", "ffna", "attn", "attno", "ffnc"))
    in_maps = []
    for ci in range(8):
        xc = np.ascontiguousarray(
            x[2 * ci:2 * ci + 2].reshape(T, D).T.astype(np.float32))
        m = dict(xT=xc)
        m.update(consts)
        in_maps.append(m)
    res = run_bass_kernel_spmd(nc, in_maps, core_ids=list(range(8)))
    outs = []
    for r in res.results:
        yT = np.asarray(r["y"])
        outs.append(yT.reshape(D, NB, S).transpose(1, 2, 0))
    return np.concatenate(outs, axis=0).astype(np.float32)


def host_consts(slope, qw, qbv, ow, obv, theta, w1, cwv, cbv, w2):
    w1T = np.ascontiguousarray(np.asarray(w1, np.float32).T)
    w2T = np.ascontiguousarray(np.asarray(w2, np.float32).T)
    qwT = np.ascontiguousarray(np.asarray(qw, np.float32).T)
    owT = np.ascontiguousarray(np.asarray(ow, np.float32).T)
    qbT = np.ascontiguousarray(
        np.asarray(qbv, np.float32).reshape(32, 128).T)
    obT = np.ascontiguousarray(
        np.asarray(obv, np.float32).reshape(8, 128).T)
    cbT = np.ascontiguousarray(
        np.asarray(cbv, np.float32).reshape(32, 128).T)
    cwT = np.ascontiguousarray(
        np.asarray(cwv, np.float32).reshape(32, 128, 3)
        .transpose(1, 0, 2).reshape(128, 96))
    thT = np.ascontiguousarray(
        np.asarray(theta, np.float32).reshape(8, 128).T)
    slp = np.ascontiguousarray(
        np.asarray(slope, np.float32).reshape(1, 8))
    idn = np.eye(128, dtype=np.float32)
    tri = np.triu(np.ones((128, 128), np.float32))
    iot = np.broadcast_to(np.arange(2048, dtype=np.float32),
                          (128, 2048)).copy()
    return dict(w1T=w1T, w2T=w2T, qwT=qwT, owT=owT, qbT=qbT, obT=obT,
                cbT=cbT, cwT=cwT, thT=thT, slp=slp, idn=idn, tri=tri,
                iot=iot)


# version 15
# speedup vs baseline: 1.1817x; 1.0224x over previous
import contextlib
import numpy as np
import ml_dtypes
import concourse.bass as bass
import concourse.mybir as mybir
from concourse.bass_utils import run_bass_kernel_spmd

F32 = mybir.dt.float32
BF16 = mybir.dt.bfloat16
I32 = mybir.dt.int32
AF = mybir.ActivationFunctionType
ALU = mybir.AluOpType
AX = mybir.AxisListType
BF = ml_dtypes.bfloat16

NB = 2
S = 2048
T = NB * S
D = 1024
NCH = 8
HID = 4096
HCH = 32
NH = 8
TT = 512
CS = 256
AMODE = "full"
LN127 = float(np.log(np.float64(127.0)))
PI2 = float(2.0 * np.pi)
PIH = float(np.pi / 2.0)
MAGIC = float(3 << 22)
ENGS = ("sy", "te", "ve", "sc", "gp")
KB_ = 1024


class K:

    SELF = {"te": "te", "ve": "ve", "sc": "sc", "gp": "gp"}

    def __init__(self, nc):
        self.nc = nc
        self.recs = {e: [] for e in ENGS}
        self.cnt = {k: 0 for k in ("ld", "gd", "te", "ve", "sc", "gp")}
        self.pend = {e: [] for e in ENGS}

    def rec(self, eng, fn, wait=None, inc=None, drain=False):
        if fn is None:
            return self.bar(eng, wait=wait, inc=inc)
        sem = self.SELF.get(eng)
        w = self.pend[eng] + list(wait or ())
        self.pend[eng] = []
        if sem is None:
            self.recs[eng].append((fn, tuple(w), None))
            return self.cnt.get(eng)
        if drain and self.cnt[sem] > 0:
            w.append((sem, self.cnt[sem]))
        self.cnt[sem] += 1
        out = self.cnt[sem]
        self.recs[eng].append((fn, tuple(w), sem))
        return out

    def bar(self, eng, wait=None, inc=None):
        if wait:
            self.pend[eng].extend(wait)
        return self.cnt.get(self.SELF.get(eng, eng), 0)

    def dma(self, eng, dst, src, wait=None, slow=False):
        sem = "ld" if eng == "sy" else "gd"
        prev = self.cnt[sem]
        self.cnt[sem] += 16
        out = self.cnt[sem]
        w = self.pend[eng] + list(wait or ())
        self.pend[eng] = []
        if prev > 0:
            w.append((sem, prev))
        self.recs[eng].append(
            (lambda e, d=dst, s=src, sl=slow: e.dma_start(
                d, s, allow_slow_non_contiguous=sl),
             tuple(w), sem))
        return out

    def v(self, sem):
        return self.cnt[sem]

    def emit(self, sems, eng, e):
        for fn, waits, inc in self.recs[eng]:
            dedup = {}
            for s, val in waits:
                if val > dedup.get(s, -1):
                    dedup[s] = val
            for s, val in dedup.items():
                if val > 0:
                    e.wait_ge(sems[s], val)
            inst = fn(e)
            if inc is not None:
                inst.then_inc(sems[inc], 16 if inc in ("ld", "gd") else 1)


def _drv(ap):
    return ap.rearrange("(c p) t -> p c t", p=128)


def build(phases=("prep", "ffna", "attn", "attno", "ffnc"), debug_out=None,
          taps=False):
    nc = bass.Bass("TRN2", target_bir_lowering=False)
    k = K(nc)
    k.taps = taps
    k.tapped = {}

    xT = nc.dram_tensor("xT", [D, T], F32, kind="ExternalInput").ap()
    w1T = nc.dram_tensor("w1T", [D, HID], F32, kind="ExternalInput").ap()
    w2T = nc.dram_tensor("w2T", [HID, D], F32, kind="ExternalInput").ap()
    qwT = nc.dram_tensor("qwT", [D, 4 * D], F32, kind="ExternalInput").ap()
    owT = nc.dram_tensor("owT", [D, D], F32, kind="ExternalInput").ap()
    qbT = nc.dram_tensor("qbT", [128, 32], F32, kind="ExternalInput").ap()
    obT = nc.dram_tensor("obT", [128, 8], F32, kind="ExternalInput").ap()
    cbT = nc.dram_tensor("cbT", [128, 32], F32, kind="ExternalInput").ap()
    cwT = nc.dram_tensor("cwT", [128, 96], F32, kind="ExternalInput").ap()
    thT = nc.dram_tensor("thT", [128, 8], F32, kind="ExternalInput").ap()
    slp = nc.dram_tensor("slp", [1, 8], F32, kind="ExternalInput").ap()
    idn = nc.dram_tensor("idn", [128, 128], F32, kind="ExternalInput").ap()
    tri = nc.dram_tensor("tri", [128, 128], F32, kind="ExternalInput").ap()
    iot = nc.dram_tensor("iot", [128, S], F32, kind="ExternalInput").ap()

    y = nc.dram_tensor("y", [D, T], F32, kind="ExternalOutput").ap()
    if taps:
        d_rp = nc.dram_tensor("d_rp", [128, 4096], BF16,
                              kind="ExternalOutput").ap()
        d_g = nc.dram_tensor("d_g", [128, HCH * 512], F32,
                             kind="ExternalOutput").ap()
        d_rh = nc.dram_tensor("d_rh", [128, HCH * 512], BF16,
                              kind="ExternalOutput").ap()
        d_ar2 = nc.dram_tensor("d_ar2", [128, 4096], F32,
                               kind="ExternalOutput").ap()
        d_rows = nc.dram_tensor("d_rows", [1, 4096], F32,
                                kind="ExternalOutput").ap()
    xs1 = nc.dram_tensor("xs1", [D, T], F32, kind="Internal").ap()
    xs2 = nc.dram_tensor("xs2", [D, T], F32, kind="Internal").ap()
    y2s = nc.dram_tensor("y2s", [D, T], BF16, kind="Internal").ap()
    rq3 = nc.dram_tensor("rq3", [D, T], BF16, kind="Internal").ap()
    wq1s = nc.dram_tensor("wq1s", [D, HID], BF16, kind="Internal").ap()
    wq2s = nc.dram_tensor("wq2s", [HID, D], BF16, kind="Internal").ap()
    wq2g = nc.dram_tensor("wq2g", [128, 32768], BF16, kind="Internal").ap()

    SBASE = 16384
    A = lambda n, sh, dt, off: nc.alloc_sbuf_tensor_at(
        n, sh, dt, offset=off + SBASE)
    identf = A("identf", [128, 128], F32, 0)
    identb = A("identb", [128, 128], BF16, 512)
    trib = A("trib", [128, 128], F32, 768)
    tribh = A("tribh", [128, 128], F32, 1280)
    qb = A("qb", [128, 32], F32, 1792)
    ob = A("ob", [128, 8], F32, 1920)
    cb = A("cb", [128, 32], F32, 1952)
    cw = A("cw", [128, 96], F32, 2080)
    thetaT = A("thetaT", [128, 8], F32, 2464)
    th2pi = A("th2pi", [128, 8], F32, 2496)
    esm1c = A("esm1c", [128, 8], F32, 2528)
    onesc = A("onesc", [128, 1], F32, 2560)
    onescb = A("onescb", [128, 1], BF16, 2592)
    onesr = A("onesr", [1, 128], F32, 2624)
    slpr = A("slpr", [1, 8], F32, 4192)
    esm1r = A("esm1r", [1, 8], F32, 4224)
    wsum = A("wsum", [128, 1], F32, 4256)
    scol = A("scol", [128, 1], F32, 4288)
    mcol = A("mcol", [128, 1], F32, 4320)
    srow = A("srow", [1, 8], F32, 4352)
    hedge = A("hedge", [128, 64], F32, 4384)
    rsqtm = A("rsqtm", [128, 32], F32, 4640)
    ceps5 = A("ceps5", [128, 1], F32, 4768)
    ceps6 = A("ceps6", [128, 1], F32, 4800)
    cl127 = A("cl127", [128, 1], F32, 4832)
    zcol = A("zcol", [128, 1], F32, 4864)
    zcolb = A("zcolb", [128, 1], BF16, 4896)
    cpih = A("cpih", [128, 1], F32, 4928)
    xedge = A("xedge", [128, 32], F32, 4960)
    nc.const_aps.register(F32, PIH, cpih[:, 0:1])
    nc.const_aps.register(F32, 1e-5, ceps5[:, 0:1])
    nc.const_aps.register(F32, 1e-6, ceps6[:, 0:1])
    nc.const_aps.register(F32, LN127, cl127[:, 0:1])
    nc.const_aps.register(F32, 0.0, zcol[:, 0:1])
    nc.const_aps.register(F32, 1.0, onesc[:, 0:1])
    nc.const_aps.register(BF16, 0.0, zcolb[:, 0:1])
    nc.const_aps.register(BF16, 1.0, onescb[:, 0:1])
    rows = A("rows", [1, 3072], F32, 8 * KB_)
    hbc = A("hbc", [128, 1024], F32, 20 * KB_)
    bca = A("bca", [128, 1536], F32, 24 * KB_)
    xa = A("xa", [128, 4096], F32, 30 * KB_)
    xb = A("xb", [128, 4096], F32, 46 * KB_)
    wba = A("wba", [128, 4096], BF16, 62 * KB_)
    wbb = A("wbb", [128, 4096], BF16, 70 * KB_)
    rp = A("rp", [128, 4096], BF16, 78 * KB_)
    Rh = A("Rh", [128, HCH * 512], BF16, 86 * KB_)
    y2st = A("y2st", [128, 4096], BF16, 86 * KB_)
    y2sq = A("y2sq", [128, 4096], BF16, 94 * KB_)
    sq = A("sq", [128, 4096], BF16, 118 * KB_)
    g = A("g", [128, HCH * 512], F32, 126 * KB_)
    ar2 = A("ar2", [128, 4096], F32, 190 * KB_)
    wfull = A("wfull", [128, 32768], F32, 30 * KB_)
    p1xs = A("p1xs", [128, 4096], F32, 158 * KB_)
    p1wq = A("p1wq", [128, 4096], BF16, 8 * KB_)

    tgr = A("tgr", [128, S], F32, 8 * KB_)
    sinf = A("sinf", [128, S], BF16, 16 * KB_)
    cosf = A("cosf", [128, S], BF16, 20 * KB_)
    tgf = A("tgf", [128, S], F32, 158 * KB_)
    tgf2 = A("tgf2", [128, S], F32, 166 * KB_)
    asb = A("asb", [128, 256], BF16, 26752)
    bsbB = A("bsbB", [128, 256], BF16, 27264)
    bsbf = A("bsbf", [128, 256], F32, 27776)
    fsb = A("fsb", [128, 768], BF16, 28800)
    bsbA = A("bsbA", [128, 1536], BF16, 5088)
    sqat = A("sqat", [128, 256], BF16, 211968)
    y2 = A("y2", [128, NCH * T], BF16, 30 * KB_)
    pat = A("pat", [128, NH * T], BF16, 94 * KB_)
    silq = A("silq", [128, S], BF16, 158 * KB_)
    silk = A("silk", [128, S], BF16, 162 * KB_)
    qcb = A("qcb", [128, S], BF16, 166 * KB_)
    qsb = A("qsb", [128, S], BF16, 170 * KB_)
    kcb = A("kcb", [128, S], BF16, 174 * KB_)
    ksb = A("ksb", [128, S], BF16, 178 * KB_)
    kcT = A("kcT", [128, S], BF16, 182 * KB_)
    ksT = A("ksT", [128, S], BF16, 186 * KB_)
    vtm = A("vtm", [128, S], BF16, 190 * KB_)
    uh = A("uh", [128, S], BF16, 194 * KB_)
    wqh = A("wqh", [128, 4096], BF16, 198 * KB_)
    vfm = A("vfm", [128, 512], BF16, 206 * KB_)

    zsb = A("zsb", [128, 4096], BF16, 30 * KB_)
    owb = A("owb", [128, 8192], BF16, 38 * KB_)
    x4a = A("x4a", [128, 4096], F32, 54 * KB_)
    sq4 = A("sq4", [128, 4096], BF16, 70 * KB_)

    psA = nc.alloc_psum_tensor("psA", [128, 512], F32)
    psB = nc.alloc_psum_tensor("psB", [128, 512], F32)
    psC = nc.alloc_psum_tensor("psC", [128, 512], F32)
    psD = nc.alloc_psum_tensor("psD", [128, 512], F32)
    psE = nc.alloc_psum_tensor("psE", [128, 512], F32)
    psF = nc.alloc_psum_tensor("psF", [128, 512], F32)
    psG = nc.alloc_psum_tensor("psG", [128, 512], F32)
    psT = nc.alloc_psum_tensor("psT", [128, 1024], BF16)

    for dst, src in [(identf, idn), (trib, tri), (qb, qbT), (ob, obT),
                     (cb, cbT), (cw, cwT), (thetaT, thT),
                     (slpr, slp)]:
        k.dma("sy", dst[:], src)
    vld0 = k.v("ld")
    k.rec("ve", lambda e: e.memset(onesc[:], 1.0))
    k.rec("ve", lambda e: e.memset(ceps5[:], 1e-5))
    k.rec("ve", lambda e: e.memset(zcol[:], 0.0))
    k.rec("ve", lambda e: e.memset(zcolb[:], 0.0))
    k.rec("ve", lambda e: e.memset(ceps6[:], 1e-6))
    k.rec("ve", lambda e: e.memset(cl127[:], LN127))
    k.rec("ve", lambda e: e.memset(onescb[:], 1.0))
    k.rec("ve", lambda e: e.memset(onesr[:], 1.0))
    k.rec("ve", lambda e: e.memset(cpih[:], PIH))
    k.rec("ve", lambda e: e.memset(hedge[:], 0.0))
    k.rec("ve", lambda e: e.tensor_scalar_mul(th2pi[:], thetaT[:], 1.0 / PI2),
          wait=[("ld", vld0)], drain=True)
    k.rec("ve", lambda e: e.tensor_copy(identb[:], identf[:]), drain=True)
    v_ms = k.bar("ve", inc="ve")
    k.rec("sc", lambda e: e.activation(esm1r[:], slpr[:], AF.Exp),
          wait=[("ld", vld0), ("ve", v_ms)])
    vs = k.bar("sc", inc="sc")
    vv = k.rec("ve", lambda e: e.tensor_scalar_add(esm1r[:], esm1r[:], -1.0),
               wait=[("sc", vs)], inc="ve", drain=True)
    vt = k.rec("te", lambda e: e.matmul(psD[0:128, 0:8], onesr[0:1, :],
                                        esm1r[0:1, :], start=True, stop=True),
               wait=[("ve", vv)], inc="te")
    v_const = k.rec("ve", lambda e: e.tensor_copy(esm1c[:], psD[:, 0:8]),
                    wait=[("te", vt)], inc="ve", drain=True)

    def wprep(wsrc, wdst, nelem, gdst=None):
        nsl = nelem // (128 * 4096)
        wfv = wfull[:, 0:nsl * 4096]
        vld = k.dma("sy",
                    wfv.rearrange("p (c t) -> p c t",
                                  c=_drv(wsrc).shape[1]),
                    _drv(wsrc),
                    wait=[("ve", k.v("ve")), ("gd", k.v("gd"))])
        vv = k.rec("ve", lambda e: e.reduce_sum(
            wsum[:], wfv, AX.X, apply_absolute_value=True),
            wait=[("ld", vld)], inc="ve", drain=True)
        vt = k.rec("te", lambda e: e.matmul(
            psD[0:1, 0:1], wsum[:, 0:1], onesc[:, 0:1], start=True,
            stop=True), wait=[("ve", vv)], inc="te")
        k.rec("ve", lambda e, n=nelem: e.tensor_scalar(
            srow[0:1, 0:1], psD[0:1, 0:1], 1.0 / n, 1e-5, ALU.mult,
            ALU.max), wait=[("te", vt)], drain=True)
        vv = k.rec("ve", lambda e: e.reciprocal(srow[0:1, 1:2],
                                                srow[0:1, 0:1]),
                   inc="ve", drain=True)
        vt = k.rec("te", lambda e: e.matmul(
            psD[0:128, 4:6], onesr[0:1, :], srow[0:1, 0:2], start=True,
            stop=True), wait=[("ve", vv)], inc="te")
        k.rec("ve", lambda e: e.tensor_copy(mcol[:, 0:1], psD[:, 4:5]),
              wait=[("te", vt)], drain=True)
        k.rec("ve", lambda e: e.tensor_copy(scol[:, 0:1], psD[:, 5:6]),
              drain=True)
        vst = 0
        for i in range(nsl):
            sl = wfull[:, i * 4096:(i + 1) * 4096]
            k.rec("ve", lambda e, sl=sl: e.tensor_scalar(
                p1xs[:], sl, scol[:, 0:1], MAGIC, ALU.mult, ALU.add),
                wait=[("gd", vst)] if vst else None, drain=True)
            k.rec("ve", lambda e: e.tensor_scalar(
                p1xs[:], p1xs[:], -MAGIC, 1.0, ALU.add, ALU.min), drain=True)
            vv = k.rec("ve", lambda e: e.tensor_scalar(
                p1wq[:], p1xs[:], -1.0, mcol[:, 0:1], ALU.max, ALU.mult),
                inc="ve", drain=True)
            ncs = _drv(wdst).shape[1] * 4096 * 128 // nelem
            vst = k.dma("gp",
                        _drv(wdst)[:, i * ncs:(i + 1) * ncs, :],
                        p1wq[:].rearrange("p (c t) -> p c t", c=ncs),
                        wait=[("ve", vv)])
            if gdst is not None:
                vst = k.dma("gp", gdst[:, i * 4096:(i + 1) * 4096],
                            p1wq[:])
        return vst

    if "prep" in phases:
        wprep(w1T, wq1s, D * HID)
        v_prep = wprep(w2T, wq2s, HID * D, gdst=wq2g)
    else:
        v_prep = 0

    def bcast(row_ap, w, wait):
        vt = k.rec("te", lambda e: e.matmul(
            psC[0:128, 0:w], onesr[0:1, :], row_ap, start=True, stop=True),
            wait=wait, inc="te")
        return k.rec("ve", lambda e: e.tensor_copy(bca[:, 0:w], psC[:, 0:w]),
                     wait=[("te", vt)], inc="ve", drain=True)

    def bcast2(row_ap, w, wait):
        vt = k.rec("te", lambda e: e.matmul(
            psC[0:128, 0:w], onesr[0:1, :], row_ap, start=True, stop=True),
            wait=wait, inc="te")
        return k.rec("ve", lambda e: e.tensor_copy(
            bca[:, 512:512 + w], psC[:, 0:w]), wait=[("te", vt)], inc="ve",
            drain=True)

    def bcast_to(dst, row_ap, w, wait):
        vt = k.rec("te", lambda e: e.matmul(
            psC[0:128, 0:w], onesr[0:1, :], row_ap, start=True, stop=True),
            wait=wait, inc="te")
        return k.rec("ve", lambda e, dst=dst: e.tensor_copy(
            dst[:, 0:w], psC[:, 0:w]), wait=[("te", vt)], inc="ve",
            drain=True)

    def cross_max(src01, w, row_dst, wait):
        nblk = (w + 127) // 128
        vts = []
        for j in range(nblk):
            wj = min(128, w - j * 128)
            vt = k.rec("te", lambda e, j=j, wj=wj: e.transpose(
                psF[0:wj, j * 128:j * 128 + 128],
                src01[:, j * 128:j * 128 + wj], identf[:]),
                wait=wait if j == 0 else None, inc="te")
            vts.append((j, wj, vt))
        v_all_tr = vts[-1][2]
        if w == 1:
            return k.rec("ve", lambda e: e.reduce_max(
                row_dst, psF[0:1, 0:128], AX.X, apply_absolute_value=True),
                wait=[("te", v_all_tr)], inc="ve", drain=True)
        vvs = []
        for j, wj, vt in vts:
            vv = k.rec("ve", lambda e, j=j, wj=wj: e.reduce_max(
                bca[0:wj, 1024 + j:1025 + j],
                psF[0:wj, j * 128:j * 128 + 128], AX.X,
                apply_absolute_value=True),
                wait=[("te", v_all_tr)] if j == 0 else None,
                inc="ve", drain=True)
            vvs.append((j, wj, vv))
        vt2 = None
        for j, wj, vv in vvs:
            vt2 = k.rec("te", lambda e, j=j, wj=wj: e.transpose(
                psG[0:1, j * 128:j * 128 + wj],
                bca[0:wj, 1024 + j:1025 + j], identf[0:wj, 0:wj]),
                wait=[("ve", vv)], inc="te")
        return k.rec("ve", lambda e, w=w: e.tensor_copy(
            row_dst, psG[0:1, 0:w]), wait=[("te", vt2)], inc="ve",
            drain=True)

    def rstd_rows(sum_ps_w, w, dst_off, wait=None):
        vv = k.rec("ve", lambda e: e.tensor_scalar_mul(
            rows[0:1, dst_off:dst_off + w], sum_ps_w, 1.0 / D),
            wait=wait, inc="ve", drain=True)
        k.rec("sc", lambda e: e.activation(
            rows[0:1, dst_off:dst_off + w], rows[0:1, dst_off:dst_off + w],
            AF.Ln, bias=1e-5), wait=[("ve", vv)], drain=True)
        return k.rec("sc", lambda e: e.activation(
            rows[0:1, dst_off:dst_off + w], rows[0:1, dst_off:dst_off + w],
            AF.Exp, scale=-0.5), inc="sc", drain=True)

    def stat_sums(src, w, ones, wait, stop_bank=None):
        vt = None
        for ch in range(NCH):
            vt = k.rec("te", lambda e, ch=ch: e.matmul(
                psD[0:1, 0:w], ones[:, 0:1],
                src[:, ch * 512:ch * 512 + w],
                start=(ch == 0), stop=(ch == NCH - 1)),
                wait=wait if ch == 0 else None, inc="te")
        return vt

    def round_chunk(dst, src, w, bc_rs_off, bc_a_off, wait, scale_first,
                    alpha_mult):
        if scale_first:
            k.rec("ve", lambda e: e.tensor_tensor(
                src, src, bca[:, bc_rs_off:bc_rs_off + w], ALU.mult),
                wait=wait, drain=True)
            wait = None
        if alpha_mult:
            k.rec("ve", lambda e: e.tensor_scalar(
                src, src, MAGIC, -MAGIC, ALU.add, ALU.add), wait=wait,
                drain=True)
            return k.rec("ve", lambda e: e.tensor_tensor(
                dst, src, bca[:, bc_a_off:bc_a_off + w], ALU.mult),
                inc="ve", drain=True)
        return k.rec("ve", lambda e: e.tensor_scalar(
            dst, src, MAGIC, -MAGIC, ALU.add, ALU.add), wait=wait,
            inc="ve", drain=True)

    def ln_quant(src, w, rdst, wait, xc_wait=None, sqb=None, xcb=None):
        xcb = ar2 if xcb is None else xcb
        vt = stat_sums(src, w, onesc,
                       list(wait) + [("ve", k.v("ve")), ("sc", k.v("sc"))])
        vv = k.rec("ve", lambda e: e.tensor_scalar_mul(
            rows[0:1, 0:w], psD[0:1, 0:w], 1.0 / D), wait=[("te", vt)],
            inc="ve", drain=True)
        vv = bcast(rows[0:1, 0:w], w, [("ve", vv)])
        for ch in range(NCH):
            wtx = ([("ve", vv)] + list(xc_wait or ())) if ch == 0 else None
            k.rec("ve", lambda e, ch=ch: e.tensor_tensor(
                xcb[:, ch * 512:ch * 512 + w],
                src[:, ch * 512:ch * 512 + w], bca[:, 0:w], ALU.subtract),
                wait=wtx, drain=True)
        vv = k.bar("ve", inc="ve")
        sqb = sq if sqb is None else sqb
        for ch in range(NCH):
            k.rec("sc", lambda e, ch=ch, sqb=sqb: e.activation(
                sqb[:, ch * 512:ch * 512 + w],
                xcb[:, ch * 512:ch * 512 + w], AF.Square),
                wait=[("ve", vv), ("te", k.v("te"))] if ch == 0 else None,
                drain=(ch == 0))
        vs = k.bar("sc", inc="sc")
        vt = stat_sums(sqb, w, onescb, [("sc", vs)])
        vs = rstd_rows(psD[0:1, 0:w], w, 512, wait=[("te", vt)])
        xcv = xcb[:, 0:NCH * 512].rearrange(
            "p (c t) -> p t c", c=NCH)[:, 0:w, :]
        vv = k.rec("ve", lambda e: e.reduce_max(
            bca[:, 512:512 + w], xcv, AX.X, apply_absolute_value=True),
            inc="ve", drain=True)
        vv = cross_max(bca[:, 512:], w, rows[0:1, 1024:1024 + w],
                       [("ve", vv)])
        k.rec("ve", lambda e: e.tensor_tensor(
            rows[0:1, 1024:1024 + w], rows[0:1, 1024:1024 + w],
            rows[0:1, 512:512 + w], ALU.mult), wait=[("sc", vs)],
            drain=True)
        k.rec("ve", lambda e: e.tensor_single_scalar(
            rows[0:1, 1024:1024 + w], rows[0:1, 1024:1024 + w], 1e-5,
            ALU.max), drain=True)
        vv = k.rec("ve", lambda e: e.tensor_scalar_mul(
            rows[0:1, 2048:2048 + w], rows[0:1, 1024:1024 + w],
            1.0 / 127.0), inc="ve", drain=True)
        k.rec("sc", lambda e: e.activation(
            rows[0:1, 1024:1024 + w], rows[0:1, 1024:1024 + w], AF.Ln),
            wait=[("ve", vv)], drain=True)
        vs = k.rec("sc", lambda e: e.activation(
            rows[0:1, 1024:1024 + w], rows[0:1, 1024:1024 + w], AF.Exp,
            scale=-1.0, bias=LN127), inc="sc", drain=True)
        vv = k.rec("ve", lambda e: e.tensor_tensor(
            rows[0:1, 1536:1536 + w], rows[0:1, 512:512 + w],
            rows[0:1, 1024:1024 + w], ALU.mult), wait=[("sc", vs)],
            inc="ve", drain=True)
        vv = bcast(rows[0:1, 1536:1536 + w], w, [("ve", vv)])
        vv = bcast2(rows[0:1, 2048:2048 + w], w, [("ve", vv)])
        vlast = None
        for ch in range(NCH):
            vlast = round_chunk(
                rdst(ch), xcb[:, ch * 512:ch * 512 + w], w, 0, 512,
                [("ve", vv)] if ch == 0 else None, True, True)
        return vlast

    def conv_group(cg, banks, vts, isfirst, w, extra_wait):
        scr = lambda cl, a, b: ar2[:, cl * 512 + a: cl * 512 + b]
        vs1 = None
        for cl in range(4):
            c = cg * 4 + cl
            hb = banks[cl]
            cw2 = cw[:, c * 3 + 2:c * 3 + 3]
            cbc = cb[:, c:c + 1]
            wt = [("te", vts[cl])]
            if cl == 0:
                wt += extra_wait
            if isfirst:
                vs1 = k.rec("sc", lambda e, cl=cl, hb=hb, cw2=cw2, cbc=cbc:
                            e.activation(scr(cl, 0, 511), hb[:, 1:512],
                                         AF.Identity, bias=cbc, scale=cw2),
                            wait=wt, inc="sc", drain=(cl == 0))
            else:
                vs1 = k.rec("sc", lambda e, cl=cl, hb=hb, cw2=cw2, cbc=cbc:
                            e.activation(scr(cl, 0, 512), hb[:, 0:512],
                                         AF.Identity, bias=cbc, scale=cw2),
                            wait=wt, inc="sc", drain=(cl == 0))
        for cl in range(4):
            c = cg * 4 + cl
            hb = banks[cl]
            cw1 = cw[:, c * 3 + 1:c * 3 + 2]
            he1 = hedge[:, c * 2 + 1:c * 2 + 2]
            wt = [("sc", vs1)] if cl == 0 else None
            if isfirst:
                k.rec("ve", lambda e, cl=cl, hb=hb, cw1=cw1:
                      e.scalar_tensor_tensor(
                    scr(cl, 0, 511), hb[:, 0:511], cw1, scr(cl, 0, 511),
                    ALU.mult, ALU.add), wait=wt, drain=(cl == 0))
            else:
                k.rec("ve", lambda e, cl=cl, hb=hb, cw1=cw1:
                      e.scalar_tensor_tensor(
                    scr(cl, 1, 512), hb[:, 0:511], cw1, scr(cl, 1, 512),
                    ALU.mult, ALU.add), wait=wt, drain=(cl == 0))
                k.rec("ve", lambda e, cl=cl, cw1=cw1, he1=he1:
                      e.scalar_tensor_tensor(
                    scr(cl, 0, 1), he1, cw1, scr(cl, 0, 1),
                    ALU.mult, ALU.add))
        for cl in range(4):
            c = cg * 4 + cl
            hb = banks[cl]
            cw0 = cw[:, c * 3 + 0:c * 3 + 1]
            he0 = hedge[:, c * 2:c * 2 + 1]
            he1 = hedge[:, c * 2 + 1:c * 2 + 2]
            gv = lambda a, b, c=c: g[:, c * 512 + a:c * 512 + b]
            if isfirst:
                k.rec("ve", lambda e, cl=cl, hb=hb, cw0=cw0, gv=gv:
                      e.scalar_tensor_tensor(
                    gv(1, 511), hb[:, 0:510], cw0, scr(cl, 1, 511),
                    ALU.mult, ALU.add), drain=(cl == 0))
                k.rec("ve", lambda e, cl=cl, gv=gv: e.tensor_copy(
                    gv(0, 1), scr(cl, 0, 1)))
            else:
                k.rec("ve", lambda e, cl=cl, hb=hb, cw0=cw0, gv=gv:
                      e.scalar_tensor_tensor(
                    gv(2, 512), hb[:, 0:510], cw0, scr(cl, 2, 512),
                    ALU.mult, ALU.add), drain=(cl == 0))
                k.rec("ve", lambda e, cl=cl, cw0=cw0, he0=he0, gv=gv:
                      e.scalar_tensor_tensor(
                    gv(0, 1), he0, cw0, scr(cl, 0, 1), ALU.mult, ALU.add))
                k.rec("ve", lambda e, cl=cl, cw0=cw0, he1=he1, gv=gv:
                      e.scalar_tensor_tensor(
                    gv(1, 2), he1, cw0, scr(cl, 1, 2), ALU.mult, ALU.add))
        vcar = None
        for cl in range(4):
            c = cg * 4 + cl
            hb = banks[cl]
            vcar = k.rec("ve", lambda e, c=c, hb=hb: e.tensor_copy(
                hedge[:, c * 2:c * 2 + 2], hb[:, 510:512]), inc="ve",
                drain=(cl == 0))
        vsil = None
        ncol = 511 if isfirst else 512
        for cl in range(4):
            c = cg * 4 + cl
            vsil = k.rec("sc", lambda e, c=c, ncol=ncol: e.activation(
                g[:, c * 512:c * 512 + ncol], g[:, c * 512:c * 512 + ncol],
                AF.Silu), wait=[("ve", vcar)] if cl == 0 else None,
                inc="sc", drain=(cl == 0))
        return vcar, vsil

    def round_hidden_grouped(w, v_rows, rh_gd=0):
        vR = None
        vpool = None
        vR_list = []
        for gi in range(8):
            h0 = gi * 4
            for cl in range(4):
                hc = h0 + cl
                vpool = k.rec("gp", lambda e, hc=hc, w=w: e.tensor_tensor(
                    g[:, hc * 512:hc * 512 + w],
                    g[:, hc * 512:hc * 512 + w], hbc[:, 0:w], ALU.mult),
                    wait=([("ve", v_rows), ("sc", k.v("sc"))]
                          if (gi == 0 and cl == 0) else None),
                    inc="gp", drain=(cl == 0))
            for cl in range(4):
                hc = h0 + cl
                wtr = None
                if cl == 0:
                    wtr = [("gp", vpool)]
                    if gi == 0 and rh_gd:
                        wtr.append(("gd", rh_gd))
                vR = k.rec("ve", lambda e, hc=hc, w=w: e.tensor_scalar(
                    Rh[:, hc * 512:hc * 512 + w],
                    g[:, hc * 512:hc * 512 + w], MAGIC, -MAGIC, ALU.add,
                    ALU.add), wait=wtr, inc="ve", drain=(cl == 0))
            vR_list.append(vR)
        return vR_list, vpool

    def conv_chunk(c, hb, v_te, isfirst, w):
        gv = lambda a, b: g[:, c * 512 + a:c * 512 + b]
        cw2 = cw[:, c * 3 + 2:c * 3 + 3]
        cw1 = cw[:, c * 3 + 1:c * 3 + 2]
        cw0 = cw[:, c * 3 + 0:c * 3 + 1]
        cbc = cb[:, c:c + 1]
        he0 = hedge[:, c * 2:c * 2 + 1]
        he1 = hedge[:, c * 2 + 1:c * 2 + 2]
        scr = bca[:, 1024:1536]
        wt = [("te", v_te)]
        if isfirst:
            k.rec("ve", lambda e: e.tensor_scalar(
                scr[:, 0:511], hb[:, 1:512], cw2, cbc, ALU.mult, ALU.add),
                wait=wt, drain=True)
            k.rec("ve", lambda e: e.scalar_tensor_tensor(
                scr[:, 0:511], hb[:, 0:511], cw1, scr[:, 0:511], ALU.mult,
                ALU.add), drain=True)
            k.rec("ve", lambda e: e.scalar_tensor_tensor(
                gv(1, 511), hb[:, 0:510], cw0, scr[:, 1:511], ALU.mult,
                ALU.add), drain=True)
            k.rec("ve", lambda e: e.tensor_copy(gv(0, 1), scr[:, 0:1]),
                  drain=True)
        else:
            k.rec("ve", lambda e: e.tensor_scalar(
                scr[:, 0:512], hb[:, 0:512], cw2, cbc, ALU.mult, ALU.add),
                wait=wt, drain=True)
            k.rec("ve", lambda e: e.scalar_tensor_tensor(
                scr[:, 1:512], hb[:, 0:511], cw1, scr[:, 1:512], ALU.mult,
                ALU.add), drain=True)
            k.rec("ve", lambda e: e.scalar_tensor_tensor(
                scr[:, 0:1], he1, cw1, scr[:, 0:1], ALU.mult, ALU.add),
                drain=True)
            k.rec("ve", lambda e: e.scalar_tensor_tensor(
                gv(2, 512), hb[:, 0:510], cw0, scr[:, 2:512], ALU.mult,
                ALU.add), drain=True)
            k.rec("ve", lambda e: e.scalar_tensor_tensor(
                gv(0, 1), he0, cw0, scr[:, 0:1], ALU.mult, ALU.add),
                drain=True)
            k.rec("ve", lambda e: e.scalar_tensor_tensor(
                gv(1, 2), he1, cw0, scr[:, 1:2], ALU.mult, ALU.add),
                drain=True)
        k.rec("ve", lambda e: e.tensor_copy(he0, hb[:, 510:511]), drain=True)
        vv = k.rec("ve", lambda e: e.tensor_copy(he1, hb[:, 511:512]),
                   inc="ve", drain=True)
        k.rec("sc", lambda e, w=w: e.activation(gv(0, w), gv(0, w),
                                                AF.Silu),
              wait=[("ve", vv)], drain=True)

    def tail_col():
        he0v = hedge[:].rearrange("p (c e) -> p c e", e=2)[:, :, 0:1]
        he1v = hedge[:].rearrange("p (c e) -> p c e", e=2)[:, :, 1:2]
        cw0v = cw[:].rearrange("p (c e) -> p c e", e=3)[:, :, 0:1]
        cw1v = cw[:].rearrange("p (c e) -> p c e", e=3)[:, :, 1:2]
        cbv = cb[:].rearrange("p (c e) -> p c e", e=1)
        gtv = g[:].rearrange("p (c t) -> p c t", t=512)[:, :, 0:1]
        scr = bca[:, 1504:1536].rearrange("p (c e) -> p c e", e=1)
        k.rec("ve", lambda e: e.tensor_tensor(scr, he0v, cw0v, ALU.mult),
              drain=True)
        k.rec("ve", lambda e: e.tensor_tensor(gtv, he1v, cw1v, ALU.mult),
              drain=True)
        k.rec("ve", lambda e: e.tensor_tensor(gtv, gtv, scr, ALU.add),
              drain=True)
        vv = k.rec("ve", lambda e: e.tensor_tensor(gtv, gtv, cbv, ALU.add),
                   inc="ve", drain=True)
        k.rec("sc", lambda e: e.activation(
            g[:].rearrange("p (c t) -> p c t", t=512)[:, :, 0:1],
            g[:].rearrange("p (c t) -> p c t", t=512)[:, :, 0:1], AF.Silu),
            wait=[("ve", vv)], drain=True)

    def hidden_rows(w, pmax=None):
        vs = k.bar("sc", inc="sc")
        if pmax is None:
            gvv = g[:].rearrange("p (c t) -> p t c", c=HCH)[:, 0:w, :]
            vv = k.rec("ve", lambda e: e.reduce_max(
                bca[:, 512:512 + w], gvv, AX.X, apply_absolute_value=True),
                wait=[("sc", vs)], inc="ve", drain=True)
        else:
            vv = pmax
        vv = cross_max(bca[:, 512:], w, rows[0:1, 1024:1024 + w],
                       [("ve", vv)])
        k.rec("ve", lambda e: e.tensor_single_scalar(
            rows[0:1, 1024:1024 + w], rows[0:1, 1024:1024 + w], 1e-5,
            ALU.max), wait=[("ve", vv)], drain=True)
        vv = k.rec("ve", lambda e: e.tensor_scalar_mul(
            rows[0:1, 2560:2560 + w], rows[0:1, 1024:1024 + w],
            0.5 / 127.0), inc="ve", drain=True)
        k.rec("sc", lambda e: e.activation(
            rows[0:1, 1024:1024 + w], rows[0:1, 1024:1024 + w], AF.Ln),
            wait=[("ve", vv)], drain=True)
        vs = k.rec("sc", lambda e: e.activation(
            rows[0:1, 1024:1024 + w], rows[0:1, 1024:1024 + w], AF.Exp,
            scale=-1.0, bias=LN127), inc="sc", drain=True)
        return bcast_to(hbc, rows[0:1, 1024:1024 + w], w,
                        [("sc", vs), ("ve", k.v("ve"))])

    def ffn(x_src, x1_dst, do_ln2, rq_src, wait0):
        wins = []
        for sqi in range(NB):
            wins.append((sqi, 0, 0, 511))
            for kk in range(1, 4):
                wins.append((sqi, 1, kk * 512, 512))
            wins.append((sqi, 2, None, 1))
        NW = len(wins)
        rq_gd = k.v("gd")
        wb_free = {id(wba): 0, id(wbb): 0}
        pb_free = {}
        HB = [psA, psB, psE, psF, psG]
        v_xfree = {}
        v_xload = {}
        v_rpq = {}
        st8 = {"x1gd": 0,
               "y2gd": 0,
               "w1te": 0}

        def xload(wi):
            sqi, kind, t0, w = wins[wi]
            xcur = xa if wi % 2 == 0 else xb
            wt = list(wait0) if wi <= 1 else [("ve", v_xfree[wi - 2])]
            vx = k.dma("sy", xcur[:].rearrange("p (c t) -> p c t", c=NCH),
                       _drv(x_src)[:, :, sqi * S + t0:sqi * S + t0 + TT],
                       wait=wt)
            v_xload[wi] = vx
            k.rec("ve", lambda e, xcur=xcur, wi=wi: e.tensor_copy(
                xedge[:, (wi % 4) * 8:(wi % 4) * 8 + 8].rearrange(
                    "p (c t) -> p c t", t=1),
                xcur[:].rearrange("p (c t) -> p c t", c=NCH)
                [:, :, 511:512]), wait=[("ld", vx)], inc="ve", drain=True)

        def prep_rp(wi, gp_wait):
            sqi, kind, t0, w = wins[wi]
            xcur = xa if wi % 2 == 0 else xb
            if rq_src is None:
                v = ln_quant(xcur, TT,
                             lambda ch: rp[:, ch * 512:(ch + 1) * 512],
                             [("ld", v_xload[wi]), ("te", st8["w1te"])],
                             xc_wait=[("gp", gp_wait)], xcb=g)
                v_rpq[wi] = ("ve", v)
            else:
                v_rpq[wi] = ("ld", k.dma(
                    "sy", rp[:].rearrange("p (c t) -> p c t", c=NCH),
                    _drv(rq_src)[:, :, sqi * S + t0:sqi * S + t0 + TT],
                    wait=[("te", st8["w1te"]), ("gd", rq_gd)]))
            pb_free[id(psF)] = {"ve": k.v("ve"), "sc": k.v("sc")}
            pb_free[id(psG)] = {"ve": k.v("ve"), "sc": k.v("sc")}

        xload(0)
        if NW > 1 and wins[1][1] != 2:
            xload(1)
        prep_rp(0, 0)
        for wi, (sqi, kind, t0, w) in enumerate(wins):
            base = sqi * S
            xcur = xa if (wi % 2 == 0) else xb
            gw0 = (t0 if kind == 0 else t0 - 1) if kind != 2 else S - 1
            if kind != 2:
                for cg in range(8):
                    wb = wba if cg % 2 == 0 else wbb
                    vw = k.dma("sy", wb[:].rearrange("p (c t) -> p c t",
                                                     c=NCH),
                               _drv(wq1s)[:, :, cg * 512:(cg + 1) * 512],
                               wait=[("te", wb_free[id(wb)])])
                    banks = [HB[(cg * 4 + cl) % 5] for cl in range(4)]
                    vts = []
                    for cl in range(4):
                        hb = banks[cl]
                        pf = pb_free.get(id(hb), {"ve": 0, "sc": 0})
                        vt = None
                        for ic in range(NCH):
                            vt = k.rec("te", lambda e, ic=ic, cl=cl, hb=hb,
                                       wb=wb: e.matmul(
                                hb[:, 0:TT],
                                wb[:, ic * 512 + cl * 128:
                                   ic * 512 + cl * 128 + 128],
                                rp[:, ic * 512:(ic + 1) * 512],
                                start=(ic == 0), stop=(ic == NCH - 1)),
                                wait=[("ld", vw), v_rpq[wi],
                                      ("ve", pf["ve"]), ("sc", pf["sc"])]
                                if ic == 0 else None, inc="te")
                        vts.append(vt)
                    xw = [("gd", st8["x1gd"]), ("ve", k.v("ve"))] \
                        if cg == 0 else []
                    conv_group(cg, banks, vts, kind == 0, w, xw)
                    for cl in range(4):
                        pb_free[id(banks[cl])] = {"ve": k.v("ve"),
                                                  "sc": k.v("sc")}
                    wb_free[id(wb)] = k.v("te")
                    gslc = g[:, cg * 2048:(cg + 1) * 2048].rearrange(
                        "p (c t) -> p t c", c=4)[:, 0:w, :]
                    if cg == 0:
                        pmax = k.rec("ve", lambda e, gslc=gslc, w=w:
                                     e.reduce_max(
                            bca[:, 512:512 + w], gslc, AX.X,
                            apply_absolute_value=True),
                            wait=[("sc", k.v("sc"))], inc="ve", drain=True)
                    else:
                        k.rec("ve", lambda e, gslc=gslc, w=w: e.reduce_max(
                            bca[:, 1024:1024 + w], gslc, AX.X,
                            apply_absolute_value=True),
                            wait=[("sc", k.v("sc"))], drain=True)
                        pmax = k.rec("ve", lambda e, w=w: e.tensor_tensor(
                            bca[:, 512:512 + w], bca[:, 512:512 + w],
                            bca[:, 1024:1024 + w], ALU.max),
                            inc="ve", drain=True)
                st8["w1te"] = k.v("te")
            else:
                tail_col()
                pmax = None
            vbc = hidden_rows(w, pmax=pmax)
            pb_free[id(psF)] = {"ve": k.v("ve"), "sc": k.v("sc")}
            pb_free[id(psG)] = {"ve": k.v("ve"), "sc": k.v("sc")}
            vR_g, vpool = round_hidden_grouped(w, vbc, rh_gd=st8["y2gd"])
            vR = vR_g[-1]
            nx = wi + 1

            def res_fc(fc, ob_, vt, vbb_):
                a2 = lambda a, b, fc=fc: ar2[:, fc * 512 + a:fc * 512 + b]
                k.rec("ve", lambda e, ob_=ob_, w=w, a2=a2: e.tensor_tensor(
                    a2(0, w), ob_[:, 0:w], hbc[:, 512:512 + w], ALU.mult),
                    wait=[("te", vt), ("ve", vbb_), ("gd", st8["x1gd"])],
                    drain=True)
                ecol = (wi - 1) % 4 * 8 + fc
                if kind == 0:
                    vv = k.rec("ve", lambda e, fc=fc, w=w, xcur=xcur, a2=a2:
                               e.tensor_tensor(
                        a2(0, w), a2(0, w),
                        xcur[:, fc * 512:fc * 512 + w], ALU.add),
                        inc="ve", drain=True)
                elif kind == 1:
                    k.rec("ve", lambda e, ecol=ecol, a2=a2:
                          e.tensor_tensor(
                        a2(0, 1), a2(0, 1),
                        xedge[:, ecol:ecol + 1], ALU.add), drain=True)
                    vv = k.rec("ve", lambda e, fc=fc, w=w, xcur=xcur, a2=a2:
                               e.tensor_tensor(
                        a2(1, w), a2(1, w),
                        xcur[:, fc * 512:fc * 512 + w - 1], ALU.add),
                        inc="ve", drain=True)
                else:
                    vv = k.rec("ve", lambda e, ecol=ecol, a2=a2:
                               e.tensor_tensor(
                        a2(0, 1), a2(0, 1),
                        xedge[:, ecol:ecol + 1], ALU.add),
                        inc="ve", drain=True)
                pb_free[id(ob_)] = {"ve": k.v("ve"), "sc": k.v("sc")}
                return vv

            if kind != 2:
                B5 = [psA, psB, psE, psF, psG]
                vt5 = [None] * 5
                for gi in range(8):
                    wb = wba if gi % 2 == 0 else wbb
                    vw = k.dma("sy", wb[:, 0:4096],
                               wq2g[:, gi * 4096:(gi + 1) * 4096],
                               wait=[("te", wb_free[id(wb)])])
                    for fc in range(5):
                        ob_ = B5[fc]
                        pf = pb_free.get(id(ob_), {"ve": 0, "sc": 0})
                        for cl in range(4):
                            wt = []
                            if fc == 0 and cl == 0:
                                wt += [("ld", vw), ("ve", vR_g[gi])]
                            if gi == 0 and cl == 0:
                                wt += [("ve", pf["ve"]), ("sc", pf["sc"])]
                            vt5[fc] = k.rec("te", lambda e, gi=gi, fc=fc,
                                            cl=cl, ob_=ob_, wb=wb, w=w:
                                            e.matmul(
                                ob_[:, 0:w],
                                wb[:, cl * 1024 + fc * 128:
                                   cl * 1024 + fc * 128 + 128],
                                Rh[:, (gi * 4 + cl) * 512:
                                   (gi * 4 + cl) * 512 + w],
                                start=(gi == 0 and cl == 0),
                                stop=(gi == 7 and cl == 3)),
                                wait=wt or None, inc="te")
                    wb_free[id(wb)] = k.v("te")
                vbb_ = bcast_to(hbc[:, 512:1024],
                                rows[0:1, 2560:2560 + w], w, [("ve", vR)])
                for fc in range(5):
                    vv = res_fc(fc, B5[fc], vt5[fc], vbb_)
                if nx < NW and wins[nx][1] != 2:
                    prep_rp(nx, vpool)
                for j, fc in enumerate((5, 6, 7)):
                    wb = wba if j % 2 == 0 else wbb
                    vw = k.dma("sy", wb[:, 0:4096].rearrange(
                        "p (c t) -> p c t", c=HCH),
                        _drv(wq2s)[:, :, fc * 128:(fc + 1) * 128],
                        wait=[("te", wb_free[id(wb)])])
                    ob_ = (psA, psB, psE)[j]
                    pf = pb_free.get(id(ob_), {"ve": 0, "sc": 0})
                    vt = None
                    for hc in range(HCH):
                        vt = k.rec("te", lambda e, hc=hc, ob_=ob_, wb=wb,
                                   w=w: e.matmul(
                            ob_[:, 0:w], wb[:, hc * 128:(hc + 1) * 128],
                            Rh[:, hc * 512:hc * 512 + w],
                            start=(hc == 0), stop=(hc == HCH - 1)),
                            wait=[("ld", vw), ("ve", pf["ve"]),
                                  ("sc", pf["sc"])] if hc == 0 else None,
                            inc="te")
                    wb_free[id(wb)] = k.v("te")
                    vv = res_fc(fc, ob_, vt, vbb_)
            else:
                vbb_ = bcast_to(hbc[:, 512:1024],
                                rows[0:1, 2560:2560 + w], w, [("ve", vR)])
                if nx < NW and wins[nx][1] != 2:
                    prep_rp(nx, vpool)
                for fc in range(NCH):
                    wb = wba if fc % 2 == 0 else wbb
                    vw = k.dma("sy", wb[:, 0:4096].rearrange(
                        "p (c t) -> p c t", c=HCH),
                        _drv(wq2s)[:, :, fc * 128:(fc + 1) * 128],
                        wait=[("te", wb_free[id(wb)])])
                    ob_ = psA if fc % 2 == 0 else psB
                    pf = pb_free.get(id(ob_), {"ve": 0, "sc": 0})
                    vt = None
                    for hc in range(HCH):
                        vt = k.rec("te", lambda e, hc=hc, ob_=ob_, wb=wb,
                                   w=w: e.matmul(
                            ob_[:, 0:w], wb[:, hc * 128:(hc + 1) * 128],
                            Rh[:, hc * 512:hc * 512 + w],
                            start=(hc == 0), stop=(hc == HCH - 1)),
                            wait=[("ld", vw), ("ve", vR), ("ve", pf["ve"]),
                                  ("sc", pf["sc"])]
                            if hc == 0 else None, inc="te")
                    wb_free[id(wb)] = k.v("te")
                    vv = res_fc(fc, ob_, vt, vbb_)
            v_xfree[wi] = vv
            vv = k.bar("ve", inc="ve")
            vst = k.dma("gp",
                        _drv(x1_dst)[:, :, base + gw0:base + gw0 + w],
                        ar2[:].rearrange("p (c t) -> p c t", c=NCH)
                        [:, :, 0:w], wait=[("ve", vv)], slow=(w == 1))
            st8["x1gd"] = vst
            if do_ln2:
                vy = ln2_window(w)
                vst = k.dma("gp",
                            _drv(y2s)[:, :, base + gw0:base + gw0 + w],
                            y2st[:].rearrange("p (c t) -> p c t", c=NCH)
                            [:, :, 0:w], wait=[("ve", vy)], slow=(w == 1))
                st8["y2gd"] = vst
            if wi + 2 < NW and wins[wi + 2][1] != 2:
                xload(wi + 2)
        return k.v("gd")

    def ln2_window(w):
        vt = stat_sums(ar2, w, onesc, [("ve", k.v("ve"))])
        vv = k.rec("ve", lambda e: e.tensor_scalar_mul(
            rows[0:1, 0:w], psD[0:1, 0:w], 1.0 / D), wait=[("te", vt)],
            inc="ve", drain=True)
        vv = bcast(rows[0:1, 0:w], w, [("ve", vv)])
        for ch in range(NCH):
            k.rec("ve", lambda e, ch=ch: e.tensor_tensor(
                g[:, 4096 + ch * 512:4096 + ch * 512 + w],
                ar2[:, ch * 512:ch * 512 + w], bca[:, 0:w], ALU.subtract),
                wait=[("ve", vv)] if ch == 0 else None, drain=True)
        vv = k.bar("ve", inc="ve")
        for ch in range(NCH):
            k.rec("sc", lambda e, ch=ch: e.activation(
                y2sq[:, ch * 512:ch * 512 + w],
                g[:, 4096 + ch * 512:4096 + ch * 512 + w],
                AF.Square), wait=[("ve", vv)] if ch == 0 else None,
                drain=(ch == 0))
        vs = k.bar("sc", inc="sc")
        vt = stat_sums(y2sq, w, onescb, [("sc", vs)])
        vs = rstd_rows(psD[0:1, 0:w], w, 512, wait=[("te", vt)])
        vv = bcast(rows[0:1, 512:512 + w], w, [("sc", vs)])
        vlast = None
        for ch in range(NCH):
            vlast = k.rec("ve", lambda e, ch=ch: e.tensor_tensor(
                y2st[:, ch * 512:ch * 512 + w],
                g[:, 4096 + ch * 512:4096 + ch * 512 + w],
                bca[:, 0:w], ALU.mult),
                wait=[("ve", vv)] if ch == 0 else None, inc="ve",
                drain=True)
        return vlast

    def table_gen(h, v_iot):
        thc = th2pi[:, h:h + 1]
        k.rec("ve", lambda e: e.tensor_scalar(
            tgf[:], tgr[:], thc, None, ALU.mult),
            wait=[("sc", k.v("sc")), ("te", k.v("te")), ("ld", v_iot)],
            drain=True)
        k.rec("ve", lambda e: e.tensor_scalar(
            tgf2[:], tgf[:], MAGIC, -MAGIC, ALU.add, ALU.add), drain=True)
        vv = k.rec("ve", lambda e: e.tensor_tensor(
            tgf[:], tgf[:], tgf2[:], ALU.subtract), inc="ve", drain=True)
        k.rec("sc", lambda e: e.activation(
            sinf[:], tgf[:], AF.Sin, scale=PI2), wait=[("ve", vv)],
            drain=True)
        k.rec("ve", lambda e: e.tensor_single_scalar(
            tgf2[:], tgf[:], 0.25, ALU.is_ge), drain=True)
        vv = k.rec("ve", lambda e: e.scalar_tensor_tensor(
            tgf2[:], tgf2[:], -1.0, tgf[:], ALU.mult, ALU.add), inc="ve",
            drain=True)
        k.rec("sc", lambda e: e.activation(
            cosf[:], tgf2[:], AF.Sin, scale=PI2, bias=PIH),
            wait=[("ve", vv)], drain=True)
        return k.rec("ve", lambda e, h=h: e.tensor_scalar(
            tribh[:], trib[:], esm1c[:, h:h + 1], None, ALU.mult),
            wait=[("ve", v_const)], inc="ve", drain=True)

    def attention():
        v_y2ld = k.dma("sy", y2[:].rearrange("p (c t) -> p c t", c=NCH),
                       _drv(y2s), wait=[("gd", k.v("gd"))])
        v_iot = k.dma("sy", tgr[:], iot,
                      wait=[("ve", k.v("ve")), ("sc", k.v("sc")),
                            ("te", k.v("te"))])
        v_wqh_free = 0
        st = {"uh": k.v("ve"), "sq_sc": k.v("sc")}
        for h in range(NH):
            vwq = None
            for pi, po in enumerate((0, D, 2 * D, 3 * D)):
                vwq = k.dma("gp",
                            wqh[:].rearrange("p (a c t) -> p a c t", a=4,
                                             c=NCH)[:, pi],
                            _drv(qwT)[:, :, po + h * 128:po + h * 128 + 128],
                            wait=[("te", v_wqh_free)] if pi == 0 else None)
            vmask = table_gen(h, v_iot)
            if vmask > st["uh"]:
                st["uh"] = vmask
            for b in range(NB):
                v_wqh_free = attn_bh(h, b, vwq, v_y2ld, vmask, st)
        if AMODE in ("proj", "state", "nostats"):
            return k.v("sc")
        vs = k.rec("sc", lambda e: e.activation(
            rsqtm[:], psD[:, 0:32], AF.Ln, bias=1e-6, scale=1.0 / D),
            wait=[("te", k.v("te"))], inc="sc", drain=True)
        return k.rec("sc", lambda e: e.activation(
            rsqtm[:], rsqtm[:], AF.Exp, scale=-0.5), inc="sc", drain=True)

    def attn_bh(h, b, vwq, v_y2ld, vmask, st):
        v0 = [("gd", vwq), ("ld", v_y2ld), ("ve", st["uh"]),
              ("sc", st["sq_sc"])]
        for pi, name in enumerate("qkvu"):
            for tb in range(4):
                pb = psA if tb % 2 == 0 else psB
                vt = None
                for ic in range(NCH):
                    wt = None
                    if ic == 0:
                        wt = v0 + [("sc", k.v("sc"))]
                        v0 = []
                    vt = k.rec("te", lambda e, pi=pi, tb=tb, ic=ic, pb=pb:
                               e.matmul(
                        pb[:, 0:512],
                        wqh[:].rearrange("p (a c t) -> p a c t", a=4,
                                         c=NCH)[:, pi, ic],
                        y2[:, ic * T + b * S + tb * 512:
                           ic * T + b * S + (tb + 1) * 512],
                        start=(ic == 0), stop=(ic == NCH - 1)),
                        wait=wt, inc="te")
                bcol = qb[:, pi * 8 + h:pi * 8 + h + 1]
                if name in "qk":
                    dst = silq if name == "q" else silk
                    k.rec("sc", lambda e, dst=dst, tb=tb, pb=pb, bcol=bcol:
                          e.activation(dst[:, tb * 512:(tb + 1) * 512],
                                       pb[:, 0:512], AF.Silu, bias=bcol),
                          wait=[("te", vt)], inc="sc", drain=True)
                elif name == "u":
                    k.rec("sc", lambda e, tb=tb, pb=pb, bcol=bcol:
                          e.activation(uh[:, tb * 512:(tb + 1) * 512],
                                       pb[:, 0:512], AF.Identity,
                                       bias=bcol),
                          wait=[("te", vt)], inc="sc", drain=True)
                else:
                    vs = k.rec("sc", lambda e, tb=tb, pb=pb, bcol=bcol:
                               e.activation(vfm[:, 0:512], pb[:, 0:512],
                                            AF.Identity, bias=bcol),
                               wait=[("te", vt)], inc="sc", drain=True)
                    vt2 = None
                    for jj in range(4):
                        vt2 = k.rec("te", lambda e, jj=jj: e.transpose(
                            psT[:, jj * 128:(jj + 1) * 128],
                            vfm[:, jj * 128:(jj + 1) * 128], identb[:]),
                            wait=[("sc", vs), ("ve", k.v("ve"))]
                            if jj == 0 else None, inc="te")
                    vv = k.rec("ve", lambda e, tb=tb: e.tensor_copy(
                        vtm[:, tb * 512:(tb + 1) * 512], psT[:, 0:512]),
                        wait=[("te", vt2)], inc="ve", drain=True)
        v_projte = k.v("te")
        vs_pr = k.bar("sc", inc="sc")
        for dst, src, tb_ in ((qcb, silq, cosf), (qsb, silq, sinf),
                              (kcb, silk, cosf), (ksb, silk, sinf)):
            k.rec("ve", lambda e, d=dst, s=src, t=tb_: e.tensor_tensor(
                d[:], s[:], t[:], ALU.mult), wait=[("sc", vs_pr)],
                drain=True)
        vv_lrpe = k.bar("ve", inc="ve")
        vv_kt = None
        for half, (srcb, dstb) in enumerate(((kcb, kcT), (ksb, ksT))):
            for grp in range(4):
                vt = None
                for jj in range(4):
                    wt_t = [("ve", k.v("ve"))] if jj == 0 else None
                    vt = k.rec("te", lambda e, srcb=srcb, grp=grp, jj=jj:
                               e.transpose(
                        psT[:, jj * 128:(jj + 1) * 128],
                        srcb[:, grp * 512 + jj * 128:
                             grp * 512 + (jj + 1) * 128], identb[:]),
                        wait=wt_t, inc="te")
                vv_kt = k.rec("ve", lambda e, dstb=dstb, grp=grp:
                              e.tensor_copy(
                    dstb[:, grp * 512:(grp + 1) * 512], psT[:, 0:512]),
                    wait=[("te", vt)], inc="ve", drain=True)
        if AMODE == "proj":
            st["uh"] = k.v("ve")
            st["sq_sc"] = k.v("sc")
            return v_projte
        UB = [psF, psG, psA, psB]
        v_pfx = []
        vtU = []
        vv_a = None

        def bsb_ap(c):
            return bsbA[:, (c - 1) * 256:c * 256] if c <= 6 else bsbB[:]

        def pfx_step(c):
            nonlocal vv_a
            if c == 0:
                k.rec("ve", lambda e: e.tensor_copy(bsbf[:],
                                                    UB[0][:, 0:256]),
                      wait=[("te", vtU[0])], drain=True)
            else:
                k.rec("ve", lambda e, ub=UB[c % 4]: e.tensor_tensor(
                    bsbf[:], bsbf[:], ub[:, 0:256], ALU.add),
                    wait=[("te", vtU[c])], drain=True)
            if c <= 6:
                vv = k.rec("ve", lambda e, dst=bsb_ap(c + 1), h=h:
                           e.tensor_scalar(dst, bsbf[:], esm1c[:, h:h + 1],
                                           None, ALU.mult), inc="ve",
                           drain=True)
                v_pfx.append(vv)
            if c == 7:
                vv_a = k.rec("ve", lambda e: e.tensor_copy(asb[:],
                                                           bsbf[:]),
                             inc="ve", drain=True)

        for c in range(8):
            ub = UB[c % 4]
            wt = None
            if c == 0:
                wt = [("ve", vv_kt)]
            elif c in (2, 3):
                wt = [("sc", vs_pr)]
            elif c >= 4:
                wt = [("ve", v_pfx[c - 4])]
            vt = None
            for half, kT in enumerate((kcT, ksT)):
                for jb in range(2):
                    jc = c * 2 + jb
                    vt = k.rec("te", lambda e, ub=ub, kT=kT,
                               off=half * 128, jc=jc, s0=(jb == 0),
                               s1=(jb == 1): e.matmul(
                        ub[:, off:off + 128],
                        kT[:, jc * 128:(jc + 1) * 128],
                        vtm[:, jc * 128:(jc + 1) * 128],
                        start=s0, stop=s1),
                        wait=wt if (half == 0 and jb == 0) else None,
                        inc="te")
            vtU.append(vt)
            if c >= 2:
                pfx_step(c - 2)
        pfx_step(6)
        pfx_step(7)
        if AMODE == "state":
            st["uh"] = k.v("ve")
            st["sq_sc"] = k.v("sc")
            return v_projte
        vintra = {}
        vmaskv = {}
        vAVt = {}
        vsq = {}
        vstat = {}
        vout = {}
        OU = [psA, psB]
        IC = [psE, psC]

        def pre(c):
            ou = OU[c % 2]
            ic_ = IC[c % 2]
            if c <= 1:
                wt = [("ve", vv_a)]
            else:
                wt = [("ve", vout[c - 2]), ("sc", vsq[c - 2])]
            k.rec("te", lambda e, c=c, ou=ou: e.matmul(
                ou[:, 0:256], asb[:, 0:128], qcb[:, c * 256:(c + 1) * 256],
                start=True, stop=False), wait=wt, inc="te")
            k.rec("te", lambda e, c=c, ou=ou: e.matmul(
                ou[:, 0:256], asb[:, 128:256],
                qsb[:, c * 256:(c + 1) * 256], start=False, stop=False),
                inc="te")
            if c > 0:
                bs = bsb_ap(c)
                k.rec("te", lambda e, c=c, ou=ou, bs=bs: e.matmul(
                    ou[:, 0:256], bs[:, 0:128],
                    qcb[:, c * 256:(c + 1) * 256], start=False,
                    stop=False), wait=[("ve", v_pfx[c - 1])], inc="te")
                k.rec("te", lambda e, c=c, ou=ou, bs=bs: e.matmul(
                    ou[:, 0:256], bs[:, 128:256],
                    qsb[:, c * 256:(c + 1) * 256], start=False,
                    stop=False), inc="te")
            wti = [("ve", vmaskv[c - 2])] if c >= 2 else None
            k.rec("te", lambda e, c=c, ic_=ic_: e.matmul(
                ic_[:, 0:256], kcb[:, c * 256:c * 256 + 128],
                qcb[:, c * 256:(c + 1) * 256], start=True, stop=False),
                wait=wti, inc="te")
            k.rec("te", lambda e, c=c, ic_=ic_: e.matmul(
                ic_[:, 0:256], ksb[:, c * 256:c * 256 + 128],
                qsb[:, c * 256:(c + 1) * 256], start=False, stop=True),
                inc="te")
            k.rec("te", lambda e, c=c, ic_=ic_: e.matmul(
                ic_[:, 256:384], kcb[:, c * 256 + 128:c * 256 + 256],
                qcb[:, c * 256 + 128:(c + 1) * 256], start=True,
                stop=False), inc="te")
            vintra[c] = k.rec("te", lambda e, c=c, ic_=ic_: e.matmul(
                ic_[:, 256:384], ksb[:, c * 256 + 128:c * 256 + 256],
                qsb[:, c * 256 + 128:(c + 1) * 256], start=False,
                stop=True), inc="te")

        def maskev(c):
            ic_ = IC[c % 2]
            fp = fsb[:, (c % 2) * 384:(c % 2) * 384 + 384]
            wt = [("te", vintra[c])]
            if c >= 2:
                wt.append(("te", vAVt[c - 2]))
            k.rec("ve", lambda e, ic_=ic_, fp=fp: e.tensor_tensor(
                fp[:, 0:128], ic_[:, 0:128], tribh[:], ALU.mult),
                wait=wt, drain=True)
            k.rec("ve", lambda e, ic_=ic_, fp=fp, h=h: e.tensor_scalar(
                fp[:, 128:256], ic_[:, 128:256], esm1c[:, h:h + 1], None,
                ALU.mult), drain=True)
            vmaskv[c] = k.rec("ve", lambda e, ic_=ic_, fp=fp:
                              e.tensor_tensor(
                fp[:, 256:384], ic_[:, 256:384], tribh[:], ALU.mult),
                inc="ve", drain=True)

        def av(c):
            ou = OU[c % 2]
            fp = fsb[:, (c % 2) * 384:(c % 2) * 384 + 384]
            k.rec("te", lambda e, c=c, ou=ou, fp=fp: e.matmul(
                ou[:, 128:256], vtm[:, c * 256 + 128:c * 256 + 256],
                fp[:, 256:384], start=False, stop=False),
                wait=[("ve", vmaskv[c])], inc="te")
            vAVt[c] = k.rec("te", lambda e, c=c, ou=ou, fp=fp: e.matmul(
                ou[:, 0:256], vtm[:, c * 256:c * 256 + 128], fp[:, 0:256],
                start=False, stop=True), inc="te")

        def outev(c):
            ou = OU[c % 2]
            pc0 = h * T + b * S + c * 256
            if AMODE == "nostats":
                vsq[c] = k.v("sc")
                vout[c] = k.rec("ve", lambda e, c=c, ou=ou, pc0=pc0:
                                e.tensor_tensor(
                    pat[:, pc0:pc0 + 256], ou[:, 0:256],
                    uh[:, c * 256:(c + 1) * 256], ALU.mult),
                    wait=[("te", vAVt[c])], inc="ve", drain=True)
                return
            wt = [("te", vAVt[c])]
            if c >= 1:
                wt.append(("te", vstat[c - 1]))
            vsq[c] = k.rec("sc", lambda e, c=c, ou=ou: e.activation(
                sqat[:], ou[:, 0:256], AF.Square), wait=wt, inc="sc",
                drain=True)
            vout[c] = k.rec("ve", lambda e, c=c, ou=ou, pc0=pc0:
                            e.tensor_tensor(
                pat[:, pc0:pc0 + 256], ou[:, 0:256],
                uh[:, c * 256:(c + 1) * 256], ALU.mult),
                wait=[("sc", vsq[c])], inc="ve", drain=True)

        def stat(c):
            vt = None
            for jb in range(2):
                col = b * 16 + c * 2 + jb
                vt = k.rec("te", lambda e, c=c, jb=jb, col=col, h=h, b=b:
                           e.matmul(
                    psD[:, col:col + 1],
                    sqat[:, jb * 128:(jb + 1) * 128],
                    onescb[:, 0:1],
                    start=(h == 0 and b == 0 and c == 0 and jb == 0),
                    stop=(h == NH - 1 and b == NB - 1 and c == 7
                          and jb == 1)),
                    wait=[("sc", vsq[c])] if jb == 0 else None, inc="te")
            vstat[c] = vt

        pre(0)
        maskev(0)
        pre(1)
        maskev(1)
        for c in range(8):
            av(c)
            if c >= 1 and AMODE != "nostats":
                stat(c - 1)
            outev(c)
            if c + 2 < 8:
                pre(c + 2)
                maskev(c + 2)
        if AMODE != "nostats":
            stat(7)
        st["uh"] = vout[7]
        st["sq_sc"] = vsq[7]
        return v_projte

    def attn_out(v_rsq):
        k.dma("gp", owb[:].rearrange("p (c t) -> p c t", c=NCH),
              _drv(owT), wait=[("te", k.v("te")), ("ve", k.v("ve")),
                               ("sc", k.v("sc"))])
        v_st = 0
        v_xfree = ("sc", k.v("sc"))
        pb4 = {id(psA): 0, id(psB): 0}
        for tl in range(8):
            b, tb = tl // 4, tl % 4
            base = b * S + tb * 512
            vt2 = None
            for jj in range(4):
                col = b * 16 + tb * 4 + jj
                vt2 = k.rec("te", lambda e, col=col, jj=jj: e.transpose(
                    psG[0:1, jj * 128:jj * 128 + 128],
                    rsqtm[:, col:col + 1], identf[:]),
                    wait=[("sc", v_rsq)] if (tl == 0 and jj == 0) else None,
                    inc="te")
            vv = k.rec("ve", lambda e: e.tensor_copy(
                rows[0:1, 0:512], psG[0:1, 0:512]), wait=[("te", vt2)],
                inc="ve", drain=True)
            vv = bcast(rows[0:1, 0:512], 512, [("ve", vv)])
            for hc in range(NCH):
                k.rec("ve", lambda e, hc=hc, base=base: e.tensor_tensor(
                    zsb[:, hc * 512:(hc + 1) * 512],
                    pat[:, hc * T + base:hc * T + base + 512],
                    bca[:, 0:512], ALU.mult),
                    wait=[("ve", vv)] if hc == 0 else None, drain=True)
            vv = k.bar("ve", inc="ve")
            vx = k.dma("sy", x4a[:].rearrange("p (c t) -> p c t", c=NCH),
                       _drv(xs1)[:, :, base:base + 512], wait=[v_xfree])
            for fc in range(NCH):
                ob_ = psA if fc % 2 == 0 else psB
                vt = None
                for ic in range(NCH):
                    wt4 = None
                    if ic == 0:
                        wt4 = [("sc", pb4[id(ob_)])]
                        if fc == 0:
                            wt4 += [("ve", vv), ("gd", k.v("gd"))]
                    vt = k.rec("te", lambda e, fc=fc, ic=ic, ob_=ob_:
                               e.matmul(
                        ob_[:, 0:512],
                        owb[:, ic * 1024 + fc * 128:ic * 1024 + fc * 128
                            + 128],
                        zsb[:, ic * 512:(ic + 1) * 512],
                        start=(ic == 0), stop=(ic == NCH - 1)),
                        wait=wt4, inc="te")
                vs = k.rec("sc", lambda e, fc=fc, ob_=ob_: e.activation(
                    ar2[:, fc * 512:(fc + 1) * 512], ob_[:, 0:512],
                    AF.Identity, bias=ob[:, fc:fc + 1]),
                    wait=[("te", vt)], inc="sc", drain=True)
                pb4[id(ob_)] = vs
                vv2 = k.rec("ve", lambda e, fc=fc: e.tensor_tensor(
                    ar2[:, fc * 512:(fc + 1) * 512],
                    ar2[:, fc * 512:(fc + 1) * 512],
                    x4a[:, fc * 512:(fc + 1) * 512], ALU.add),
                    wait=[("sc", vs), ("ld", vx)], inc="ve", drain=True)
            v_xfree = ("ve", vv2)
            vv = k.bar("ve", inc="ve")
            vst = k.dma("gp", _drv(xs2)[:, :, base:base + 512],
                        ar2[:].rearrange("p (c t) -> p c t", c=NCH),
                        wait=[("ve", vv)])
            vq = ln_quant(ar2, 512,
                          lambda ch: rp[:, ch * 512:(ch + 1) * 512],
                          [("gd", vst), ("gd", v_st)],
                          xc_wait=[("gd", vst)], sqb=sq4)
            v_st = k.dma("gp", _drv(rq3)[:, :, base:base + 512],
                         rp[:].rearrange("p (c t) -> p c t", c=NCH),
                         wait=[("ve", vq)])
        return v_st

    wt0 = [("gd", v_prep)] if "prep" in phases else []
    last_stage = "none"
    if "ffna" in phases:
        ffn(xT, xs1, "attn" in phases, None, wt0)
        last_stage = "ffna"
    if "attn" in phases:
        v_rsq = attention()
        last_stage = "attn"
    if "attno" in phases:
        attn_out(v_rsq)
        last_stage = "attno"
    if "ffnc" in phases:
        ffn(xs2, y, False, rq3, [("gd", k.v("gd")), ("ve", k.v("ve")),
                                 ("sc", k.v("sc")), ("te", k.v("te"))])
        last_stage = "ffnc"

    if last_stage != "ffnc":
        src = {"ffna": xs1, "attn": xs1, "attno": xs2}.get(last_stage, xT)
        if debug_out is not None:
            src = {"xs1": xs1, "xs2": xs2, "y2s": None, "xT": xT}[debug_out]
        wt = [("gd", k.v("gd")), ("ve", k.v("ve")), ("sc", k.v("sc")),
              ("te", k.v("te")), ("ld", k.v("ld"))]
        for i in range(8):
            vx = k.dma("sy", xa[:].rearrange("p (c t) -> p c t", c=1),
                       _drv(src)[:, i:i + 1, :],
                       wait=wt if i == 0 else [("gd", k.v("gd"))])
            k.dma("gp", _drv(y)[:, i:i + 1, :],
                  xa[:].rearrange("p (c t) -> p c t", c=1),
                  wait=[("ld", vx)])

    fin = [("gd", k.v("gd")), ("ld", k.v("ld")), ("ve", k.v("ve")),
           ("sc", k.v("sc")), ("te", k.v("te"))]
    k.rec("sy", lambda e: e.nop(), wait=fin)

    build._k = k
    with contextlib.ExitStack() as st:
        sems = {}
        for s in ("ld", "gd", "te", "ve", "sc", "gp"):
            sems[s] = st.enter_context(nc.semaphore("sem_" + s))
        blk = st.enter_context(nc.Block())
        blk.sync(lambda e: k.emit(sems, "sy", e))
        blk.tensor(lambda e: k.emit(sems, "te", e))
        blk.vector(lambda e: k.emit(sems, "ve", e))
        blk.scalar(lambda e: k.emit(sems, "sc", e))
        blk.gpsimd(lambda e: k.emit(sems, "gp", e))
    return nc


def _inputs_for_core(ci, x, slope, qw, qbv, ow, obv, theta, w1, cwv, cbv,
                     w2, consts):
    xc = np.ascontiguousarray(
        x[2 * ci:2 * ci + 2].reshape(T, D).T.astype(np.float32))
    d = dict(xT=xc)
    d.update(consts)
    return d


def kernel(L0_attn_x, L0_attn_slope_rate, L0_attn_qkvu_w, L0_attn_qkvu_b,
           L0_attn_out_w, L0_attn_out_b, L0_attn_theta, L0_ffn_w1,
           L0_ffn_conv_w, L0_ffn_conv_b, L0_ffn_w2, _phases=None):
    x = np.asarray(L0_attn_x, dtype=np.float32)
    consts = host_consts(L0_attn_slope_rate, L0_attn_qkvu_w, L0_attn_qkvu_b,
                         L0_attn_out_w, L0_attn_out_b, L0_attn_theta,
                         L0_ffn_w1, L0_ffn_conv_w, L0_ffn_conv_b, L0_ffn_w2)
    nc = build(_phases or ("prep", "ffna", "attn", "attno", "ffnc"))
    in_maps = []
    for ci in range(8):
        xc = np.ascontiguousarray(
            x[2 * ci:2 * ci + 2].reshape(T, D).T.astype(np.float32))
        m = dict(xT=xc)
        m.update(consts)
        in_maps.append(m)
    res = run_bass_kernel_spmd(nc, in_maps, core_ids=list(range(8)))
    outs = []
    for r in res.results:
        yT = np.asarray(r["y"])
        outs.append(yT.reshape(D, NB, S).transpose(1, 2, 0))
    return np.concatenate(outs, axis=0).astype(np.float32)


def host_consts(slope, qw, qbv, ow, obv, theta, w1, cwv, cbv, w2):
    w1T = np.ascontiguousarray(np.asarray(w1, np.float32).T)
    w2T = np.ascontiguousarray(np.asarray(w2, np.float32).T)
    qwT = np.ascontiguousarray(np.asarray(qw, np.float32).T)
    owT = np.ascontiguousarray(np.asarray(ow, np.float32).T)
    qbT = np.ascontiguousarray(
        np.asarray(qbv, np.float32).reshape(32, 128).T)
    obT = np.ascontiguousarray(
        np.asarray(obv, np.float32).reshape(8, 128).T)
    cbT = np.ascontiguousarray(
        np.asarray(cbv, np.float32).reshape(32, 128).T)
    cwT = np.ascontiguousarray(
        np.asarray(cwv, np.float32).reshape(32, 128, 3)
        .transpose(1, 0, 2).reshape(128, 96))
    thT = np.ascontiguousarray(
        np.asarray(theta, np.float32).reshape(8, 128).T)
    slp = np.ascontiguousarray(
        np.asarray(slope, np.float32).reshape(1, 8))
    idn = np.eye(128, dtype=np.float32)
    tri = np.triu(np.ones((128, 128), np.float32))
    iot = np.broadcast_to(np.arange(2048, dtype=np.float32),
                          (128, 2048)).copy()
    return dict(w1T=w1T, w2T=w2T, qwT=qwT, owT=owT, qbT=qbT, obT=obT,
                cbT=cbT, cwT=cwT, thT=thT, slp=slp, idn=idn, tri=tri,
                iot=iot)
